# version 26
# baseline (speedup 1.0000x reference)
import numpy as np
from contextlib import ExitStack

import concourse.bass as bass
import concourse.mybir as mybir
from concourse.bass_utils import run_bass_kernel_spmd

F32 = mybir.dt.float32
BF16 = mybir.dt.bfloat16
AF = mybir.ActivationFunctionType
OP = mybir.AluOpType

B, T, D = 2, 8192, 1024
DKV = 512
DFF = 4096
RANK = 256
NCORE = 8
SEGS = 4
TC = T // SEGS
NSC = 4
SC = TC // NSC
EPS = 1e-5
AOFF = (0, 512, 896, 1152)
ANW = (512, 384, 256, 128)

ENGINES = ("tensor", "vector", "scalar", "gpsimd", "sync")


def build_nc():
    nc = bass.Bass("TRN2", target_bir_lowering=False, debug=False)

    x_f = nc.dram_tensor("x_f", [D, TC], F32, kind="ExternalInput").ap()
    x_t = nc.dram_tensor("x_t", [TC, D], F32, kind="ExternalInput").ap()
    wqT = nc.dram_tensor("wqT", [D, DKV], F32, kind="ExternalInput").ap()
    wkT = nc.dram_tensor("wkT", [D, DKV], F32, kind="ExternalInput").ap()
    wvT = nc.dram_tensor("wvT", [D, DKV], F32, kind="ExternalInput").ap()
    woT = nc.dram_tensor("woT", [DKV, D], F32, kind="ExternalInput").ap()
    u1 = nc.dram_tensor("u1", [D, RANK], F32, kind="ExternalInput").ap()
    v1 = nc.dram_tensor("v1", [RANK, DFF], F32, kind="ExternalInput").ap()
    u2 = nc.dram_tensor("u2", [DFF, RANK], F32, kind="ExternalInput").ap()
    v2 = nc.dram_tensor("v2", [RANK, D], F32, kind="ExternalInput").ap()
    b1i = nc.dram_tensor("b1i", [128, DFF // 128], F32, kind="ExternalInput").ap()
    b2r = nc.dram_tensor("b2r", [1, D], F32, kind="ExternalInput").ap()
    wmi = nc.dram_tensor("wmi", [128, NCORE], F32, kind="ExternalInput").ap()
    out_t = nc.dram_tensor("out_t", [TC, D], F32, kind="ExternalOutput").ap()
    cin_a = nc.dram_tensor("cin_a", [DKV, DKV], BF16)
    cin_b = nc.dram_tensor("cin_b", [DKV, DKV], BF16)
    cout_a = nc.dram_tensor("cout_a", [NCORE * DKV, DKV], BF16,
                            addr_space="Shared")
    cout_b = nc.dram_tensor("cout_b", [NCORE * DKV, DKV], BF16,
                            addr_space="Shared")

    ctx = ExitStack()
    with ctx:
        sbt = lambda n, s, d: ctx.enter_context(nc.sbuf_tensor(n, s, d))
        pstt = lambda n, s, d: ctx.enter_context(nc.psum_tensor(n, s, d))
        mksem = lambda n: ctx.enter_context(nc.semaphore(n))

        WKVU2 = sbt("WKVU2", [128, 8192], BF16)
        WO = sbt("WO", [128, 4, 1024], BF16)
        U1B = sbt("U1B", [128, 8, 256], BF16)
        V1B = sbt("V1B", [128, 2, 4096], BF16)
        V2B = sbt("V2B", [128, 2, 1024], BF16)
        VT = sbt("VT", [128, 16, 512], BF16)
        QK = sbt("QK", [128, 16, 512], BF16)
        AMP = sbt("AMP", [128, 4, 1280], BF16)
        MTB = sbt("MTB", [128, 16, 512], BF16)
        B1S = sbt("B1S", [128, DFF // 128], F32)
        B2R = sbt("B2R", [1, 1024], F32)
        B2B = sbt("B2B", [1, 1024], BF16)
        WMS = sbt("WMS", [128, NCORE], F32)
        MSK = sbt("MSK", [128, 128], BF16)
        IDN = sbt("IDN", [128, 128], BF16)
        IDN32 = sbt("IDN32", [128, 128], F32)
        ONER = sbt("ONER", [1, 128], BF16)
        ST = sbt("ST", [128, 128], F32)
        OUTS = sbt("OUTS", [128, 2, 1024], F32)

        ARN = ctx.enter_context(nc.sbuf_tensor("ARN", [128, 38912], BF16))
        arnb = ARN
        arnf = ARN.bitcast(F32)
        fap = arnf.ap()
        bap = arnb.ap()
        XT = fap[:, 0:4096].rearrange("p (c n) -> p c n", n=512)
        STG = fap[:, 4096:8192].rearrange("p (c n) -> p c n", n=2048)
        MR = fap[:, 8192:10240].rearrange("p (c n) -> p c n", n=512)
        MRf = fap[:, 8192:10240]
        WQ = bap[:, 20480:24576].rearrange("p (c n) -> p c n", n=512)
        XB = bap[:, 24576:28672].rearrange("p (c n) -> p c n", n=512)
        KT = bap[:, 28672:30720].rearrange("p (c n) -> p c n", n=512)
        KF = bap[:, 30720:38912].rearrange("p (c n) -> p c n", n=512)
        GST = bap[:, 8192:16384].rearrange("p (c n) -> p c n", n=512)
        XTOK = fap[:, 0:4096].rearrange("p (c n) -> p c n", n=1024)
        ZB_A = fap[:, 4096:8192].rearrange("p (c n) -> p c n", n=1024)
        ZB_B = fap[:, 15360:19456].rearrange("p (c n) -> p c n", n=1024)
        SQB = bap[:, 16384:17408].rearrange("p (c n) -> p c n", n=1024)
        PB = bap[:, 18432:19456].rearrange("p (c n) -> p c n", n=512)
        Q2B = bap[:, 19456:20480].rearrange("p (c n) -> p c n", n=512)
        XF = bap[:, 20480:24576].rearrange("p (c n) -> p c n", n=512)
        HB = bap[:, 24576:28672].rearrange("p (c n) -> p c n", n=512)
        YF = bap[:, 28672:30720].rearrange("p (c n) -> p c n", n=512)

        AMPf = AMP.bitcast(F32)
        wk_v = WKVU2[:, 0:4096].rearrange("p (c n) -> p c n", n=512)
        wv_v = WKVU2[:, 4096:8192].rearrange("p (c n) -> p c n", n=512)
        u2_v = WKVU2[:, :].rearrange("p (c n) -> p c n", n=256)

        psA = pstt("psA", [128, 512], F32)
        psB = pstt("psB", [128, 512], F32)
        psC = pstt("psC", [128, 512], F32)
        psD = pstt("psD", [128, 512], F32)
        psY0 = pstt("psY0", [128, 512], F32)
        psY1 = pstt("psY1", [128, 512], F32)
        psT0 = pstt("psT0", [128, 512], F32)
        psT1 = pstt("psT1", [128, 512], F32)
        psT0b = psT0.bitcast(BF16)
        psT1b = psT1.bitcast(BF16)

        sems = {n: mksem(n) for n in
                ["sx", "swa", "swb", "sba", "scc", "scd", "so",
                 "sp", "sv", "sa", "sq"]}

        streams = {e: [] for e in ENGINES}
        cnt = {n: 0 for n in sems}
        ENG_SEM = {"tensor": "sp", "vector": "sv", "scalar": "sa",
                   "gpsimd": "sq"}

        def emit(eng, fn):
            streams[eng].append(fn)

        def wait(eng, sem_name, val):
            if val is None or val <= 0:
                return
            s = sems[sem_name]
            emit(eng, lambda e, s=s, v=val: e.wait_ge(s, v))

        def tick(eng, builder, sem_name=None, inc=1):
            sn = sem_name or ENG_SEM[eng]
            s = sems[sn]
            cnt[sn] += inc
            v = cnt[sn]
            emit(eng, lambda e, b=builder, s=s, k=inc: b(e).then_inc(s, k))
            return v

        def quiet(eng, builder):
            emit(eng, lambda e, b=builder: b(e))

        def dma(eng, out_ap, in_ap, sem_name):
            s = sems[sem_name]
            cnt[sem_name] += 16
            v = cnt[sem_name]
            emit(eng, lambda e, o=out_ap, i=in_ap, s=s:
                 e.dma_start(out=o, in_=i).then_inc(s, 16))
            return v

        bankfree = {id(p): ("sv", 0) for p in
                    (psA, psB, psC, psD, psY0, psY1, psT0, psT1)}

        def mm_group(bank, mms, skip_check=False):
            sn, sval = bankfree[id(bank)]
            wait("tensor", sn, sval)
            v = None
            for i, (lh, rh, osl) in enumerate(mms):
                st_ = (i == 0)
                sp_ = (i == len(mms) - 1)
                fn = (lambda e, lh=lh, rh=rh, o=osl, st=st_, sp=sp_, sk=skip_check:
                      e.matmul(o, lh, rh, start=st, stop=sp, skip_group_check=sk))
                if sp_:
                    v = tick("tensor", fn)
                else:
                    quiet("tensor", fn)
            return v

        _c1 = tick("gpsimd", lambda g: g.memset(IDN[:, :], 0.0))
        wait("gpsimd", "sq", _c1)
        tick("gpsimd", lambda g: g.affine_select(
            out=IDN[:, :], in_=IDN[:, :], compare_op=OP.not_equal, fill=1.0,
            base=0, pattern=[[-1, 128]], channel_multiplier=1))
        _c1b = tick("gpsimd", lambda g: g.memset(IDN32[:, :], 0.0))
        wait("gpsimd", "sq", _c1b)
        tick("gpsimd", lambda g: g.affine_select(
            out=IDN32[:, :], in_=IDN32[:, :], compare_op=OP.not_equal, fill=1.0,
            base=0, pattern=[[-1, 128]], channel_multiplier=1))
        _c2 = tick("gpsimd", lambda g: g.memset(MSK[:, :], 1.0))
        wait("gpsimd", "sq", _c2)
        tick("gpsimd", lambda g: g.affine_select(
            out=MSK[:, :], in_=MSK[:, :], compare_op=OP.is_gt,
            fill=0.0, base=0, pattern=[[1, 128]], channel_multiplier=-1))
        consts_done = tick("gpsimd", lambda g: g.memset(ONER[0:1, :], 1.0))
        wait("scalar", "sq", consts_done)
        wait("vector", "sq", consts_done)
        wait("tensor", "sq", consts_done)

        _slots = [STG[:, 0, :], STG[:, 1, :],
                  AMPf.ap().rearrange("p c n -> p (c n)")[:, 0:2048]]
        _sema = ["swa", "swb", "sba"]
        wstate = {"i": 0, "cast": {}, "dmav": {}, "pend": [], "slotc_cast": None}

        def stage_issue(src2d, blocker=None):
            i = wstate["i"]
            h = i % 3
            prev = wstate["cast"].get(i - 3)
            if prev is not None:
                wait("gpsimd", prev[0], prev[1])
            if blocker is not None:
                wait("gpsimd", blocker[0], blocker[1])
            n = int(src2d.shape[0]) * int(src2d.shape[1]) // 128
            nin = int(src2d.shape[1])
            sv = _slots[h][:, 0:n].rearrange("p (c n) -> p c n", n=nin)
            wstate["dmav"][i] = (dma("gpsimd", sv,
                                     src2d.rearrange("(c p) n -> p c n", p=128),
                                     _sema[h]), _sema[h])
            wstate["i"] = i + 1
            return i

        def stage_cast(i, dst3d, ceng, blocker=None):
            h = i % 3
            dv, sm = wstate["dmav"][i]
            wait(ceng, sm, dv)
            if blocker is not None:
                wait(ceng, blocker[0], blocker[1])
            nch, nin = int(dst3d.shape[1]), int(dst3d.shape[2])
            sview = _slots[h][:, 0:nch * nin].rearrange(
                "p (c n) -> p c n", n=nin)
            if ceng == "scalar":
                v = tick(ceng, lambda e, d=dst3d, s=sview: e.copy(out=d, in_=s))
            else:
                v = tick(ceng, lambda e, d=dst3d, s=sview:
                         e.tensor_copy(out=d, in_=s))
            wstate["cast"][i] = (ENG_SEM[ceng], v)
            if i % 3 == 2:
                wstate["slotc_cast"] = (ENG_SEM[ceng], v)
            return v

        def run_weight_jobs(jlist, blocker=None, two_slot=False):
            for (src2d, dst3d, ceng) in jlist:
                if two_slot and wstate["i"] % 3 == 2:
                    wstate["i"] += 1
                i = stage_issue(src2d)
                stage_cast(i, dst3d, ceng, blocker=blocker)
                blocker = None

        jobs_kv = []
        for m, dstv in ((wkT, wk_v), (wvT, wv_v)):
            for c in range(2):
                jobs_kv.append((m[c * 512:(c + 1) * 512, :],
                                dstv[:, 4 * c:4 * c + 4, :], "scalar"))
        jobs_q = [(wqT[c * 512:(c + 1) * 512, :], WQ[:, 4 * c:4 * c + 4, :],
                   "scalar") for c in range(2)]
        jobs_rest = []
        for c in range(2):
            jobs_rest.append((woT[c * 256:(c + 1) * 256, :],
                              WO[:, 2 * c:2 * c + 2, :], "scalar"))
        jobs_rest.append((u1, U1B[:, :, :], "scalar"))
        for rc in range(2):
            for k in range(2):
                jobs_rest.append((v1[rc * 128:(rc + 1) * 128,
                                     k * 2048:(k + 1) * 2048],
                                  V1B[:, rc, k * 2048:(k + 1) * 2048]
                                  .rearrange("p (c n) -> p c n", c=1),
                                  "scalar"))
        jobs_u2 = [(u2[c * 1024:(c + 1) * 1024, :],
                    u2_v[:, 8 * c:8 * c + 8, :], "scalar") for c in range(4)]
        jobs_u2 += [(v2[c * 128:(c + 1) * 128, :],
                     V2B[:, c, :].rearrange("p (c n) -> p c n", c=1),
                     "scalar") for c in range(2)]

        dma("sync", B1S[:, :], b1i, "sx")
        dma("sync", B2R[0:1, :], b2r, "sx")
        v_wm = dma("sync", WMS[:, :], wmi, "sx")
        SBA_MISC = v_wm
        wait("scalar", "sx", SBA_MISC)
        b2b_v = tick("scalar", lambda e: e.copy(out=B2B[0:1, :], in_=B2R[0:1, :]))

        x_src = x_f.rearrange("d (s t) -> d s t", t=SC)
        st1 = {
            "xt_free": ("sv", 0),
            "xb_free_p": 0,
            "mr_v": 0,
            "kt_evac": {},
            "vt_evac": {},
            "qf_evac": {},
            "kf_evac": {},
            "mtb_snap": {},
        }
        kv_last_p = 0

        pre_vx = dma("sync", XT[:, :, :],
                     x_src[:, 0, :].rearrange("(c p) t -> p c t", p=128), "sx")
        run_weight_jobs(jobs_kv[:2])
        kvcast_done = cnt["sa"]
        run_weight_jobs(jobs_kv[2:])
        vvcast_done = cnt["sa"]

        for sc in range(NSC):
            if sc == 0:
                vx = pre_vx
                wait("vector", "sx", vx)
                xcast = tick("vector", lambda e: e.tensor_copy(
                    out=XB[:, :, :].rearrange("p c n -> p (c n)"),
                    in_=XT[:, :, :].rearrange("p c n -> p (c n)")))
                st1["xt_free"] = ("sv", xcast)
            else:
                xcast = st1["pre_cast"]
            if sc < 3:
                sn, v = st1["xt_free"]
                wait("sync", sn, v)
                st1["nvx"] = dma("sync", XT[:, :, :],
                                 x_src[:, sc + 1, :]
                                 .rearrange("(c p) t -> p c t", p=128), "sx")
            if sc == 0:
                run_weight_jobs(jobs_q)
            elif sc == 1:
                run_weight_jobs(jobs_rest[:6])
            elif sc == 2:
                run_weight_jobs(jobs_rest[6:])

            wait("tensor", "sv", xcast)
            if sc == 0:
                wait("tensor", "sa", kvcast_done)
            for tc in range(4):
                bank = psA if tc % 2 == 0 else psB
                gv = mm_group(bank, [
                    (XB[:, c, tc * 128:(tc + 1) * 128], wk_v[:, c, :],
                     bank[:, :]) for c in range(8)])
                wait("vector", "sp", gv)
                if sc > 0:
                    wait("vector", "sp", kv_last_p)
                ev = tick("vector", lambda e, b=bank, t=tc:
                          e.tensor_copy(out=KT[:, t, :], in_=b[:, :]))
                bankfree[id(bank)] = ("sv", ev)
                st1["kt_evac"][tc] = ev
            if sc == 0:
                wait("tensor", "sa", vvcast_done)
            for tc in range(4):
                bank = psA if tc % 2 == 0 else psB
                gv = mm_group(bank, [
                    (XB[:, c, tc * 128:(tc + 1) * 128], wv_v[:, c, :],
                     bank[:, :]) for c in range(8)])
                wait("vector", "sp", gv)
                ev = tick("vector", lambda e, b=bank, s=sc, t=tc:
                          e.tensor_copy(out=VT[:, 4 * s + t, :], in_=b[:, :]))
                bankfree[id(bank)] = ("sv", ev)
                st1["vt_evac"][(sc, tc)] = ev

            if sc == 0:
                wait("tensor", "sa", cnt["sa"])
            for jc in range(4):
                bank = psY0 if jc % 2 == 0 else psY1
                gv = mm_group(bank, [
                    (WQ[:, c, jc * 128:(jc + 1) * 128], XB[:, c, :],
                     bank[:, :]) for c in range(8)])
                wait("vector", "sp", gv)
                ev = tick("vector", lambda e, b=bank, s=sc, j=jc:
                          e.tensor_copy(out=QK[:, 4 * s + j, :], in_=b[:, :]))
                bankfree[id(bank)] = ("sv", ev)
                st1["qf_evac"][(sc, jc)] = ev
            st1["xb_free_p"] = cnt["sp"]

            if sc < 3:
                wait("vector", "sx", st1["nvx"])
                wait("vector", "sp", st1["xb_free_p"])
                ncast = tick("vector", lambda e: e.tensor_copy(
                    out=XB[:, :, :].rearrange("p c n -> p (c n)"),
                    in_=XT[:, :, :].rearrange("p c n -> p (c n)")))
                st1["xt_free"] = ("sv", ncast)
                st1["pre_cast"] = ncast

            for tc in range(4):
                wait("tensor", "sv", st1["kt_evac"][tc])
            for lc in range(4):
                bank = psT0 if lc % 2 == 0 else psT1
                bankb = psT0b if lc % 2 == 0 else psT1b
                sn, v = bankfree[id(bank)]
                wait("tensor", sn, v)
                gv = None
                for tcc in range(4):
                    fn = (lambda e, b=bankb, l=lc, t=tcc:
                          e.transpose(b[:, t * 128:(t + 1) * 128],
                                      KT[:, t, l * 128:(l + 1) * 128],
                                      IDN[:, :]))
                    if tcc == 3:
                        gv = tick("tensor", fn)
                    else:
                        quiet("tensor", fn)
                wait("scalar", "sp", gv)
                ev = tick("scalar", lambda e, b=bankb, s=sc, l=lc:
                          e.copy(out=KF[:, 4 * s + l, :], in_=b[:, 0:512]))
                bankfree[id(bank)] = ("sa", ev)
                st1["kf_evac"][(sc, lc)] = ev
            for tc in range(4):
                wait("tensor", "sv", st1["kt_evac"][tc])
                wait("tensor", "sv", st1["vt_evac"][(sc, tc)])
            for kc in range(4):
                bank = psC if kc % 2 == 0 else psD
                gv = mm_group(bank, [
                    (KT[:, tc, kc * 128:(kc + 1) * 128], VT[:, 4 * sc + tc, :],
                     bank[:, :]) for tc in range(4)])
                wait("vector", "sp", gv)
                wait("vector", "sv", st1["mr_v"])
                if sc == 0:
                    mv = tick("vector", lambda e, b=bank, k=kc:
                              e.tensor_copy(out=MR[:, k, :], in_=b[:, :]))
                elif sc < 3:
                    mv = tick("vector", lambda e, b=bank, k=kc:
                              e.tensor_tensor(out=MR[:, k, :], in0=MR[:, k, :],
                                              in1=b[:, :], op=OP.add))
                else:
                    mv = tick("vector", lambda e, b=bank, k=kc:
                              e.tensor_copy(out=MTB[:, k, :], in_=b[:, :]))
                    wait("scalar", "sv", mv)
                    st1["cinb_dma"] = dma(
                        "scalar",
                        cin_b[kc * 128:(kc + 1) * 128, :]
                        .rearrange("(c p) i -> p c i", p=128),
                        MTB[:, kc:kc + 1, :], "sba")
                bankfree[id(bank)] = ("sv", mv)
                st1["mr_v"] = mv
            if sc == 3:
                wait("gpsimd", "sba", st1["cinb_dma"])
                wait("gpsimd", "scc", 1)
                tick("gpsimd", lambda e: e.collective_compute(
                    "AllGather", OP.bypass,
                    replica_groups=[list(range(NCORE))],
                    ins=[cin_b.ap()], outs=[cout_b.ap()]), sem_name="scd")

            if sc < 3:
                wait("scalar", "sv", st1["mr_v"])
                snap = tick("scalar", lambda e, s=sc: e.copy(
                    out=MTB[:, 4 * (s + 1):4 * (s + 2), :]
                    .rearrange("p c n -> p (c n)"), in_=MRf))
                st1["mtb_snap"][sc + 1] = snap
            if sc == 2:
                wait("scalar", "sa", st1["mtb_snap"][3])
                v_cina = dma("scalar",
                             cin_a.ap().rearrange("(c p) i -> p c i", p=128),
                             MTB[:, 12:16, :], "sba")
                wait("gpsimd", "sba", v_cina)
                tick("gpsimd", lambda e: e.collective_compute(
                    "AllGather", OP.bypass,
                    replica_groups=[list(range(NCORE))],
                    ins=[cin_a.ap()], outs=[cout_a.ap()]), sem_name="scc")
            kv_last_p = cnt["sp"]

        sn, v = st1["xt_free"]
        wait("sync", sn, v)
        xtok_v = dma("sync", XTOK[:, :, :],
                     x_t[0:SC, :].rearrange("(c p) n -> p c n", p=128), "sx")

        run_weight_jobs(jobs_u2, blocker=("sp", kv_last_p),
                        two_slot=True)
        w_all_sq = ("sa", cnt["sa"])

        if wstate["slotc_cast"] is not None:
            wait("vector", wstate["slotc_cast"][0], wstate["slotc_cast"][1])
        amask = {}
        for sc in range(NSC):
            for lc in range(4):
                wait("tensor", "sa", st1["kf_evac"][(sc, lc)])
                wait("tensor", "sv", st1["qf_evac"][(sc, lc)])
        for sc in range(NSC):
            for sj in range(4):
                nw = ANW[sj]
                bank = psA if sj % 2 == 0 else psB
                gv = mm_group(bank, [
                    (KF[:, 4 * sc + lc, sj * 128:(sj + 1) * 128],
                     QK[:, 4 * sc + lc, sj * 128:512], bank[:, 0:nw])
                    for lc in range(4)])
                off = AOFF[sj]
                wait("vector", "sp", gv)
                quiet("vector", lambda e, b=bank, s=sc, o=off:
                      e.tensor_tensor(out=AMP[:, s, o:o + 128],
                                      in0=b[:, 0:128], in1=MSK[:, :],
                                      op=OP.mult))
                if nw > 128:
                    mv = tick("vector", lambda e, b=bank, s=sc, o=off, n=nw:
                              e.tensor_copy(out=AMP[:, s, o + 128:o + n],
                                            in_=b[:, 128:n]))
                else:
                    mv = tick("vector", lambda e: e.engine_nop())
                bankfree[id(bank)] = ("sv", mv)
                amask[(sc, sj)] = mv

        for i, cv in list(wstate["cast"].items())[-3:]:
            wait("sync", cv[0], cv[1])
        wait("vector", "sx", SBA_MISC)
        wait("vector", "sv", st1["mr_v"])
        wait("vector", "sa", st1["mtb_snap"][3])
        vstg = [None, None, None, None]
        mg = 0
        for r in range(2 * NCORE):
            h = r % 4
            cc_sem = "scc" if r < NCORE else "scd"
            cc_src = cout_a if r < NCORE else cout_b
            rr = r % NCORE
            if vstg[h] is not None:
                wait("sync", "sv", vstg[h])
            wait("sync", cc_sem, 1)
            vg = dma("sync", GST[:, 4 * h:4 * h + 4, :],
                     cc_src[rr * DKV:(rr + 1) * DKV, :]
                     .rearrange("(c p) i -> p c i", p=128), "sba")
            wait("vector", "sba", vg)
            gfl = GST[:, 4 * h:4 * h + 4, :].rearrange("p c n -> p (c n)")
            if r == 0:
                mg = tick("vector", lambda e, g=gfl: e.tensor_scalar(
                    out=MRf, in0=g, scalar1=WMS[:, 0:1], scalar2=None,
                    op0=OP.mult))
            else:
                wait("vector", "sv", mg)
                mg = tick("vector", lambda e, g=gfl, rr=rr:
                          e.scalar_tensor_tensor(
                              out=MRf, in0=g, scalar=WMS[:, rr:rr + 1],
                              in1=MRf, op0=OP.mult, op1=OP.add))
            vstg[h] = mg
        mtb_fin = {}
        wait("vector", "sv", mg)
        wait("vector", "sba", st1["cinb_dma"])
        f0 = tick("vector", lambda e: e.tensor_copy(
            out=MTB[:, 0:4, :].rearrange("p c n -> p (c n)"), in_=MRf))
        mtb_fin[0] = f0
        for s in range(1, 4):
            wait("vector", "sa", st1["mtb_snap"][s])
            wait("vector", "sv", mtb_fin[s - 1])
            fs = tick("vector", lambda e, s=s: e.tensor_tensor(
                out=MTB[:, 4 * s:4 * (s + 1), :].rearrange("p c n -> p (c n)"),
                in0=MRf,
                in1=MTB[:, 4 * s:4 * (s + 1), :].rearrange("p c n -> p (c n)"),
                op=OP.add))
            mtb_fin[s] = fs
        mtb_fin_last = mtb_fin[3]

        st3 = {
            "yf_free": ("sa", 0),
            "wo_last": {},
            "zadd": {},
            "zadd_last": {},
            "apply": {},
            "apply_last": {},
            "rstd1": {}, "rstd2": {},
            "xf_evac": {},
            "gelu": {},
            "u2_tick": {},
            "q2_evac": {},
            "pb_evac": {},
            "z2add": {},
            "z2_last": {},
            "z2_last": {},
            "apply2": {},
            "outs_dma": [0, 0],
            "apply2_last": {},
            "xtok_v": {0: xtok_v},
            "y_last": {},
            "ffn_last": {},
        }

        def ZB(sc):
            return ZB_A if sc % 2 == 0 else ZB_B

        def stc(sc, which, i):
            base = (sc % 2) * 64 + (0 if which == 1 else 32)
            return ST[:, base + i:base + i + 1]

        def stc4(sc, which, i0):
            base = (sc % 2) * 64 + (0 if which == 1 else 32)
            return ST[:, base + i0:base + i0 + 4]

        def emit_Y(sc):
            wait("tensor", "sv", mtb_fin[sc])
            for sj in range(4):
                wait("tensor", "sv", amask[(sc, sj)])
            sn, v = st3["yf_free"]
            for vfc in range(4):
                bank = psY0 if vfc % 2 == 0 else psY1
                bsn, bv = bankfree[id(bank)]
                wait("tensor", bsn, bv)
                quiet("tensor", lambda e, b=bank, s=sc, i=vfc:
                      e.matmul(b[:, 0:512], VT[:, 4 * s, i * 128:(i + 1) * 128],
                               AMP[:, s, 0:512], start=True, stop=False))
                for tk in range(1, 4):
                    quiet("tensor", lambda e, b=bank, s=sc, i=vfc, t=tk:
                          e.matmul(b[:, 128 * t:512],
                                   VT[:, 4 * s + t, i * 128:(i + 1) * 128],
                                   AMP[:, s, AOFF[t]:AOFF[t] + ANW[t]],
                                   start=False, stop=False))
                gv = None
                for lc in range(4):
                    fn = (lambda e, b=bank, s=sc, i=vfc, l=lc, sp_=(lc == 3):
                          e.matmul(b[:, 0:512],
                                   MTB[:, 4 * s + l, i * 128:(i + 1) * 128],
                                   QK[:, 4 * s + l, :], start=False, stop=sp_))
                    if lc == 3:
                        gv = tick("tensor", fn)
                    else:
                        quiet("tensor", fn)
                wait("scalar", "sp", gv)
                sn, v = st3["yf_free"]
                wait("scalar", sn, v)
                ev = tick("scalar", lambda e, b=bank, i=vfc:
                          e.copy(out=YF[:, i, :], in_=b[:, :]))
                bankfree[id(bank)] = ("sa", ev)
            st3["y_last"][sc] = cnt["sa"]

        def emit_Wo_z(sc):
            yev = st3["y_last"][sc]
            wait("tensor", "sa", yev)
            banks = (psA, psB, psC, psT0, psT1)
            gi = 0
            for tc in range(4):
                zv_h = []
                for h in range(2):
                    bank = banks[gi % 5]
                    gi += 1
                    gv = mm_group(bank, [
                        (YF[:, ic, tc * 128:(tc + 1) * 128],
                         WO[:, ic, h * 512:(h + 1) * 512], bank[:, :])
                        for ic in range(4)])
                    wait("vector", "sp", gv)
                    wait("vector", "sx", st3["xtok_v"][sc])
                    if sc >= 2 and tc == 0 and h == 0:
                        wait("vector", "sv", st3["apply2_last"][sc - 2])
                    zv = tick("vector", lambda e, b=bank, s=sc, t=tc, hh=h:
                              e.scalar_tensor_tensor(
                                  out=ZB(s)[:, t, hh * 512:(hh + 1) * 512],
                                  in0=b[:, :], scalar=1.0,
                                  in1=XTOK[:, t, hh * 512:(hh + 1) * 512],
                                  op0=OP.mult, op1=OP.add,
                                  accum_out=stc(s, 1, 4 * hh + t)))
                    bankfree[id(bank)] = ("sv", zv)
                    zv_h.append(zv)
                st3["zadd"][(sc, tc)] = zv_h[-1]
            st3["zadd_last"][sc] = cnt["sv"]
            st3["wo_last"][sc] = cnt["sp"]
            if sc < NSC - 1:
                wait("sync", "sv", st3["zadd_last"][sc])
                st3["xtok_v"][sc + 1] = dma(
                    "sync", XTOK[:, :, :],
                    x_t[(sc + 1) * SC:(sc + 2) * SC, :]
                    .rearrange("(c p) n -> p c n", p=128), "sx")

        def emit_LN(sc, which):
            zb = ZB(sc)
            zlast = (st3["zadd_last"][sc] if which == 1
                     else st3["z2_last"][sc])
            sqv = {}
            for tc in range(4):
                if which == 1 and sc == 0 and tc == 0:
                    wait("scalar", "sv", mtb_fin_last)
                wait("scalar", "sv",
                     st3["zadd"][(sc, tc)] if which == 1
                     else st3["z2add"][(sc, tc, 1)])
                sqv[tc] = tick("scalar", lambda e, t=tc, z=zb, s=sc, w=which:
                               e.activation(out=SQB[:, 0, :], in_=z[:, t, :],
                                            func=AF.Square,
                                            accum_out=stc(s, w, 8 + t)))
            wait("vector", "sv", zlast)
            zs = tick("vector", lambda e, s=sc, w=which: e.tensor_tensor(
                out=stc4(s, w, 0), in0=stc4(s, w, 0), in1=stc4(s, w, 4),
                op=OP.add))
            wait("vector", "sv", zs)
            mean = tick("vector", lambda e, s=sc, w=which: e.tensor_scalar(
                out=stc4(s, w, 12), in0=stc4(s, w, 0), scalar1=1.0 / 1024.0,
                scalar2=None, op0=OP.mult))
            wait("vector", "sv", mean)
            msq = tick("vector", lambda e, s=sc, w=which: e.tensor_tensor(
                out=stc4(s, w, 16), in0=stc4(s, w, 12), in1=stc4(s, w, 12),
                op=OP.mult))
            for tc in range(4):
                wait("vector", "sa", sqv[tc])
            wait("vector", "sv", msq)
            varv = tick("vector", lambda e, s=sc, w=which: e.scalar_tensor_tensor(
                out=stc4(s, w, 16), in0=stc4(s, w, 8), scalar=1.0 / 1024.0,
                in1=stc4(s, w, 16), op0=OP.mult, op1=OP.subtract))
            wait("vector", "sv", varv)
            vee = tick("vector", lambda e, s=sc, w=which: e.tensor_scalar(
                out=stc4(s, w, 16), in0=stc4(s, w, 16), scalar1=EPS,
                scalar2=None, op0=OP.add))
            wait("vector", "sv", vee)
            inv = tick("vector", lambda e, s=sc, w=which: e.reciprocal(
                out=stc4(s, w, 20), in_=stc4(s, w, 16)))
            wait("scalar", "sv", inv)
            rst = tick("scalar", lambda e, s=sc, w=which: e.activation(
                out=stc4(s, w, 16), in_=stc4(s, w, 20), func=AF.Sqrt))
            if which == 1:
                st3["rstd1"][sc] = rst
            else:
                st3["rstd2"][sc] = rst
            return rst

        def emit_apply1(sc):
            rst = st3["rstd1"][sc]
            zb = ZB(sc)
            for tc in range(4):
                wait("vector", "sa", rst)
                av = tick("vector", lambda e, t=tc, z=zb, s=sc: e.tensor_scalar(
                    out=z[:, t, :], in0=z[:, t, :],
                    scalar1=stc(s, 1, 12 + t), scalar2=stc(s, 1, 16 + t),
                    op0=OP.subtract, op1=OP.mult))
                st3["apply"][(sc, tc)] = av
            st3["apply_last"][sc] = cnt["sv"]

        def emit_xT(sc):
            zb = ZB(sc)
            for dc in range(8):
                bank = psY0 if dc % 2 == 0 else psY1
                bsn, bv = bankfree[id(bank)]
                wait("tensor", bsn, bv)
                for tj in range(4):
                    if dc == 0:
                        wait("tensor", "sv", st3["apply"][(sc, tj)])
                gv = None
                for tj in range(4):
                    fn = (lambda e, b=bank, z=zb, t=tj, d=dc:
                          e.transpose(b[:, t * 128:(t + 1) * 128]
                                      .bitcast(F32) if False else
                                      b[:, t * 128:(t + 1) * 128],
                                      z[:, t, d * 128:(d + 1) * 128],
                                      IDN32[:, :]))
                    if tj == 3:
                        gv = tick("tensor", fn)
                    else:
                        quiet("tensor", fn)
                wait("scalar", "sp", gv)
                if sc == 0 and dc == 0:
                    wait("scalar", "sp", st1["xb_free_p"])
                ev = tick("scalar", lambda e, b=bank, d=dc:
                          e.copy(out=XF[:, d, :], in_=b[:, :]))
                bankfree[id(bank)] = ("sa", ev)
                st3["xf_evac"][(sc, dc)] = ev

        def emit_FFN(sc):
            for dc in range(8):
                wait("tensor", "sa", st3["xf_evac"][(sc, dc)])
            if sc == 0:
                wait("tensor", w_all_sq[0], w_all_sq[1])
                wait("tensor", "sa", b2b_v)
            for rc in range(2):
                bank = psC if rc == 0 else psD
                gv = mm_group(bank, [
                    (U1B[:, c, rc * 128:(rc + 1) * 128], XF[:, c, :],
                     bank[:, :]) for c in range(8)])
                wait("scalar", "sp", gv)
                if sc == 0 and rc == 0:
                    wait("scalar", "sv", mtb_fin_last)
                ev = tick("scalar", lambda e, b=bank, r=rc:
                          e.copy(out=PB[:, r, :], in_=b[:, :]))
                bankfree[id(bank)] = ("sa", ev)
                st3["pb_evac"][(sc, rc)] = ev
            for rc in range(2):
                wait("tensor", "sa", st3["pb_evac"][(sc, rc)])
            v1banks = (psY0, psY1, psA)

            def emit_V1(fc):
                vbank = v1banks[fc % 3]
                bsn, bv = bankfree[id(vbank)]
                wait("tensor", bsn, bv)
                gv = mm_group(vbank, [
                    (V1B[:, rc, fc * 128:(fc + 1) * 128], PB[:, rc, :],
                     vbank[:, :]) for rc in range(2)])
                wait("scalar", "sp", gv)
                if fc >= 8:
                    wait("scalar", "sp", st3["u2_tick"][(sc, fc - 8)])
                hslot = fc % 8
                ge = tick("scalar", lambda e, b=vbank, h=hslot, f=fc:
                          e.activation(out=HB[:, h, :], in_=b[:, :],
                                       func=AF.Gelu_apprx_tanh,
                                       bias=B1S[:, f:f + 1]))
                st3["gelu"][(sc, fc)] = ge
                bankfree[id(vbank)] = ("sa", ge)

            def emit_U2(fc):
                hslot = fc % 8
                wait("tensor", "sa", st3["gelu"][(sc, fc)])
                quiet("tensor", lambda e, f=fc, h=hslot, st_=(fc == 0):
                      e.matmul(psC[:, :], u2_v[:, f, 0:128], HB[:, h, :],
                               start=st_, stop=(f == 31),
                               skip_group_check=True))
                fn = (lambda e, f=fc, h=hslot, st_=(fc == 0):
                      e.matmul(psD[:, :], u2_v[:, f, 128:256], HB[:, h, :],
                               start=st_, stop=(f == 31),
                               skip_group_check=True))
                st3["u2_tick"][(sc, fc)] = tick("tensor", fn)

            emit_V1(0)
            emit_V1(1)
            for fc in range(32):
                if fc + 2 < 32:
                    emit_V1(fc + 2)
                emit_U2(fc)
            u2l = st3["u2_tick"][(sc, 31)]
            wait("scalar", "sp", u2l)
            q2a = tick("scalar", lambda e: e.copy(out=Q2B[:, 0, :], in_=psC[:, :]))
            q2b = tick("scalar", lambda e: e.copy(out=Q2B[:, 1, :], in_=psD[:, :]))
            bankfree[id(psC)] = ("sa", q2b)
            bankfree[id(psD)] = ("sa", q2b)
            st3["q2_evac"][sc] = q2b
            zb = ZB(sc)
            banks = (psA, psB, psT0, psT1)
            for tc in range(4):
                for h in range(2):
                    bank = banks[(2 * tc + h) % 4]
                    bsn, bv = bankfree[id(bank)]
                    wait("tensor", bsn, bv)
                    quiet("tensor", lambda e, b=bank, hh=h:
                          e.matmul(b[:, :], ONER[0:1, :],
                                   B2B[0:1, hh * 512:(hh + 1) * 512],
                                   start=True, stop=False))
                    if tc == 0 and h == 0:
                        wait("tensor", "sa", q2b)
                    gv = None
                    for rc in range(2):
                        fn = (lambda e, b=bank, t=tc, hh=h, r=rc, sp_=(rc == 1):
                              e.matmul(b[:, :], Q2B[:, r, t * 128:(t + 1) * 128],
                                       V2B[:, r, hh * 512:(hh + 1) * 512],
                                       start=False, stop=sp_))
                        if rc == 1:
                            gv = tick("tensor", fn)
                        else:
                            quiet("tensor", fn)
                    wait("vector", "sp", gv)
                    wait("vector", "sv", st3["apply"][(sc, tc)])
                    zv = tick("vector", lambda e, b=bank, z=zb, t=tc, hh=h, s=sc:
                              e.scalar_tensor_tensor(
                                  out=z[:, t, hh * 512:(hh + 1) * 512],
                                  in0=b[:, :], scalar=1.0,
                                  in1=z[:, t, hh * 512:(hh + 1) * 512],
                                  op0=OP.mult, op1=OP.add,
                                  accum_out=stc(s, 2, 4 * hh + t)))
                    bankfree[id(bank)] = ("sv", zv)
                    st3["z2add"][(sc, tc, h)] = zv
            st3["z2_last"][sc] = cnt["sv"]
            st3["ffn_last"][sc] = cnt["sp"]

        def emit_apply2_out(sc):
            rst = st3["rstd2"][sc]
            zb = ZB(sc)
            for tc in range(4):
                wait("vector", "sa", rst)
                slot = tc % 2
                wait("vector", "so", st3["outs_dma"][slot])
                av = tick("vector", lambda e, t=tc, z=zb, s=sc, sl=slot:
                          e.tensor_scalar(
                              out=OUTS[:, sl, :], in0=z[:, t, :],
                              scalar1=stc(s, 2, 12 + t),
                              scalar2=stc(s, 2, 16 + t),
                              op0=OP.subtract, op1=OP.mult))
                st3["apply2"][(sc, tc)] = av
                wait("scalar", "sv", av)
                st3["outs_dma"][slot] = dma(
                    "scalar",
                    out_t[sc * SC + tc * 128: sc * SC + (tc + 1) * 128, :],
                    OUTS[:, slot, :], "so")
            st3["apply2_last"][sc] = cnt["sv"]

        emit_Y(0)
        emit_Wo_z(0)
        emit_Y(1)
        emit_LN(0, 1); emit_apply1(0)
        emit_Wo_z(1)
        emit_xT(0)
        emit_FFN(0)
        emit_LN(1, 1); emit_apply1(1)
        emit_Y(2)
        emit_LN(0, 2); emit_apply2_out(0)
        emit_Wo_z(2)
        emit_xT(1)
        emit_FFN(1)
        emit_LN(2, 1); emit_apply1(2)
        emit_Y(3)
        emit_LN(1, 2); emit_apply2_out(1)
        emit_Wo_z(3)
        emit_xT(2)
        emit_FFN(2)
        emit_LN(3, 1); emit_apply1(3)
        emit_LN(2, 2); emit_apply2_out(2)
        emit_xT(3)
        emit_FFN(3)
        emit_LN(3, 2)
        emit_apply2_out(3)

        wait("sync", "so", cnt["so"])

        block = ctx.enter_context(nc.Block())

        @block.sync
        def _(e):
            for fn in streams["sync"]:
                fn(e)

        @block.tensor
        def _(e):
            for fn in streams["tensor"]:
                fn(e)

        @block.vector
        def _(e):
            for fn in streams["vector"]:
                fn(e)

        @block.scalar
        def _(e):
            for fn in streams["scalar"]:
                fn(e)

        @block.gpsimd
        def _(e):
            for fn in streams["gpsimd"]:
                fn(e)

    return nc


_NC_CACHE = None


def kernel(L0_attn_x, L0_attn_Wq, L0_attn_Wk, L0_attn_Wv, L0_attn_Wo,
           L0_ffn_U1, L0_ffn_V1, L0_ffn_b1, L0_ffn_U2, L0_ffn_V2, L0_ffn_b2):
    global _NC_CACHE
    x = np.ascontiguousarray(np.asarray(L0_attn_x, dtype=np.float32))
    wq = np.asarray(L0_attn_Wq, np.float32)
    wk = np.asarray(L0_attn_Wk, np.float32)
    wv = np.asarray(L0_attn_Wv, np.float32)
    wo = np.asarray(L0_attn_Wo, np.float32)
    u1 = np.ascontiguousarray(np.asarray(L0_ffn_U1, np.float32))
    v1 = np.ascontiguousarray(np.asarray(L0_ffn_V1, np.float32))
    u2 = np.ascontiguousarray(np.asarray(L0_ffn_U2, np.float32))
    v2 = np.ascontiguousarray(np.asarray(L0_ffn_V2, np.float32))
    b1 = np.asarray(L0_ffn_b1, np.float32)
    b2 = np.asarray(L0_ffn_b2, np.float32)

    wqT = np.ascontiguousarray(wq.T)
    wkT = np.ascontiguousarray(wk.T)
    wvT = np.ascontiguousarray(wv.T)
    woT = np.ascontiguousarray(wo.T)
    b1m = np.ascontiguousarray(b1.reshape(DFF // 128, 128).T)
    b2m = np.ascontiguousarray(b2.reshape(1, D))

    in_maps = []
    for c in range(NCORE):
        b, seg = divmod(c, SEGS)
        xs_f = np.ascontiguousarray(x[b, seg * TC:(seg + 1) * TC, :].T)
        xs_t = np.ascontiguousarray(x[b, seg * TC:(seg + 1) * TC, :])
        wm = np.zeros((128, NCORE), np.float32)
        for r in range(NCORE):
            if r // SEGS == b and r % SEGS < seg:
                wm[:, r] = 1.0
        in_maps.append({
            "x_f": xs_f, "x_t": xs_t,
            "wqT": wqT, "wkT": wkT, "wvT": wvT, "woT": woT,
            "u1": u1, "v1": v1, "u2": u2, "v2": v2,
            "b1i": b1m, "b2r": b2m, "wmi": wm,
        })

    if _NC_CACHE is None:
        _NC_CACHE = build_nc()
    res = run_bass_kernel_spmd(_NC_CACHE, in_maps, core_ids=list(range(NCORE)))
    out = np.empty((B, T, D), np.float32)
    for c in range(NCORE):
        b, seg = divmod(c, SEGS)
        out[b, seg * TC:(seg + 1) * TC, :] = \
            np.asarray(res.results[c]["out_t"], np.float32)
    return out


if __name__ == "__main__":
    nc = build_nc()
    print("built ok")


# version 27
# speedup vs baseline: 1.0225x; 1.0225x over previous
import numpy as np
from contextlib import ExitStack

import concourse.bass as bass
import concourse.mybir as mybir
from concourse.bass_utils import run_bass_kernel_spmd

F32 = mybir.dt.float32
BF16 = mybir.dt.bfloat16
AF = mybir.ActivationFunctionType
OP = mybir.AluOpType

B, T, D = 2, 8192, 1024
DKV = 512
DFF = 4096
RANK = 256
NCORE = 8
SEGS = 4
TC = T // SEGS
NSC = 4
SC = TC // NSC
EPS = 1e-5
AOFF = (0, 512, 896, 1152)
ANW = (512, 384, 256, 128)

ENGINES = ("tensor", "vector", "scalar", "gpsimd", "sync")


def build_nc():
    nc = bass.Bass("TRN2", target_bir_lowering=False, debug=False)

    x_f = nc.dram_tensor("x_f", [D, TC], F32, kind="ExternalInput").ap()
    x_t = nc.dram_tensor("x_t", [TC, D], F32, kind="ExternalInput").ap()
    wqT = nc.dram_tensor("wqT", [D, DKV], F32, kind="ExternalInput").ap()
    wkT = nc.dram_tensor("wkT", [D, DKV], F32, kind="ExternalInput").ap()
    wvT = nc.dram_tensor("wvT", [D, DKV], F32, kind="ExternalInput").ap()
    woT = nc.dram_tensor("woT", [DKV, D], F32, kind="ExternalInput").ap()
    u1 = nc.dram_tensor("u1", [D, RANK], F32, kind="ExternalInput").ap()
    v1 = nc.dram_tensor("v1", [RANK, DFF], F32, kind="ExternalInput").ap()
    u2 = nc.dram_tensor("u2", [DFF, RANK], F32, kind="ExternalInput").ap()
    v2 = nc.dram_tensor("v2", [RANK, D], F32, kind="ExternalInput").ap()
    b1i = nc.dram_tensor("b1i", [128, DFF // 128], F32, kind="ExternalInput").ap()
    b2r = nc.dram_tensor("b2r", [1, D], F32, kind="ExternalInput").ap()
    wmi = nc.dram_tensor("wmi", [128, NCORE], F32, kind="ExternalInput").ap()
    out_t = nc.dram_tensor("out_t", [TC, D], F32, kind="ExternalOutput").ap()
    cin_a = nc.dram_tensor("cin_a", [DKV, DKV], BF16)
    cin_b = nc.dram_tensor("cin_b", [DKV, DKV], BF16)
    cout_a = nc.dram_tensor("cout_a", [NCORE * DKV, DKV], BF16,
                            addr_space="Shared")
    cout_b = nc.dram_tensor("cout_b", [NCORE * DKV, DKV], BF16,
                            addr_space="Shared")

    ctx = ExitStack()
    with ctx:
        sbt = lambda n, s, d: ctx.enter_context(nc.sbuf_tensor(n, s, d))
        pstt = lambda n, s, d: ctx.enter_context(nc.psum_tensor(n, s, d))
        mksem = lambda n: ctx.enter_context(nc.semaphore(n))

        WKVU2 = sbt("WKVU2", [128, 8192], BF16)
        WO = sbt("WO", [128, 4, 1024], BF16)
        U1B = sbt("U1B", [128, 8, 256], BF16)
        V1B = sbt("V1B", [128, 2, 4096], BF16)
        V2B = sbt("V2B", [128, 2, 1024], BF16)
        VT = sbt("VT", [128, 16, 512], BF16)
        QK = sbt("QK", [128, 16, 512], BF16)
        AMP = sbt("AMP", [128, 4, 1280], BF16)
        MTB = sbt("MTB", [128, 16, 512], BF16)
        B1S = sbt("B1S", [128, DFF // 128], F32)
        B2R = sbt("B2R", [1, 1024], F32)
        B2B = sbt("B2B", [1, 1024], BF16)
        WMS = sbt("WMS", [128, NCORE], F32)
        MSK = sbt("MSK", [128, 128], BF16)
        IDN = sbt("IDN", [128, 128], BF16)
        IDN32 = sbt("IDN32", [128, 128], F32)
        ONER = sbt("ONER", [1, 128], BF16)
        ST = sbt("ST", [128, 128], F32)
        OUTS = sbt("OUTS", [128, 2, 1024], F32)

        ARN = ctx.enter_context(nc.sbuf_tensor("ARN", [128, 38912], BF16))
        arnb = ARN
        arnf = ARN.bitcast(F32)
        fap = arnf.ap()
        bap = arnb.ap()
        XT = fap[:, 0:4096].rearrange("p (c n) -> p c n", n=512)
        STG = fap[:, 4096:8192].rearrange("p (c n) -> p c n", n=2048)
        MR = fap[:, 8192:10240].rearrange("p (c n) -> p c n", n=512)
        MRf = fap[:, 8192:10240]
        WQ = bap[:, 20480:24576].rearrange("p (c n) -> p c n", n=512)
        XB = bap[:, 24576:28672].rearrange("p (c n) -> p c n", n=512)
        KT = bap[:, 28672:30720].rearrange("p (c n) -> p c n", n=512)
        KF = bap[:, 30720:38912].rearrange("p (c n) -> p c n", n=512)
        GST = bap[:, 8192:16384].rearrange("p (c n) -> p c n", n=512)
        XTOK = fap[:, 0:4096].rearrange("p (c n) -> p c n", n=1024)
        ZB_A = fap[:, 4096:8192].rearrange("p (c n) -> p c n", n=1024)
        ZB_B = fap[:, 15360:19456].rearrange("p (c n) -> p c n", n=1024)
        SQB = bap[:, 16384:17408].rearrange("p (c n) -> p c n", n=1024)
        PB = bap[:, 18432:19456].rearrange("p (c n) -> p c n", n=512)
        Q2B = bap[:, 19456:20480].rearrange("p (c n) -> p c n", n=512)
        XF = bap[:, 20480:24576].rearrange("p (c n) -> p c n", n=512)
        HB = bap[:, 24576:28672].rearrange("p (c n) -> p c n", n=512)
        YF = bap[:, 28672:30720].rearrange("p (c n) -> p c n", n=512)

        AMPf = AMP.bitcast(F32)
        wk_v = WKVU2[:, 0:4096].rearrange("p (c n) -> p c n", n=512)
        wv_v = WKVU2[:, 4096:8192].rearrange("p (c n) -> p c n", n=512)
        u2_v = WKVU2[:, :].rearrange("p (c n) -> p c n", n=256)

        psA = pstt("psA", [128, 512], F32)
        psB = pstt("psB", [128, 512], F32)
        psC = pstt("psC", [128, 512], F32)
        psD = pstt("psD", [128, 512], F32)
        psY0 = pstt("psY0", [128, 512], F32)
        psY1 = pstt("psY1", [128, 512], F32)
        psT0 = pstt("psT0", [128, 512], F32)
        psT1 = pstt("psT1", [128, 512], F32)
        psT0b = psT0.bitcast(BF16)
        psT1b = psT1.bitcast(BF16)

        sems = {n: mksem(n) for n in
                ["sx", "swa", "swb", "sba", "scc", "scd", "so",
                 "sp", "sv", "sa", "sq"]}

        streams = {e: [] for e in ENGINES}
        cnt = {n: 0 for n in sems}
        ENG_SEM = {"tensor": "sp", "vector": "sv", "scalar": "sa",
                   "gpsimd": "sq"}

        def emit(eng, fn):
            streams[eng].append(fn)

        def wait(eng, sem_name, val):
            if val is None or val <= 0:
                return
            s = sems[sem_name]
            emit(eng, lambda e, s=s, v=val: e.wait_ge(s, v))

        def tick(eng, builder, sem_name=None, inc=1):
            sn = sem_name or ENG_SEM[eng]
            s = sems[sn]
            cnt[sn] += inc
            v = cnt[sn]
            emit(eng, lambda e, b=builder, s=s, k=inc: b(e).then_inc(s, k))
            return v

        def quiet(eng, builder):
            emit(eng, lambda e, b=builder: b(e))

        def dma(eng, out_ap, in_ap, sem_name):
            s = sems[sem_name]
            cnt[sem_name] += 16
            v = cnt[sem_name]
            emit(eng, lambda e, o=out_ap, i=in_ap, s=s:
                 e.dma_start(out=o, in_=i).then_inc(s, 16))
            return v

        bankfree = {id(p): ("sv", 0) for p in
                    (psA, psB, psC, psD, psY0, psY1, psT0, psT1)}

        def mm_group(bank, mms, skip_check=False):
            sn, sval = bankfree[id(bank)]
            wait("tensor", sn, sval)
            v = None
            for i, (lh, rh, osl) in enumerate(mms):
                st_ = (i == 0)
                sp_ = (i == len(mms) - 1)
                fn = (lambda e, lh=lh, rh=rh, o=osl, st=st_, sp=sp_, sk=skip_check:
                      e.matmul(o, lh, rh, start=st, stop=sp, skip_group_check=sk))
                if sp_:
                    v = tick("tensor", fn)
                else:
                    quiet("tensor", fn)
            return v

        _c1 = tick("gpsimd", lambda g: g.memset(IDN[:, :], 0.0))
        wait("gpsimd", "sq", _c1)
        tick("gpsimd", lambda g: g.affine_select(
            out=IDN[:, :], in_=IDN[:, :], compare_op=OP.not_equal, fill=1.0,
            base=0, pattern=[[-1, 128]], channel_multiplier=1))
        _c1b = tick("gpsimd", lambda g: g.memset(IDN32[:, :], 0.0))
        wait("gpsimd", "sq", _c1b)
        tick("gpsimd", lambda g: g.affine_select(
            out=IDN32[:, :], in_=IDN32[:, :], compare_op=OP.not_equal, fill=1.0,
            base=0, pattern=[[-1, 128]], channel_multiplier=1))
        _c2 = tick("gpsimd", lambda g: g.memset(MSK[:, :], 1.0))
        wait("gpsimd", "sq", _c2)
        tick("gpsimd", lambda g: g.affine_select(
            out=MSK[:, :], in_=MSK[:, :], compare_op=OP.is_gt,
            fill=0.0, base=0, pattern=[[1, 128]], channel_multiplier=-1))
        consts_done = tick("gpsimd", lambda g: g.memset(ONER[0:1, :], 1.0))
        wait("scalar", "sq", consts_done)
        wait("vector", "sq", consts_done)
        wait("tensor", "sq", consts_done)

        _slots = [STG[:, 0, :], STG[:, 1, :],
                  AMPf.ap().rearrange("p c n -> p (c n)")[:, 0:2048]]
        _sema = ["swa", "swb", "sba"]
        wstate = {"i": 0, "cast": {}, "dmav": {}, "pend": [], "slotc_cast": None}

        def stage_issue(src2d, blocker=None):
            i = wstate["i"]
            h = i % 3
            prev = wstate["cast"].get(i - 3)
            if prev is not None:
                wait("gpsimd", prev[0], prev[1])
            if blocker is not None:
                wait("gpsimd", blocker[0], blocker[1])
            n = int(src2d.shape[0]) * int(src2d.shape[1]) // 128
            nin = int(src2d.shape[1])
            sv = _slots[h][:, 0:n].rearrange("p (c n) -> p c n", n=nin)
            wstate["dmav"][i] = (dma("gpsimd", sv,
                                     src2d.rearrange("(c p) n -> p c n", p=128),
                                     _sema[h]), _sema[h])
            wstate["i"] = i + 1
            return i

        def stage_cast(i, dst3d, ceng, blocker=None):
            h = i % 3
            dv, sm = wstate["dmav"][i]
            wait(ceng, sm, dv)
            if blocker is not None:
                wait(ceng, blocker[0], blocker[1])
            nch, nin = int(dst3d.shape[1]), int(dst3d.shape[2])
            sview = _slots[h][:, 0:nch * nin].rearrange(
                "p (c n) -> p c n", n=nin)
            if ceng == "scalar":
                v = tick(ceng, lambda e, d=dst3d, s=sview: e.copy(out=d, in_=s))
            else:
                v = tick(ceng, lambda e, d=dst3d, s=sview:
                         e.tensor_copy(out=d, in_=s))
            wstate["cast"][i] = (ENG_SEM[ceng], v)
            if i % 3 == 2:
                wstate["slotc_cast"] = (ENG_SEM[ceng], v)
            return v

        def run_weight_jobs(jlist, blocker=None, two_slot=False):
            for (src2d, dst3d, ceng) in jlist:
                if two_slot and wstate["i"] % 3 == 2:
                    wstate["i"] += 1
                i = stage_issue(src2d)
                stage_cast(i, dst3d, ceng, blocker=blocker)
                blocker = None

        jobs_kv = []
        for m, dstv in ((wkT, wk_v), (wvT, wv_v)):
            for c in range(2):
                jobs_kv.append((m[c * 512:(c + 1) * 512, :],
                                dstv[:, 4 * c:4 * c + 4, :], "scalar"))
        jobs_q = [(wqT[c * 512:(c + 1) * 512, :], WQ[:, 4 * c:4 * c + 4, :],
                   "scalar") for c in range(2)]
        jobs_rest = []
        for c in range(2):
            jobs_rest.append((woT[c * 256:(c + 1) * 256, :],
                              WO[:, 2 * c:2 * c + 2, :], "scalar"))
        jobs_rest.append((u1, U1B[:, :, :], "scalar"))
        for rc in range(2):
            for k in range(2):
                jobs_rest.append((v1[rc * 128:(rc + 1) * 128,
                                     k * 2048:(k + 1) * 2048],
                                  V1B[:, rc, k * 2048:(k + 1) * 2048]
                                  .rearrange("p (c n) -> p c n", c=1),
                                  "scalar"))
        jobs_u2 = [(u2[c * 1024:(c + 1) * 1024, :],
                    u2_v[:, 8 * c:8 * c + 8, :], "scalar") for c in range(4)]
        jobs_u2 += [(v2[c * 128:(c + 1) * 128, :],
                     V2B[:, c, :].rearrange("p (c n) -> p c n", c=1),
                     "scalar") for c in range(2)]

        dma("sync", B1S[:, :], b1i, "sx")
        dma("sync", B2R[0:1, :], b2r, "sx")
        v_wm = dma("sync", WMS[:, :], wmi, "sx")
        SBA_MISC = v_wm
        wait("scalar", "sx", SBA_MISC)
        b2b_v = tick("scalar", lambda e: e.copy(out=B2B[0:1, :], in_=B2R[0:1, :]))

        x_src = x_f.rearrange("d (s t) -> d s t", t=SC)
        st1 = {
            "xt_free": ("sv", 0),
            "xb_free_p": 0,
            "mr_v": 0,
            "kt_evac": {},
            "vt_evac": {},
            "qf_evac": {},
            "kf_evac": {},
            "mtb_snap": {},
        }
        kv_last_p = 0

        pre_vx = dma("sync", XT[:, :, :],
                     x_src[:, 0, :].rearrange("(c p) t -> p c t", p=128), "sx")
        run_weight_jobs(jobs_kv[:2])
        kvcast_done = cnt["sa"]
        run_weight_jobs(jobs_kv[2:])
        vvcast_done = cnt["sa"]

        for sc in range(NSC):
            if sc == 0:
                vx = pre_vx
                wait("vector", "sx", vx)
                xcast = tick("vector", lambda e: e.tensor_copy(
                    out=XB[:, :, :].rearrange("p c n -> p (c n)"),
                    in_=XT[:, :, :].rearrange("p c n -> p (c n)")))
                st1["xt_free"] = ("sv", xcast)
            else:
                xcast = st1["pre_cast"]
            if sc < 3:
                sn, v = st1["xt_free"]
                wait("sync", sn, v)
                st1["nvx"] = dma("sync", XT[:, :, :],
                                 x_src[:, sc + 1, :]
                                 .rearrange("(c p) t -> p c t", p=128), "sx")
            if sc == 0:
                run_weight_jobs(jobs_q)
            elif sc == 1:
                run_weight_jobs(jobs_rest[:6])
            elif sc == 2:
                run_weight_jobs(jobs_rest[6:])

            wait("tensor", "sv", xcast)
            if sc == 0:
                wait("tensor", "sa", kvcast_done)
            for tc in range(4):
                bank = psA if tc % 2 == 0 else psB
                gv = mm_group(bank, [
                    (XB[:, c, tc * 128:(tc + 1) * 128], wk_v[:, c, :],
                     bank[:, :]) for c in range(8)])
                wait("vector", "sp", gv)
                if sc > 0:
                    wait("vector", "sp", kv_last_p)
                ev = tick("vector", lambda e, b=bank, t=tc:
                          e.tensor_copy(out=KT[:, t, :], in_=b[:, :]))
                bankfree[id(bank)] = ("sv", ev)
                st1["kt_evac"][tc] = ev
            if sc == 0:
                wait("tensor", "sa", vvcast_done)
            for tc in range(4):
                bank = psA if tc % 2 == 0 else psB
                gv = mm_group(bank, [
                    (XB[:, c, tc * 128:(tc + 1) * 128], wv_v[:, c, :],
                     bank[:, :]) for c in range(8)])
                wait("vector", "sp", gv)
                ev = tick("vector", lambda e, b=bank, s=sc, t=tc:
                          e.tensor_copy(out=VT[:, 4 * s + t, :], in_=b[:, :]))
                bankfree[id(bank)] = ("sv", ev)
                st1["vt_evac"][(sc, tc)] = ev

            if sc == 0:
                wait("tensor", "sa", cnt["sa"])
            for jc in range(4):
                bank = psY0 if jc % 2 == 0 else psY1
                gv = mm_group(bank, [
                    (WQ[:, c, jc * 128:(jc + 1) * 128], XB[:, c, :],
                     bank[:, :]) for c in range(8)])
                wait("vector", "sp", gv)
                ev = tick("vector", lambda e, b=bank, s=sc, j=jc:
                          e.tensor_copy(out=QK[:, 4 * s + j, :], in_=b[:, :]))
                bankfree[id(bank)] = ("sv", ev)
                st1["qf_evac"][(sc, jc)] = ev
            st1["xb_free_p"] = cnt["sp"]

            if sc < 3:
                wait("vector", "sx", st1["nvx"])
                wait("vector", "sp", st1["xb_free_p"])
                ncast = tick("vector", lambda e: e.tensor_copy(
                    out=XB[:, :, :].rearrange("p c n -> p (c n)"),
                    in_=XT[:, :, :].rearrange("p c n -> p (c n)")))
                st1["xt_free"] = ("sv", ncast)
                st1["pre_cast"] = ncast

            for tc in range(4):
                wait("tensor", "sv", st1["kt_evac"][tc])
            for lc in range(4):
                bank = psT0 if lc % 2 == 0 else psT1
                bankb = psT0b if lc % 2 == 0 else psT1b
                sn, v = bankfree[id(bank)]
                wait("tensor", sn, v)
                gv = None
                for tcc in range(4):
                    fn = (lambda e, b=bankb, l=lc, t=tcc:
                          e.transpose(b[:, t * 128:(t + 1) * 128],
                                      KT[:, t, l * 128:(l + 1) * 128],
                                      IDN[:, :]))
                    if tcc == 3:
                        gv = tick("tensor", fn)
                    else:
                        quiet("tensor", fn)
                wait("scalar", "sp", gv)
                ev = tick("scalar", lambda e, b=bankb, s=sc, l=lc:
                          e.copy(out=KF[:, 4 * s + l, :], in_=b[:, 0:512]))
                bankfree[id(bank)] = ("sa", ev)
                st1["kf_evac"][(sc, lc)] = ev
            for tc in range(4):
                wait("tensor", "sv", st1["kt_evac"][tc])
                wait("tensor", "sv", st1["vt_evac"][(sc, tc)])
            for kc in range(4):
                bank = psC if kc % 2 == 0 else psD
                gv = mm_group(bank, [
                    (KT[:, tc, kc * 128:(kc + 1) * 128], VT[:, 4 * sc + tc, :],
                     bank[:, :]) for tc in range(4)])
                wait("vector", "sp", gv)
                wait("vector", "sv", st1["mr_v"])
                if sc == 0:
                    mv = tick("vector", lambda e, b=bank, k=kc:
                              e.tensor_copy(out=MR[:, k, :], in_=b[:, :]))
                elif sc < 3:
                    mv = tick("vector", lambda e, b=bank, k=kc:
                              e.tensor_tensor(out=MR[:, k, :], in0=MR[:, k, :],
                                              in1=b[:, :], op=OP.add))
                else:
                    mv0 = tick("vector", lambda e, b=bank, k=kc:
                               e.tensor_tensor(out=MR[:, k, :], in0=MR[:, k, :],
                                               in1=b[:, :], op=OP.add))
                    wait("vector", "sv", mv0)
                    mv = tick("vector", lambda e, k=kc:
                              e.tensor_copy(out=MTB[:, k, :], in_=MR[:, k, :]))
                    wait("scalar", "sv", mv)
                    st1["cinb_dma"] = dma(
                        "scalar",
                        cin_b[kc * 128:(kc + 1) * 128, :]
                        .rearrange("(c p) i -> p c i", p=128),
                        MTB[:, kc:kc + 1, :], "sba")
                bankfree[id(bank)] = ("sv", mv)
                st1["mr_v"] = mv
            if sc == 3:
                wait("gpsimd", "sba", st1["cinb_dma"])
                tick("gpsimd", lambda e: e.collective_compute(
                    "AllGather", OP.bypass,
                    replica_groups=[list(range(NCORE))],
                    ins=[cin_b.ap()], outs=[cout_b.ap()]), sem_name="scd")

            if sc < 3:
                wait("scalar", "sv", st1["mr_v"])
                snap = tick("scalar", lambda e, s=sc: e.copy(
                    out=MTB[:, 4 * (s + 1):4 * (s + 2), :]
                    .rearrange("p c n -> p (c n)"), in_=MRf))
                st1["mtb_snap"][sc + 1] = snap
            kv_last_p = cnt["sp"]

        sn, v = st1["xt_free"]
        wait("sync", sn, v)
        xtok_v = dma("sync", XTOK[:, :, :],
                     x_t[0:SC, :].rearrange("(c p) n -> p c n", p=128), "sx")

        run_weight_jobs(jobs_u2, blocker=("sp", kv_last_p),
                        two_slot=True)
        w_all_sq = ("sa", cnt["sa"])

        if wstate["slotc_cast"] is not None:
            wait("vector", wstate["slotc_cast"][0], wstate["slotc_cast"][1])
        amask = {}
        for sc in range(NSC):
            for lc in range(4):
                wait("tensor", "sa", st1["kf_evac"][(sc, lc)])
                wait("tensor", "sv", st1["qf_evac"][(sc, lc)])
        for sc in range(NSC):
            for sj in range(4):
                nw = ANW[sj]
                bank = psA if sj % 2 == 0 else psB
                gv = mm_group(bank, [
                    (KF[:, 4 * sc + lc, sj * 128:(sj + 1) * 128],
                     QK[:, 4 * sc + lc, sj * 128:512], bank[:, 0:nw])
                    for lc in range(4)])
                off = AOFF[sj]
                wait("vector", "sp", gv)
                quiet("vector", lambda e, b=bank, s=sc, o=off:
                      e.tensor_tensor(out=AMP[:, s, o:o + 128],
                                      in0=b[:, 0:128], in1=MSK[:, :],
                                      op=OP.mult))
                if nw > 128:
                    mv = tick("vector", lambda e, b=bank, s=sc, o=off, n=nw:
                              e.tensor_copy(out=AMP[:, s, o + 128:o + n],
                                            in_=b[:, 128:n]))
                else:
                    mv = tick("vector", lambda e: e.engine_nop())
                bankfree[id(bank)] = ("sv", mv)
                amask[(sc, sj)] = mv

        for i, cv in list(wstate["cast"].items())[-3:]:
            wait("sync", cv[0], cv[1])
        wait("vector", "sx", SBA_MISC)
        wait("vector", "sv", st1["mr_v"])
        wait("vector", "sa", st1["mtb_snap"][3])
        vstg = [None, None, None, None]
        mg = 0
        for r in range(NCORE):
            h = r % 4
            cc_sem = "scd"
            cc_src = cout_b
            rr = r % NCORE
            if vstg[h] is not None:
                wait("sync", "sv", vstg[h])
            wait("sync", cc_sem, 1)
            vg = dma("sync", GST[:, 4 * h:4 * h + 4, :],
                     cc_src[rr * DKV:(rr + 1) * DKV, :]
                     .rearrange("(c p) i -> p c i", p=128), "sba")
            wait("vector", "sba", vg)
            gfl = GST[:, 4 * h:4 * h + 4, :].rearrange("p c n -> p (c n)")
            if r == 0:
                mg = tick("vector", lambda e, g=gfl: e.tensor_scalar(
                    out=MRf, in0=g, scalar1=WMS[:, 0:1], scalar2=None,
                    op0=OP.mult))
            else:
                wait("vector", "sv", mg)
                mg = tick("vector", lambda e, g=gfl, rr=rr:
                          e.scalar_tensor_tensor(
                              out=MRf, in0=g, scalar=WMS[:, rr:rr + 1],
                              in1=MRf, op0=OP.mult, op1=OP.add))
            vstg[h] = mg
        mtb_fin = {}
        wait("vector", "sv", mg)
        wait("vector", "sba", st1["cinb_dma"])
        f0 = tick("vector", lambda e: e.tensor_copy(
            out=MTB[:, 0:4, :].rearrange("p c n -> p (c n)"), in_=MRf))
        mtb_fin[0] = f0
        for s in range(1, 4):
            wait("vector", "sa", st1["mtb_snap"][s])
            wait("vector", "sv", mtb_fin[s - 1])
            fs = tick("vector", lambda e, s=s: e.tensor_tensor(
                out=MTB[:, 4 * s:4 * (s + 1), :].rearrange("p c n -> p (c n)"),
                in0=MRf,
                in1=MTB[:, 4 * s:4 * (s + 1), :].rearrange("p c n -> p (c n)"),
                op=OP.add))
            mtb_fin[s] = fs
        mtb_fin_last = mtb_fin[3]

        st3 = {
            "yf_free": ("sa", 0),
            "wo_last": {},
            "zadd": {},
            "zadd_last": {},
            "apply": {},
            "apply_last": {},
            "rstd1": {}, "rstd2": {},
            "xf_evac": {},
            "gelu": {},
            "u2_tick": {},
            "q2_evac": {},
            "pb_evac": {},
            "z2add": {},
            "z2_last": {},
            "z2_last": {},
            "apply2": {},
            "outs_dma": [0, 0],
            "apply2_last": {},
            "xtok_v": {0: xtok_v},
            "y_last": {},
            "ffn_last": {},
        }

        def ZB(sc):
            return ZB_A if sc % 2 == 0 else ZB_B

        def stc(sc, which, i):
            base = (sc % 2) * 64 + (0 if which == 1 else 32)
            return ST[:, base + i:base + i + 1]

        def stc4(sc, which, i0):
            base = (sc % 2) * 64 + (0 if which == 1 else 32)
            return ST[:, base + i0:base + i0 + 4]

        def emit_Y(sc):
            wait("tensor", "sv", mtb_fin[sc])
            for sj in range(4):
                wait("tensor", "sv", amask[(sc, sj)])
            sn, v = st3["yf_free"]
            for vfc in range(4):
                bank = psY0 if vfc % 2 == 0 else psY1
                bsn, bv = bankfree[id(bank)]
                wait("tensor", bsn, bv)
                quiet("tensor", lambda e, b=bank, s=sc, i=vfc:
                      e.matmul(b[:, 0:512], VT[:, 4 * s, i * 128:(i + 1) * 128],
                               AMP[:, s, 0:512], start=True, stop=False))
                for tk in range(1, 4):
                    quiet("tensor", lambda e, b=bank, s=sc, i=vfc, t=tk:
                          e.matmul(b[:, 128 * t:512],
                                   VT[:, 4 * s + t, i * 128:(i + 1) * 128],
                                   AMP[:, s, AOFF[t]:AOFF[t] + ANW[t]],
                                   start=False, stop=False))
                gv = None
                for lc in range(4):
                    fn = (lambda e, b=bank, s=sc, i=vfc, l=lc, sp_=(lc == 3):
                          e.matmul(b[:, 0:512],
                                   MTB[:, 4 * s + l, i * 128:(i + 1) * 128],
                                   QK[:, 4 * s + l, :], start=False, stop=sp_))
                    if lc == 3:
                        gv = tick("tensor", fn)
                    else:
                        quiet("tensor", fn)
                wait("scalar", "sp", gv)
                sn, v = st3["yf_free"]
                wait("scalar", sn, v)
                ev = tick("scalar", lambda e, b=bank, i=vfc:
                          e.copy(out=YF[:, i, :], in_=b[:, :]))
                bankfree[id(bank)] = ("sa", ev)
            st3["y_last"][sc] = cnt["sa"]

        def emit_Wo_z(sc):
            yev = st3["y_last"][sc]
            wait("tensor", "sa", yev)
            banks = (psA, psB, psC, psT0, psT1)
            gi = 0
            for tc in range(4):
                zv_h = []
                for h in range(2):
                    bank = banks[gi % 5]
                    gi += 1
                    gv = mm_group(bank, [
                        (YF[:, ic, tc * 128:(tc + 1) * 128],
                         WO[:, ic, h * 512:(h + 1) * 512], bank[:, :])
                        for ic in range(4)])
                    wait("vector", "sp", gv)
                    wait("vector", "sx", st3["xtok_v"][sc])
                    if sc >= 2 and tc == 0 and h == 0:
                        wait("vector", "sv", st3["apply2_last"][sc - 2])
                    zv = tick("vector", lambda e, b=bank, s=sc, t=tc, hh=h:
                              e.scalar_tensor_tensor(
                                  out=ZB(s)[:, t, hh * 512:(hh + 1) * 512],
                                  in0=b[:, :], scalar=1.0,
                                  in1=XTOK[:, t, hh * 512:(hh + 1) * 512],
                                  op0=OP.mult, op1=OP.add,
                                  accum_out=stc(s, 1, 4 * hh + t)))
                    bankfree[id(bank)] = ("sv", zv)
                    zv_h.append(zv)
                st3["zadd"][(sc, tc)] = zv_h[-1]
            st3["zadd_last"][sc] = cnt["sv"]
            st3["wo_last"][sc] = cnt["sp"]
            if sc < NSC - 1:
                wait("sync", "sv", st3["zadd_last"][sc])
                st3["xtok_v"][sc + 1] = dma(
                    "sync", XTOK[:, :, :],
                    x_t[(sc + 1) * SC:(sc + 2) * SC, :]
                    .rearrange("(c p) n -> p c n", p=128), "sx")

        def emit_LN(sc, which):
            zb = ZB(sc)
            zlast = (st3["zadd_last"][sc] if which == 1
                     else st3["z2_last"][sc])
            sqv = {}
            for tc in range(4):
                if which == 1 and sc == 0 and tc == 0:
                    wait("scalar", "sv", mtb_fin_last)
                wait("scalar", "sv",
                     st3["zadd"][(sc, tc)] if which == 1
                     else st3["z2add"][(sc, tc, 1)])
                sqv[tc] = tick("scalar", lambda e, t=tc, z=zb, s=sc, w=which:
                               e.activation(out=SQB[:, 0, :], in_=z[:, t, :],
                                            func=AF.Square,
                                            accum_out=stc(s, w, 8 + t)))
            wait("vector", "sv", zlast)
            zs = tick("vector", lambda e, s=sc, w=which: e.tensor_tensor(
                out=stc4(s, w, 0), in0=stc4(s, w, 0), in1=stc4(s, w, 4),
                op=OP.add))
            wait("vector", "sv", zs)
            mean = tick("vector", lambda e, s=sc, w=which: e.tensor_scalar(
                out=stc4(s, w, 12), in0=stc4(s, w, 0), scalar1=1.0 / 1024.0,
                scalar2=None, op0=OP.mult))
            wait("vector", "sv", mean)
            msq = tick("vector", lambda e, s=sc, w=which: e.tensor_tensor(
                out=stc4(s, w, 16), in0=stc4(s, w, 12), in1=stc4(s, w, 12),
                op=OP.mult))
            for tc in range(4):
                wait("vector", "sa", sqv[tc])
            wait("vector", "sv", msq)
            varv = tick("vector", lambda e, s=sc, w=which: e.scalar_tensor_tensor(
                out=stc4(s, w, 16), in0=stc4(s, w, 8), scalar=1.0 / 1024.0,
                in1=stc4(s, w, 16), op0=OP.mult, op1=OP.subtract))
            wait("vector", "sv", varv)
            vee = tick("vector", lambda e, s=sc, w=which: e.tensor_scalar(
                out=stc4(s, w, 16), in0=stc4(s, w, 16), scalar1=EPS,
                scalar2=None, op0=OP.add))
            wait("vector", "sv", vee)
            inv = tick("vector", lambda e, s=sc, w=which: e.reciprocal(
                out=stc4(s, w, 20), in_=stc4(s, w, 16)))
            wait("scalar", "sv", inv)
            rst = tick("scalar", lambda e, s=sc, w=which: e.activation(
                out=stc4(s, w, 16), in_=stc4(s, w, 20), func=AF.Sqrt))
            if which == 1:
                st3["rstd1"][sc] = rst
            else:
                st3["rstd2"][sc] = rst
            return rst

        def emit_apply1(sc):
            rst = st3["rstd1"][sc]
            zb = ZB(sc)
            for tc in range(4):
                wait("vector", "sa", rst)
                av = tick("vector", lambda e, t=tc, z=zb, s=sc: e.tensor_scalar(
                    out=z[:, t, :], in0=z[:, t, :],
                    scalar1=stc(s, 1, 12 + t), scalar2=stc(s, 1, 16 + t),
                    op0=OP.subtract, op1=OP.mult))
                st3["apply"][(sc, tc)] = av
            st3["apply_last"][sc] = cnt["sv"]

        def emit_xT(sc):
            zb = ZB(sc)
            for dc in range(8):
                bank = psY0 if dc % 2 == 0 else psY1
                bsn, bv = bankfree[id(bank)]
                wait("tensor", bsn, bv)
                for tj in range(4):
                    if dc == 0:
                        wait("tensor", "sv", st3["apply"][(sc, tj)])
                gv = None
                for tj in range(4):
                    fn = (lambda e, b=bank, z=zb, t=tj, d=dc:
                          e.transpose(b[:, t * 128:(t + 1) * 128]
                                      .bitcast(F32) if False else
                                      b[:, t * 128:(t + 1) * 128],
                                      z[:, t, d * 128:(d + 1) * 128],
                                      IDN32[:, :]))
                    if tj == 3:
                        gv = tick("tensor", fn)
                    else:
                        quiet("tensor", fn)
                wait("scalar", "sp", gv)
                if sc == 0 and dc == 0:
                    wait("scalar", "sp", st1["xb_free_p"])
                ev = tick("scalar", lambda e, b=bank, d=dc:
                          e.copy(out=XF[:, d, :], in_=b[:, :]))
                bankfree[id(bank)] = ("sa", ev)
                st3["xf_evac"][(sc, dc)] = ev

        def emit_FFN(sc):
            for dc in range(8):
                wait("tensor", "sa", st3["xf_evac"][(sc, dc)])
            if sc == 0:
                wait("tensor", w_all_sq[0], w_all_sq[1])
                wait("tensor", "sa", b2b_v)
            for rc in range(2):
                bank = psC if rc == 0 else psD
                gv = mm_group(bank, [
                    (U1B[:, c, rc * 128:(rc + 1) * 128], XF[:, c, :],
                     bank[:, :]) for c in range(8)])
                wait("scalar", "sp", gv)
                if sc == 0 and rc == 0:
                    wait("scalar", "sv", mtb_fin_last)
                ev = tick("scalar", lambda e, b=bank, r=rc:
                          e.copy(out=PB[:, r, :], in_=b[:, :]))
                bankfree[id(bank)] = ("sa", ev)
                st3["pb_evac"][(sc, rc)] = ev
            for rc in range(2):
                wait("tensor", "sa", st3["pb_evac"][(sc, rc)])
            v1banks = (psY0, psY1, psA)

            def emit_V1(fc):
                vbank = v1banks[fc % 3]
                bsn, bv = bankfree[id(vbank)]
                wait("tensor", bsn, bv)
                gv = mm_group(vbank, [
                    (V1B[:, rc, fc * 128:(fc + 1) * 128], PB[:, rc, :],
                     vbank[:, :]) for rc in range(2)])
                wait("scalar", "sp", gv)
                if fc >= 8:
                    wait("scalar", "sp", st3["u2_tick"][(sc, fc - 8)])
                hslot = fc % 8
                ge = tick("scalar", lambda e, b=vbank, h=hslot, f=fc:
                          e.activation(out=HB[:, h, :], in_=b[:, :],
                                       func=AF.Gelu_apprx_tanh,
                                       bias=B1S[:, f:f + 1]))
                st3["gelu"][(sc, fc)] = ge
                bankfree[id(vbank)] = ("sa", ge)

            def emit_U2(fc):
                hslot = fc % 8
                wait("tensor", "sa", st3["gelu"][(sc, fc)])
                quiet("tensor", lambda e, f=fc, h=hslot, st_=(fc == 0):
                      e.matmul(psC[:, :], u2_v[:, f, 0:128], HB[:, h, :],
                               start=st_, stop=(f == 31),
                               skip_group_check=True))
                fn = (lambda e, f=fc, h=hslot, st_=(fc == 0):
                      e.matmul(psD[:, :], u2_v[:, f, 128:256], HB[:, h, :],
                               start=st_, stop=(f == 31),
                               skip_group_check=True))
                st3["u2_tick"][(sc, fc)] = tick("tensor", fn)

            emit_V1(0)
            emit_V1(1)
            for fc in range(32):
                if fc + 2 < 32:
                    emit_V1(fc + 2)
                emit_U2(fc)
            u2l = st3["u2_tick"][(sc, 31)]
            wait("scalar", "sp", u2l)
            q2a = tick("scalar", lambda e: e.copy(out=Q2B[:, 0, :], in_=psC[:, :]))
            q2b = tick("scalar", lambda e: e.copy(out=Q2B[:, 1, :], in_=psD[:, :]))
            bankfree[id(psC)] = ("sa", q2b)
            bankfree[id(psD)] = ("sa", q2b)
            st3["q2_evac"][sc] = q2b
            zb = ZB(sc)
            banks = (psA, psB, psT0, psT1)
            for tc in range(4):
                for h in range(2):
                    bank = banks[(2 * tc + h) % 4]
                    bsn, bv = bankfree[id(bank)]
                    wait("tensor", bsn, bv)
                    quiet("tensor", lambda e, b=bank, hh=h:
                          e.matmul(b[:, :], ONER[0:1, :],
                                   B2B[0:1, hh * 512:(hh + 1) * 512],
                                   start=True, stop=False))
                    if tc == 0 and h == 0:
                        wait("tensor", "sa", q2b)
                    gv = None
                    for rc in range(2):
                        fn = (lambda e, b=bank, t=tc, hh=h, r=rc, sp_=(rc == 1):
                              e.matmul(b[:, :], Q2B[:, r, t * 128:(t + 1) * 128],
                                       V2B[:, r, hh * 512:(hh + 1) * 512],
                                       start=False, stop=sp_))
                        if rc == 1:
                            gv = tick("tensor", fn)
                        else:
                            quiet("tensor", fn)
                    wait("vector", "sp", gv)
                    wait("vector", "sv", st3["apply"][(sc, tc)])
                    zv = tick("vector", lambda e, b=bank, z=zb, t=tc, hh=h, s=sc:
                              e.scalar_tensor_tensor(
                                  out=z[:, t, hh * 512:(hh + 1) * 512],
                                  in0=b[:, :], scalar=1.0,
                                  in1=z[:, t, hh * 512:(hh + 1) * 512],
                                  op0=OP.mult, op1=OP.add,
                                  accum_out=stc(s, 2, 4 * hh + t)))
                    bankfree[id(bank)] = ("sv", zv)
                    st3["z2add"][(sc, tc, h)] = zv
            st3["z2_last"][sc] = cnt["sv"]
            st3["ffn_last"][sc] = cnt["sp"]

        def emit_apply2_out(sc):
            rst = st3["rstd2"][sc]
            zb = ZB(sc)
            for tc in range(4):
                wait("vector", "sa", rst)
                slot = tc % 2
                wait("vector", "so", st3["outs_dma"][slot])
                av = tick("vector", lambda e, t=tc, z=zb, s=sc, sl=slot:
                          e.tensor_scalar(
                              out=OUTS[:, sl, :], in0=z[:, t, :],
                              scalar1=stc(s, 2, 12 + t),
                              scalar2=stc(s, 2, 16 + t),
                              op0=OP.subtract, op1=OP.mult))
                st3["apply2"][(sc, tc)] = av
                wait("scalar", "sv", av)
                st3["outs_dma"][slot] = dma(
                    "scalar",
                    out_t[sc * SC + tc * 128: sc * SC + (tc + 1) * 128, :],
                    OUTS[:, slot, :], "so")
            st3["apply2_last"][sc] = cnt["sv"]

        emit_Y(0)
        emit_Wo_z(0)
        emit_Y(1)
        emit_LN(0, 1); emit_apply1(0)
        emit_Wo_z(1)
        emit_xT(0)
        emit_FFN(0)
        emit_LN(1, 1); emit_apply1(1)
        emit_Y(2)
        emit_LN(0, 2); emit_apply2_out(0)
        emit_Wo_z(2)
        emit_xT(1)
        emit_FFN(1)
        emit_LN(2, 1); emit_apply1(2)
        emit_Y(3)
        emit_LN(1, 2); emit_apply2_out(1)
        emit_Wo_z(3)
        emit_xT(2)
        emit_FFN(2)
        emit_LN(3, 1); emit_apply1(3)
        emit_LN(2, 2); emit_apply2_out(2)
        emit_xT(3)
        emit_FFN(3)
        emit_LN(3, 2)
        emit_apply2_out(3)

        wait("sync", "so", cnt["so"])

        block = ctx.enter_context(nc.Block())

        @block.sync
        def _(e):
            for fn in streams["sync"]:
                fn(e)

        @block.tensor
        def _(e):
            for fn in streams["tensor"]:
                fn(e)

        @block.vector
        def _(e):
            for fn in streams["vector"]:
                fn(e)

        @block.scalar
        def _(e):
            for fn in streams["scalar"]:
                fn(e)

        @block.gpsimd
        def _(e):
            for fn in streams["gpsimd"]:
                fn(e)

    return nc


_NC_CACHE = None


def kernel(L0_attn_x, L0_attn_Wq, L0_attn_Wk, L0_attn_Wv, L0_attn_Wo,
           L0_ffn_U1, L0_ffn_V1, L0_ffn_b1, L0_ffn_U2, L0_ffn_V2, L0_ffn_b2):
    global _NC_CACHE
    x = np.ascontiguousarray(np.asarray(L0_attn_x, dtype=np.float32))
    wq = np.asarray(L0_attn_Wq, np.float32)
    wk = np.asarray(L0_attn_Wk, np.float32)
    wv = np.asarray(L0_attn_Wv, np.float32)
    wo = np.asarray(L0_attn_Wo, np.float32)
    u1 = np.ascontiguousarray(np.asarray(L0_ffn_U1, np.float32))
    v1 = np.ascontiguousarray(np.asarray(L0_ffn_V1, np.float32))
    u2 = np.ascontiguousarray(np.asarray(L0_ffn_U2, np.float32))
    v2 = np.ascontiguousarray(np.asarray(L0_ffn_V2, np.float32))
    b1 = np.asarray(L0_ffn_b1, np.float32)
    b2 = np.asarray(L0_ffn_b2, np.float32)

    wqT = np.ascontiguousarray(wq.T)
    wkT = np.ascontiguousarray(wk.T)
    wvT = np.ascontiguousarray(wv.T)
    woT = np.ascontiguousarray(wo.T)
    b1m = np.ascontiguousarray(b1.reshape(DFF // 128, 128).T)
    b2m = np.ascontiguousarray(b2.reshape(1, D))

    in_maps = []
    for c in range(NCORE):
        b, seg = divmod(c, SEGS)
        xs_f = np.ascontiguousarray(x[b, seg * TC:(seg + 1) * TC, :].T)
        xs_t = np.ascontiguousarray(x[b, seg * TC:(seg + 1) * TC, :])
        wm = np.zeros((128, NCORE), np.float32)
        for r in range(NCORE):
            if r // SEGS == b and r % SEGS < seg:
                wm[:, r] = 1.0
        in_maps.append({
            "x_f": xs_f, "x_t": xs_t,
            "wqT": wqT, "wkT": wkT, "wvT": wvT, "woT": woT,
            "u1": u1, "v1": v1, "u2": u2, "v2": v2,
            "b1i": b1m, "b2r": b2m, "wmi": wm,
        })

    if _NC_CACHE is None:
        _NC_CACHE = build_nc()
    res = run_bass_kernel_spmd(_NC_CACHE, in_maps, core_ids=list(range(NCORE)))
    out = np.empty((B, T, D), np.float32)
    for c in range(NCORE):
        b, seg = divmod(c, SEGS)
        out[b, seg * TC:(seg + 1) * TC, :] = \
            np.asarray(res.results[c]["out_t"], np.float32)
    return out


if __name__ == "__main__":
    nc = build_nc()
    print("built ok")


# version 28
# speedup vs baseline: 1.0263x; 1.0037x over previous
import numpy as np
from contextlib import ExitStack

import concourse.bass as bass
import concourse.mybir as mybir
from concourse.bass_utils import run_bass_kernel_spmd

F32 = mybir.dt.float32
BF16 = mybir.dt.bfloat16
AF = mybir.ActivationFunctionType
OP = mybir.AluOpType

B, T, D = 2, 8192, 1024
DKV = 512
DFF = 4096
RANK = 256
NCORE = 8
SEGS = 4
TC = T // SEGS
NSC = 4
SC = TC // NSC
EPS = 1e-5
AOFF = (0, 512, 896, 1152)
ANW = (512, 384, 256, 128)

ENGINES = ("tensor", "vector", "scalar", "gpsimd", "sync")


def build_nc():
    nc = bass.Bass("TRN2", target_bir_lowering=False, debug=False)

    x_f = nc.dram_tensor("x_f", [D, TC], F32, kind="ExternalInput").ap()
    x_t = nc.dram_tensor("x_t", [TC, D], F32, kind="ExternalInput").ap()
    wqT = nc.dram_tensor("wqT", [D, DKV], F32, kind="ExternalInput").ap()
    wkT = nc.dram_tensor("wkT", [D, DKV], F32, kind="ExternalInput").ap()
    wvT = nc.dram_tensor("wvT", [D, DKV], F32, kind="ExternalInput").ap()
    woT = nc.dram_tensor("woT", [DKV, D], F32, kind="ExternalInput").ap()
    u1 = nc.dram_tensor("u1", [D, RANK], F32, kind="ExternalInput").ap()
    v1 = nc.dram_tensor("v1", [RANK, DFF], F32, kind="ExternalInput").ap()
    u2 = nc.dram_tensor("u2", [DFF, RANK], F32, kind="ExternalInput").ap()
    v2 = nc.dram_tensor("v2", [RANK, D], F32, kind="ExternalInput").ap()
    b1i = nc.dram_tensor("b1i", [128, DFF // 128], F32, kind="ExternalInput").ap()
    b2r = nc.dram_tensor("b2r", [1, D], F32, kind="ExternalInput").ap()
    wmi = nc.dram_tensor("wmi", [128, NCORE], F32, kind="ExternalInput").ap()
    out_t = nc.dram_tensor("out_t", [TC, D], F32, kind="ExternalOutput").ap()
    cin_a = nc.dram_tensor("cin_a", [DKV, DKV], BF16)
    cin_b = nc.dram_tensor("cin_b", [DKV, DKV], BF16)
    cout_a = nc.dram_tensor("cout_a", [NCORE * DKV, DKV], BF16,
                            addr_space="Shared")
    cout_b = nc.dram_tensor("cout_b", [NCORE * DKV, DKV], BF16,
                            addr_space="Shared")

    ctx = ExitStack()
    with ctx:
        sbt = lambda n, s, d: ctx.enter_context(nc.sbuf_tensor(n, s, d))
        pstt = lambda n, s, d: ctx.enter_context(nc.psum_tensor(n, s, d))
        mksem = lambda n: ctx.enter_context(nc.semaphore(n))

        WKVU2 = sbt("WKVU2", [128, 8192], BF16)
        WO = sbt("WO", [128, 4, 1024], BF16)
        U1B = sbt("U1B", [128, 8, 256], BF16)
        V1B = sbt("V1B", [128, 2, 4096], BF16)
        V2B = sbt("V2B", [128, 2, 1024], BF16)
        VT = sbt("VT", [128, 16, 512], BF16)
        QK = sbt("QK", [128, 16, 512], BF16)
        AMP = sbt("AMP", [128, 4, 1280], BF16)
        MTB = sbt("MTB", [128, 16, 512], BF16)
        B1S = sbt("B1S", [128, DFF // 128], F32)
        B2R = sbt("B2R", [1, 1024], F32)
        B2B = sbt("B2B", [1, 1024], BF16)
        WMS = sbt("WMS", [128, NCORE], F32)
        MSK = sbt("MSK", [128, 128], BF16)
        IDN = sbt("IDN", [128, 128], BF16)
        IDN32 = sbt("IDN32", [128, 128], F32)
        ONER = sbt("ONER", [1, 128], BF16)
        ST = sbt("ST", [128, 128], F32)
        OUTS = sbt("OUTS", [128, 2, 1024], F32)

        ARN = ctx.enter_context(nc.sbuf_tensor("ARN", [128, 38912], BF16))
        arnb = ARN
        arnf = ARN.bitcast(F32)
        fap = arnf.ap()
        bap = arnb.ap()
        XT = fap[:, 0:4096].rearrange("p (c n) -> p c n", n=512)
        STG = fap[:, 4096:8192].rearrange("p (c n) -> p c n", n=2048)
        MR = fap[:, 8192:10240].rearrange("p (c n) -> p c n", n=512)
        MRf = fap[:, 8192:10240]
        WQ = bap[:, 20480:24576].rearrange("p (c n) -> p c n", n=512)
        XB = bap[:, 24576:28672].rearrange("p (c n) -> p c n", n=512)
        KT = bap[:, 28672:30720].rearrange("p (c n) -> p c n", n=512)
        KF = bap[:, 30720:38912].rearrange("p (c n) -> p c n", n=512)
        GST = bap[:, 8192:16384].rearrange("p (c n) -> p c n", n=512)
        XTOK = fap[:, 0:4096].rearrange("p (c n) -> p c n", n=1024)
        ZB_A = fap[:, 4096:8192].rearrange("p (c n) -> p c n", n=1024)
        ZB_B = fap[:, 15360:19456].rearrange("p (c n) -> p c n", n=1024)
        SQB = bap[:, 16384:17408].rearrange("p (c n) -> p c n", n=1024)
        PB = bap[:, 18432:19456].rearrange("p (c n) -> p c n", n=512)
        Q2B = bap[:, 19456:20480].rearrange("p (c n) -> p c n", n=512)
        XF = bap[:, 20480:24576].rearrange("p (c n) -> p c n", n=512)
        HB = bap[:, 24576:28672].rearrange("p (c n) -> p c n", n=512)
        YF = bap[:, 28672:30720].rearrange("p (c n) -> p c n", n=512)

        AMPf = AMP.bitcast(F32)
        wk_v = WKVU2[:, 0:4096].rearrange("p (c n) -> p c n", n=512)
        wv_v = WKVU2[:, 4096:8192].rearrange("p (c n) -> p c n", n=512)
        u2_v = WKVU2[:, :].rearrange("p (c n) -> p c n", n=256)

        psA = pstt("psA", [128, 512], F32)
        psB = pstt("psB", [128, 512], F32)
        psC = pstt("psC", [128, 512], F32)
        psD = pstt("psD", [128, 512], F32)
        psY0 = pstt("psY0", [128, 512], F32)
        psY1 = pstt("psY1", [128, 512], F32)
        psT0 = pstt("psT0", [128, 512], F32)
        psT1 = pstt("psT1", [128, 512], F32)
        psT0b = psT0.bitcast(BF16)
        psT1b = psT1.bitcast(BF16)

        sems = {n: mksem(n) for n in
                ["sx", "swa", "swb", "sba", "sbm", "sg0", "sg1", "sg2", "sg3", "so0", "so1", "scc", "scd", "so",
                 "sp", "sv", "sa", "sq"]}

        streams = {e: [] for e in ENGINES}
        cnt = {n: 0 for n in sems}
        ENG_SEM = {"tensor": "sp", "vector": "sv", "scalar": "sa",
                   "gpsimd": "sq"}

        def emit(eng, fn):
            streams[eng].append(fn)

        def wait(eng, sem_name, val):
            if val is None or val <= 0:
                return
            s = sems[sem_name]
            emit(eng, lambda e, s=s, v=val: e.wait_ge(s, v))

        def tick(eng, builder, sem_name=None, inc=1):
            sn = sem_name or ENG_SEM[eng]
            s = sems[sn]
            cnt[sn] += inc
            v = cnt[sn]
            emit(eng, lambda e, b=builder, s=s, k=inc: b(e).then_inc(s, k))
            return v

        def quiet(eng, builder):
            emit(eng, lambda e, b=builder: b(e))

        def dma(eng, out_ap, in_ap, sem_name):
            s = sems[sem_name]
            cnt[sem_name] += 16
            v = cnt[sem_name]
            emit(eng, lambda e, o=out_ap, i=in_ap, s=s:
                 e.dma_start(out=o, in_=i).then_inc(s, 16))
            return v

        bankfree = {id(p): ("sv", 0) for p in
                    (psA, psB, psC, psD, psY0, psY1, psT0, psT1)}

        def mm_group(bank, mms, skip_check=False):
            sn, sval = bankfree[id(bank)]
            wait("tensor", sn, sval)
            v = None
            for i, (lh, rh, osl) in enumerate(mms):
                st_ = (i == 0)
                sp_ = (i == len(mms) - 1)
                fn = (lambda e, lh=lh, rh=rh, o=osl, st=st_, sp=sp_, sk=skip_check:
                      e.matmul(o, lh, rh, start=st, stop=sp, skip_group_check=sk))
                if sp_:
                    v = tick("tensor", fn)
                else:
                    quiet("tensor", fn)
            return v

        _c1 = tick("gpsimd", lambda g: g.memset(IDN[:, :], 0.0))
        wait("gpsimd", "sq", _c1)
        tick("gpsimd", lambda g: g.affine_select(
            out=IDN[:, :], in_=IDN[:, :], compare_op=OP.not_equal, fill=1.0,
            base=0, pattern=[[-1, 128]], channel_multiplier=1))
        _c1b = tick("gpsimd", lambda g: g.memset(IDN32[:, :], 0.0))
        wait("gpsimd", "sq", _c1b)
        tick("gpsimd", lambda g: g.affine_select(
            out=IDN32[:, :], in_=IDN32[:, :], compare_op=OP.not_equal, fill=1.0,
            base=0, pattern=[[-1, 128]], channel_multiplier=1))
        _c2 = tick("gpsimd", lambda g: g.memset(MSK[:, :], 1.0))
        wait("gpsimd", "sq", _c2)
        tick("gpsimd", lambda g: g.affine_select(
            out=MSK[:, :], in_=MSK[:, :], compare_op=OP.is_gt,
            fill=0.0, base=0, pattern=[[1, 128]], channel_multiplier=-1))
        consts_done = tick("gpsimd", lambda g: g.memset(ONER[0:1, :], 1.0))
        wait("scalar", "sq", consts_done)
        wait("vector", "sq", consts_done)
        wait("tensor", "sq", consts_done)

        _slots = [STG[:, 0, :], STG[:, 1, :],
                  AMPf.ap().rearrange("p c n -> p (c n)")[:, 0:2048]]
        _sema = ["swa", "swb", "sba"]
        wstate = {"i": 0, "cast": {}, "dmav": {}, "pend": [], "slotc_cast": None}

        def stage_issue(src2d, blocker=None):
            i = wstate["i"]
            h = i % 3
            prev = wstate["cast"].get(i - 3)
            if prev is not None:
                wait("gpsimd", prev[0], prev[1])
            if blocker is not None:
                wait("gpsimd", blocker[0], blocker[1])
            n = int(src2d.shape[0]) * int(src2d.shape[1]) // 128
            nin = int(src2d.shape[1])
            sv = _slots[h][:, 0:n].rearrange("p (c n) -> p c n", n=nin)
            wstate["dmav"][i] = (dma("gpsimd", sv,
                                     src2d.rearrange("(c p) n -> p c n", p=128),
                                     _sema[h]), _sema[h])
            wstate["i"] = i + 1
            return i

        def stage_cast(i, dst3d, ceng, blocker=None):
            h = i % 3
            dv, sm = wstate["dmav"][i]
            wait(ceng, sm, dv)
            if blocker is not None:
                wait(ceng, blocker[0], blocker[1])
            nch, nin = int(dst3d.shape[1]), int(dst3d.shape[2])
            sview = _slots[h][:, 0:nch * nin].rearrange(
                "p (c n) -> p c n", n=nin)
            if ceng == "scalar":
                v = tick(ceng, lambda e, d=dst3d, s=sview: e.copy(out=d, in_=s))
            else:
                v = tick(ceng, lambda e, d=dst3d, s=sview:
                         e.tensor_copy(out=d, in_=s))
            wstate["cast"][i] = (ENG_SEM[ceng], v)
            if i % 3 == 2:
                wstate["slotc_cast"] = (ENG_SEM[ceng], v)
            return v

        def run_weight_jobs(jlist, blocker=None, two_slot=False):
            for (src2d, dst3d, ceng) in jlist:
                if two_slot and wstate["i"] % 3 == 2:
                    wstate["i"] += 1
                i = stage_issue(src2d)
                stage_cast(i, dst3d, ceng, blocker=blocker)
                blocker = None

        jobs_kv = []
        for m, dstv in ((wkT, wk_v), (wvT, wv_v)):
            for c in range(2):
                jobs_kv.append((m[c * 512:(c + 1) * 512, :],
                                dstv[:, 4 * c:4 * c + 4, :], "scalar"))
        jobs_q = [(wqT[c * 512:(c + 1) * 512, :], WQ[:, 4 * c:4 * c + 4, :],
                   "scalar") for c in range(2)]
        jobs_rest = []
        for c in range(2):
            jobs_rest.append((woT[c * 256:(c + 1) * 256, :],
                              WO[:, 2 * c:2 * c + 2, :], "scalar"))
        jobs_rest.append((u1, U1B[:, :, :], "scalar"))
        for rc in range(2):
            for k in range(2):
                jobs_rest.append((v1[rc * 128:(rc + 1) * 128,
                                     k * 2048:(k + 1) * 2048],
                                  V1B[:, rc, k * 2048:(k + 1) * 2048]
                                  .rearrange("p (c n) -> p c n", c=1),
                                  "scalar"))
        jobs_u2 = [(u2[c * 1024:(c + 1) * 1024, :],
                    u2_v[:, 8 * c:8 * c + 8, :], "scalar") for c in range(4)]
        jobs_u2 += [(v2[c * 128:(c + 1) * 128, :],
                     V2B[:, c, :].rearrange("p (c n) -> p c n", c=1),
                     "scalar") for c in range(2)]

        dma("sync", B1S[:, :], b1i, "sbm")
        dma("sync", B2R[0:1, :], b2r, "sbm")
        v_wm = dma("sync", WMS[:, :], wmi, "sbm")
        SBA_MISC = v_wm
        wait("scalar", "sbm", SBA_MISC)
        b2b_v = tick("scalar", lambda e: e.copy(out=B2B[0:1, :], in_=B2R[0:1, :]))

        x_src = x_f.rearrange("d (s t) -> d s t", t=SC)
        st1 = {
            "xt_free": ("sv", 0),
            "xb_free_p": 0,
            "mr_v": 0,
            "kt_evac": {},
            "vt_evac": {},
            "qf_evac": {},
            "kf_evac": {},
            "mtb_snap": {},
        }
        kv_last_p = 0

        pre_vx = dma("sync", XT[:, :, :],
                     x_src[:, 0, :].rearrange("(c p) t -> p c t", p=128), "sx")
        run_weight_jobs(jobs_kv[:2])
        kvcast_done = cnt["sa"]
        run_weight_jobs(jobs_kv[2:])
        vvcast_done = cnt["sa"]

        for sc in range(NSC):
            if sc == 0:
                vx = pre_vx
                wait("vector", "sx", vx)
                xcast = tick("vector", lambda e: e.tensor_copy(
                    out=XB[:, :, :].rearrange("p c n -> p (c n)"),
                    in_=XT[:, :, :].rearrange("p c n -> p (c n)")))
                st1["xt_free"] = ("sv", xcast)
            else:
                xcast = st1["pre_cast"]
            if sc < 3:
                sn, v = st1["xt_free"]
                wait("sync", sn, v)
                st1["nvx"] = dma("sync", XT[:, :, :],
                                 x_src[:, sc + 1, :]
                                 .rearrange("(c p) t -> p c t", p=128), "sx")
            if sc == 0:
                run_weight_jobs(jobs_q)
            elif sc == 1:
                run_weight_jobs(jobs_rest[:6])
            elif sc == 2:
                run_weight_jobs(jobs_rest[6:])

            wait("tensor", "sv", xcast)
            if sc == 0:
                wait("tensor", "sa", kvcast_done)
            for tc in range(4):
                bank = psA if tc % 2 == 0 else psB
                gv = mm_group(bank, [
                    (XB[:, c, tc * 128:(tc + 1) * 128], wk_v[:, c, :],
                     bank[:, :]) for c in range(8)])
                wait("vector", "sp", gv)
                if sc > 0:
                    wait("vector", "sp", kv_last_p)
                ev = tick("vector", lambda e, b=bank, t=tc:
                          e.tensor_copy(out=KT[:, t, :], in_=b[:, :]))
                bankfree[id(bank)] = ("sv", ev)
                st1["kt_evac"][tc] = ev
            if sc == 0:
                wait("tensor", "sa", vvcast_done)
            for tc in range(4):
                bank = psA if tc % 2 == 0 else psB
                gv = mm_group(bank, [
                    (XB[:, c, tc * 128:(tc + 1) * 128], wv_v[:, c, :],
                     bank[:, :]) for c in range(8)])
                wait("vector", "sp", gv)
                ev = tick("vector", lambda e, b=bank, s=sc, t=tc:
                          e.tensor_copy(out=VT[:, 4 * s + t, :], in_=b[:, :]))
                bankfree[id(bank)] = ("sv", ev)
                st1["vt_evac"][(sc, tc)] = ev

            if sc == 0:
                wait("tensor", "sa", cnt["sa"])
            for jc in range(4):
                bank = psY0 if jc % 2 == 0 else psY1
                gv = mm_group(bank, [
                    (WQ[:, c, jc * 128:(jc + 1) * 128], XB[:, c, :],
                     bank[:, :]) for c in range(8)])
                wait("vector", "sp", gv)
                ev = tick("vector", lambda e, b=bank, s=sc, j=jc:
                          e.tensor_copy(out=QK[:, 4 * s + j, :], in_=b[:, :]))
                bankfree[id(bank)] = ("sv", ev)
                st1["qf_evac"][(sc, jc)] = ev
            st1["xb_free_p"] = cnt["sp"]

            if sc < 3:
                wait("vector", "sx", st1["nvx"])
                wait("vector", "sp", st1["xb_free_p"])
                ncast = tick("vector", lambda e: e.tensor_copy(
                    out=XB[:, :, :].rearrange("p c n -> p (c n)"),
                    in_=XT[:, :, :].rearrange("p c n -> p (c n)")))
                st1["xt_free"] = ("sv", ncast)
                st1["pre_cast"] = ncast

            for tc in range(4):
                wait("tensor", "sv", st1["kt_evac"][tc])
            for lc in range(4):
                bank = psT0 if lc % 2 == 0 else psT1
                bankb = psT0b if lc % 2 == 0 else psT1b
                sn, v = bankfree[id(bank)]
                wait("tensor", sn, v)
                gv = None
                for tcc in range(4):
                    fn = (lambda e, b=bankb, l=lc, t=tcc:
                          e.transpose(b[:, t * 128:(t + 1) * 128],
                                      KT[:, t, l * 128:(l + 1) * 128],
                                      IDN[:, :]))
                    if tcc == 3:
                        gv = tick("tensor", fn)
                    else:
                        quiet("tensor", fn)
                wait("scalar", "sp", gv)
                ev = tick("scalar", lambda e, b=bankb, s=sc, l=lc:
                          e.copy(out=KF[:, 4 * s + l, :], in_=b[:, 0:512]))
                bankfree[id(bank)] = ("sa", ev)
                st1["kf_evac"][(sc, lc)] = ev
            for tc in range(4):
                wait("tensor", "sv", st1["kt_evac"][tc])
                wait("tensor", "sv", st1["vt_evac"][(sc, tc)])
            for kc in range(4):
                bank = psC if kc % 2 == 0 else psD
                gv = mm_group(bank, [
                    (KT[:, tc, kc * 128:(kc + 1) * 128], VT[:, 4 * sc + tc, :],
                     bank[:, :]) for tc in range(4)])
                wait("vector", "sp", gv)
                wait("vector", "sv", st1["mr_v"])
                if sc == 0:
                    mv = tick("vector", lambda e, b=bank, k=kc:
                              e.tensor_copy(out=MR[:, k, :], in_=b[:, :]))
                elif sc < 3:
                    mv = tick("vector", lambda e, b=bank, k=kc:
                              e.tensor_tensor(out=MR[:, k, :], in0=MR[:, k, :],
                                              in1=b[:, :], op=OP.add))
                else:
                    mv0 = tick("vector", lambda e, b=bank, k=kc:
                               e.tensor_tensor(out=MR[:, k, :], in0=MR[:, k, :],
                                               in1=b[:, :], op=OP.add))
                    wait("vector", "sv", mv0)
                    mv = tick("vector", lambda e, k=kc:
                              e.tensor_copy(out=MTB[:, k, :], in_=MR[:, k, :]))
                    wait("scalar", "sv", mv)
                    st1["cinb_dma"] = dma(
                        "scalar",
                        cin_b[kc * 128:(kc + 1) * 128, :]
                        .rearrange("(c p) i -> p c i", p=128),
                        MTB[:, kc:kc + 1, :], "sba")
                bankfree[id(bank)] = ("sv", mv)
                st1["mr_v"] = mv
            if sc == 3:
                wait("gpsimd", "sba", st1["cinb_dma"])
                tick("gpsimd", lambda e: e.collective_compute(
                    "AllGather", OP.bypass,
                    replica_groups=[list(range(NCORE))],
                    ins=[cin_b.ap()], outs=[cout_b.ap()]), sem_name="scd")

            if sc < 3:
                wait("scalar", "sv", st1["mr_v"])
                snap = tick("scalar", lambda e, s=sc: e.copy(
                    out=MTB[:, 4 * (s + 1):4 * (s + 2), :]
                    .rearrange("p c n -> p (c n)"), in_=MRf))
                st1["mtb_snap"][sc + 1] = snap
            kv_last_p = cnt["sp"]

        sn, v = st1["xt_free"]
        wait("sync", sn, v)
        xtok_v = dma("sync", XTOK[:, :, :],
                     x_t[0:SC, :].rearrange("(c p) n -> p c n", p=128), "sx")

        run_weight_jobs(jobs_u2, blocker=("sp", kv_last_p),
                        two_slot=True)
        w_all_sq = ("sa", cnt["sa"])

        if wstate["slotc_cast"] is not None:
            wait("vector", wstate["slotc_cast"][0], wstate["slotc_cast"][1])
        amask = {}
        for sc in range(NSC):
            for lc in range(4):
                wait("tensor", "sa", st1["kf_evac"][(sc, lc)])
                wait("tensor", "sv", st1["qf_evac"][(sc, lc)])
        for sc in range(NSC):
            for sj in range(4):
                nw = ANW[sj]
                bank = psA if sj % 2 == 0 else psB
                gv = mm_group(bank, [
                    (KF[:, 4 * sc + lc, sj * 128:(sj + 1) * 128],
                     QK[:, 4 * sc + lc, sj * 128:512], bank[:, 0:nw])
                    for lc in range(4)])
                off = AOFF[sj]
                wait("vector", "sp", gv)
                quiet("vector", lambda e, b=bank, s=sc, o=off:
                      e.tensor_tensor(out=AMP[:, s, o:o + 128],
                                      in0=b[:, 0:128], in1=MSK[:, :],
                                      op=OP.mult))
                if nw > 128:
                    mv = tick("vector", lambda e, b=bank, s=sc, o=off, n=nw:
                              e.tensor_copy(out=AMP[:, s, o + 128:o + n],
                                            in_=b[:, 128:n]))
                else:
                    mv = tick("vector", lambda e: e.engine_nop())
                bankfree[id(bank)] = ("sv", mv)
                amask[(sc, sj)] = mv

        for i, cv in list(wstate["cast"].items())[-3:]:
            wait("sync", cv[0], cv[1])
        wait("vector", "sbm", SBA_MISC)
        wait("vector", "sv", st1["mr_v"])
        wait("vector", "sa", st1["mtb_snap"][3])
        vstg = [None, None, None, None]
        mg = 0
        for r in range(NCORE):
            h = r % 4
            cc_sem = "scd"
            cc_src = cout_b
            rr = r % NCORE
            if vstg[h] is not None:
                wait("sync", "sv", vstg[h])
            wait("sync", cc_sem, 1)
            vg = dma("sync", GST[:, 4 * h:4 * h + 4, :],
                     cc_src[rr * DKV:(rr + 1) * DKV, :]
                     .rearrange("(c p) i -> p c i", p=128), f"sg{h}")
            wait("vector", f"sg{h}", vg)
            gfl = GST[:, 4 * h:4 * h + 4, :].rearrange("p c n -> p (c n)")
            if r == 0:
                mg = tick("vector", lambda e, g=gfl: e.tensor_scalar(
                    out=MRf, in0=g, scalar1=WMS[:, 0:1], scalar2=None,
                    op0=OP.mult))
            else:
                wait("vector", "sv", mg)
                mg = tick("vector", lambda e, g=gfl, rr=rr:
                          e.scalar_tensor_tensor(
                              out=MRf, in0=g, scalar=WMS[:, rr:rr + 1],
                              in1=MRf, op0=OP.mult, op1=OP.add))
            vstg[h] = mg
        mtb_fin = {}
        wait("vector", "sv", mg)
        wait("vector", "sba", st1["cinb_dma"])
        f0 = tick("vector", lambda e: e.tensor_copy(
            out=MTB[:, 0:4, :].rearrange("p c n -> p (c n)"), in_=MRf))
        mtb_fin[0] = f0
        for s in range(1, 4):
            wait("vector", "sa", st1["mtb_snap"][s])
            wait("vector", "sv", mtb_fin[s - 1])
            fs = tick("vector", lambda e, s=s: e.tensor_tensor(
                out=MTB[:, 4 * s:4 * (s + 1), :].rearrange("p c n -> p (c n)"),
                in0=MRf,
                in1=MTB[:, 4 * s:4 * (s + 1), :].rearrange("p c n -> p (c n)"),
                op=OP.add))
            mtb_fin[s] = fs
        mtb_fin_last = mtb_fin[3]

        st3 = {
            "yf_free": ("sa", 0),
            "wo_last": {},
            "zadd": {},
            "zadd_last": {},
            "apply": {},
            "apply_last": {},
            "rstd1": {}, "rstd2": {},
            "xf_evac": {},
            "gelu": {},
            "u2_tick": {},
            "q2_evac": {},
            "pb_evac": {},
            "z2add": {},
            "z2_last": {},
            "z2_last": {},
            "apply2": {},
            "outs_dma": [0, 0],
            "apply2_last": {},
            "xtok_v": {0: xtok_v},
            "y_last": {},
            "ffn_last": {},
        }

        def ZB(sc):
            return ZB_A if sc % 2 == 0 else ZB_B

        def stc(sc, which, i):
            base = (sc % 2) * 64 + (0 if which == 1 else 32)
            return ST[:, base + i:base + i + 1]

        def stc4(sc, which, i0):
            base = (sc % 2) * 64 + (0 if which == 1 else 32)
            return ST[:, base + i0:base + i0 + 4]

        def emit_Y(sc):
            wait("tensor", "sv", mtb_fin[sc])
            for sj in range(4):
                wait("tensor", "sv", amask[(sc, sj)])
            sn, v = st3["yf_free"]
            for vfc in range(4):
                bank = psY0 if vfc % 2 == 0 else psY1
                bsn, bv = bankfree[id(bank)]
                wait("tensor", bsn, bv)
                quiet("tensor", lambda e, b=bank, s=sc, i=vfc:
                      e.matmul(b[:, 0:512], VT[:, 4 * s, i * 128:(i + 1) * 128],
                               AMP[:, s, 0:512], start=True, stop=False))
                for tk in range(1, 4):
                    quiet("tensor", lambda e, b=bank, s=sc, i=vfc, t=tk:
                          e.matmul(b[:, 128 * t:512],
                                   VT[:, 4 * s + t, i * 128:(i + 1) * 128],
                                   AMP[:, s, AOFF[t]:AOFF[t] + ANW[t]],
                                   start=False, stop=False))
                gv = None
                for lc in range(4):
                    fn = (lambda e, b=bank, s=sc, i=vfc, l=lc, sp_=(lc == 3):
                          e.matmul(b[:, 0:512],
                                   MTB[:, 4 * s + l, i * 128:(i + 1) * 128],
                                   QK[:, 4 * s + l, :], start=False, stop=sp_))
                    if lc == 3:
                        gv = tick("tensor", fn)
                    else:
                        quiet("tensor", fn)
                wait("scalar", "sp", gv)
                sn, v = st3["yf_free"]
                wait("scalar", sn, v)
                ev = tick("scalar", lambda e, b=bank, i=vfc:
                          e.copy(out=YF[:, i, :], in_=b[:, :]))
                bankfree[id(bank)] = ("sa", ev)
            st3["y_last"][sc] = cnt["sa"]

        def emit_Wo_z(sc):
            yev = st3["y_last"][sc]
            wait("tensor", "sa", yev)
            banks = (psA, psB, psC, psT0, psT1)
            gi = 0
            for tc in range(4):
                zv_h = []
                for h in range(2):
                    bank = banks[gi % 5]
                    gi += 1
                    gv = mm_group(bank, [
                        (YF[:, ic, tc * 128:(tc + 1) * 128],
                         WO[:, ic, h * 512:(h + 1) * 512], bank[:, :])
                        for ic in range(4)])
                    wait("vector", "sp", gv)
                    wait("vector", "sx", st3["xtok_v"][sc])
                    if sc >= 2 and tc == 0 and h == 0:
                        wait("vector", "sv", st3["apply2_last"][sc - 2])
                    zv = tick("vector", lambda e, b=bank, s=sc, t=tc, hh=h:
                              e.scalar_tensor_tensor(
                                  out=ZB(s)[:, t, hh * 512:(hh + 1) * 512],
                                  in0=b[:, :], scalar=1.0,
                                  in1=XTOK[:, t, hh * 512:(hh + 1) * 512],
                                  op0=OP.mult, op1=OP.add,
                                  accum_out=stc(s, 1, 4 * hh + t)))
                    bankfree[id(bank)] = ("sv", zv)
                    zv_h.append(zv)
                st3["zadd"][(sc, tc)] = zv_h[-1]
            st3["zadd_last"][sc] = cnt["sv"]
            st3["wo_last"][sc] = cnt["sp"]
            if sc < NSC - 1:
                wait("sync", "sv", st3["zadd_last"][sc])
                st3["xtok_v"][sc + 1] = dma(
                    "sync", XTOK[:, :, :],
                    x_t[(sc + 1) * SC:(sc + 2) * SC, :]
                    .rearrange("(c p) n -> p c n", p=128), "sx")

        def emit_LN(sc, which):
            zb = ZB(sc)
            zlast = (st3["zadd_last"][sc] if which == 1
                     else st3["z2_last"][sc])
            sqv = {}
            for tc in range(4):
                if which == 1 and sc == 0 and tc == 0:
                    wait("scalar", "sv", mtb_fin_last)
                wait("scalar", "sv",
                     st3["zadd"][(sc, tc)] if which == 1
                     else st3["z2add"][(sc, tc, 1)])
                sqv[tc] = tick("scalar", lambda e, t=tc, z=zb, s=sc, w=which:
                               e.activation(out=SQB[:, 0, :], in_=z[:, t, :],
                                            func=AF.Square,
                                            accum_out=stc(s, w, 8 + t)))
            wait("vector", "sv", zlast)
            zs = tick("vector", lambda e, s=sc, w=which: e.tensor_tensor(
                out=stc4(s, w, 0), in0=stc4(s, w, 0), in1=stc4(s, w, 4),
                op=OP.add))
            wait("vector", "sv", zs)
            mean = tick("vector", lambda e, s=sc, w=which: e.tensor_scalar(
                out=stc4(s, w, 12), in0=stc4(s, w, 0), scalar1=1.0 / 1024.0,
                scalar2=None, op0=OP.mult))
            wait("vector", "sv", mean)
            msq = tick("vector", lambda e, s=sc, w=which: e.tensor_tensor(
                out=stc4(s, w, 16), in0=stc4(s, w, 12), in1=stc4(s, w, 12),
                op=OP.mult))
            for tc in range(4):
                wait("vector", "sa", sqv[tc])
            wait("vector", "sv", msq)
            varv = tick("vector", lambda e, s=sc, w=which: e.scalar_tensor_tensor(
                out=stc4(s, w, 16), in0=stc4(s, w, 8), scalar=1.0 / 1024.0,
                in1=stc4(s, w, 16), op0=OP.mult, op1=OP.subtract))
            wait("vector", "sv", varv)
            vee = tick("vector", lambda e, s=sc, w=which: e.tensor_scalar(
                out=stc4(s, w, 16), in0=stc4(s, w, 16), scalar1=EPS,
                scalar2=None, op0=OP.add))
            wait("vector", "sv", vee)
            inv = tick("vector", lambda e, s=sc, w=which: e.reciprocal(
                out=stc4(s, w, 20), in_=stc4(s, w, 16)))
            wait("scalar", "sv", inv)
            rst = tick("scalar", lambda e, s=sc, w=which: e.activation(
                out=stc4(s, w, 16), in_=stc4(s, w, 20), func=AF.Sqrt))
            if which == 1:
                st3["rstd1"][sc] = rst
            else:
                st3["rstd2"][sc] = rst
            return rst

        def emit_apply1(sc):
            rst = st3["rstd1"][sc]
            zb = ZB(sc)
            for tc in range(4):
                wait("vector", "sa", rst)
                av = tick("vector", lambda e, t=tc, z=zb, s=sc: e.tensor_scalar(
                    out=z[:, t, :], in0=z[:, t, :],
                    scalar1=stc(s, 1, 12 + t), scalar2=stc(s, 1, 16 + t),
                    op0=OP.subtract, op1=OP.mult))
                st3["apply"][(sc, tc)] = av
            st3["apply_last"][sc] = cnt["sv"]

        def emit_xT(sc):
            zb = ZB(sc)
            for dc in range(8):
                bank = psY0 if dc % 2 == 0 else psY1
                bsn, bv = bankfree[id(bank)]
                wait("tensor", bsn, bv)
                for tj in range(4):
                    if dc == 0:
                        wait("tensor", "sv", st3["apply"][(sc, tj)])
                gv = None
                for tj in range(4):
                    fn = (lambda e, b=bank, z=zb, t=tj, d=dc:
                          e.transpose(b[:, t * 128:(t + 1) * 128]
                                      .bitcast(F32) if False else
                                      b[:, t * 128:(t + 1) * 128],
                                      z[:, t, d * 128:(d + 1) * 128],
                                      IDN32[:, :]))
                    if tj == 3:
                        gv = tick("tensor", fn)
                    else:
                        quiet("tensor", fn)
                wait("scalar", "sp", gv)
                if sc == 0 and dc == 0:
                    wait("scalar", "sp", st1["xb_free_p"])
                ev = tick("scalar", lambda e, b=bank, d=dc:
                          e.copy(out=XF[:, d, :], in_=b[:, :]))
                bankfree[id(bank)] = ("sa", ev)
                st3["xf_evac"][(sc, dc)] = ev

        def emit_FFN(sc):
            for dc in range(8):
                wait("tensor", "sa", st3["xf_evac"][(sc, dc)])
            if sc == 0:
                wait("tensor", w_all_sq[0], w_all_sq[1])
                wait("tensor", "sa", b2b_v)
            for rc in range(2):
                bank = psC if rc == 0 else psD
                gv = mm_group(bank, [
                    (U1B[:, c, rc * 128:(rc + 1) * 128], XF[:, c, :],
                     bank[:, :]) for c in range(8)])
                wait("scalar", "sp", gv)
                if sc == 0 and rc == 0:
                    wait("scalar", "sv", mtb_fin_last)
                ev = tick("scalar", lambda e, b=bank, r=rc:
                          e.copy(out=PB[:, r, :], in_=b[:, :]))
                bankfree[id(bank)] = ("sa", ev)
                st3["pb_evac"][(sc, rc)] = ev
            for rc in range(2):
                wait("tensor", "sa", st3["pb_evac"][(sc, rc)])
            v1banks = (psY0, psY1, psA)

            def emit_V1(fc):
                vbank = v1banks[fc % 3]
                bsn, bv = bankfree[id(vbank)]
                wait("tensor", bsn, bv)
                gv = mm_group(vbank, [
                    (V1B[:, rc, fc * 128:(fc + 1) * 128], PB[:, rc, :],
                     vbank[:, :]) for rc in range(2)])
                wait("scalar", "sp", gv)
                if fc >= 8:
                    wait("scalar", "sp", st3["u2_tick"][(sc, fc - 8)])
                hslot = fc % 8
                ge = tick("scalar", lambda e, b=vbank, h=hslot, f=fc:
                          e.activation(out=HB[:, h, :], in_=b[:, :],
                                       func=AF.Gelu_apprx_tanh,
                                       bias=B1S[:, f:f + 1]))
                st3["gelu"][(sc, fc)] = ge
                bankfree[id(vbank)] = ("sa", ge)

            def emit_U2(fc):
                hslot = fc % 8
                wait("tensor", "sa", st3["gelu"][(sc, fc)])
                quiet("tensor", lambda e, f=fc, h=hslot, st_=(fc == 0):
                      e.matmul(psC[:, :], u2_v[:, f, 0:128], HB[:, h, :],
                               start=st_, stop=(f == 31),
                               skip_group_check=True))
                fn = (lambda e, f=fc, h=hslot, st_=(fc == 0):
                      e.matmul(psD[:, :], u2_v[:, f, 128:256], HB[:, h, :],
                               start=st_, stop=(f == 31),
                               skip_group_check=True))
                st3["u2_tick"][(sc, fc)] = tick("tensor", fn)

            emit_V1(0)
            emit_V1(1)
            for fc in range(32):
                if fc + 2 < 32:
                    emit_V1(fc + 2)
                emit_U2(fc)
            u2l = st3["u2_tick"][(sc, 31)]
            wait("scalar", "sp", u2l)
            q2a = tick("scalar", lambda e: e.copy(out=Q2B[:, 0, :], in_=psC[:, :]))
            q2b = tick("scalar", lambda e: e.copy(out=Q2B[:, 1, :], in_=psD[:, :]))
            bankfree[id(psC)] = ("sa", q2b)
            bankfree[id(psD)] = ("sa", q2b)
            st3["q2_evac"][sc] = q2b
            zb = ZB(sc)
            banks = (psA, psB, psT0, psT1)
            for tc in range(4):
                for h in range(2):
                    bank = banks[(2 * tc + h) % 4]
                    bsn, bv = bankfree[id(bank)]
                    wait("tensor", bsn, bv)
                    quiet("tensor", lambda e, b=bank, hh=h:
                          e.matmul(b[:, :], ONER[0:1, :],
                                   B2B[0:1, hh * 512:(hh + 1) * 512],
                                   start=True, stop=False))
                    if tc == 0 and h == 0:
                        wait("tensor", "sa", q2b)
                    gv = None
                    for rc in range(2):
                        fn = (lambda e, b=bank, t=tc, hh=h, r=rc, sp_=(rc == 1):
                              e.matmul(b[:, :], Q2B[:, r, t * 128:(t + 1) * 128],
                                       V2B[:, r, hh * 512:(hh + 1) * 512],
                                       start=False, stop=sp_))
                        if rc == 1:
                            gv = tick("tensor", fn)
                        else:
                            quiet("tensor", fn)
                    wait("vector", "sp", gv)
                    wait("vector", "sv", st3["apply"][(sc, tc)])
                    zv = tick("vector", lambda e, b=bank, z=zb, t=tc, hh=h, s=sc:
                              e.scalar_tensor_tensor(
                                  out=z[:, t, hh * 512:(hh + 1) * 512],
                                  in0=b[:, :], scalar=1.0,
                                  in1=z[:, t, hh * 512:(hh + 1) * 512],
                                  op0=OP.mult, op1=OP.add,
                                  accum_out=stc(s, 2, 4 * hh + t)))
                    bankfree[id(bank)] = ("sv", zv)
                    st3["z2add"][(sc, tc, h)] = zv
            st3["z2_last"][sc] = cnt["sv"]
            st3["ffn_last"][sc] = cnt["sp"]

        def emit_apply2_out(sc):
            rst = st3["rstd2"][sc]
            zb = ZB(sc)
            for tc in range(4):
                wait("vector", "sa", rst)
                slot = tc % 2
                wait("vector", f"so{slot}", st3["outs_dma"][slot])
                av = tick("vector", lambda e, t=tc, z=zb, s=sc, sl=slot:
                          e.tensor_scalar(
                              out=OUTS[:, sl, :], in0=z[:, t, :],
                              scalar1=stc(s, 2, 12 + t),
                              scalar2=stc(s, 2, 16 + t),
                              op0=OP.subtract, op1=OP.mult))
                st3["apply2"][(sc, tc)] = av
                wait("scalar", "sv", av)
                st3["outs_dma"][slot] = dma(
                    "scalar",
                    out_t[sc * SC + tc * 128: sc * SC + (tc + 1) * 128, :],
                    OUTS[:, slot, :], f"so{slot}")
            st3["apply2_last"][sc] = cnt["sv"]

        emit_Y(0)
        emit_Wo_z(0)
        emit_Y(1)
        emit_LN(0, 1); emit_apply1(0)
        emit_Wo_z(1)
        emit_xT(0)
        emit_FFN(0)
        emit_LN(1, 1); emit_apply1(1)
        emit_Y(2)
        emit_LN(0, 2); emit_apply2_out(0)
        emit_Wo_z(2)
        emit_xT(1)
        emit_FFN(1)
        emit_LN(2, 1); emit_apply1(2)
        emit_Y(3)
        emit_LN(1, 2); emit_apply2_out(1)
        emit_Wo_z(3)
        emit_xT(2)
        emit_FFN(2)
        emit_LN(3, 1); emit_apply1(3)
        emit_LN(2, 2); emit_apply2_out(2)
        emit_xT(3)
        emit_FFN(3)
        emit_LN(3, 2)
        emit_apply2_out(3)

        wait("sync", "so0", cnt["so0"])
        wait("sync", "so1", cnt["so1"])

        block = ctx.enter_context(nc.Block())

        @block.sync
        def _(e):
            for fn in streams["sync"]:
                fn(e)

        @block.tensor
        def _(e):
            for fn in streams["tensor"]:
                fn(e)

        @block.vector
        def _(e):
            for fn in streams["vector"]:
                fn(e)

        @block.scalar
        def _(e):
            for fn in streams["scalar"]:
                fn(e)

        @block.gpsimd
        def _(e):
            for fn in streams["gpsimd"]:
                fn(e)

    return nc


_NC_CACHE = None


def kernel(L0_attn_x, L0_attn_Wq, L0_attn_Wk, L0_attn_Wv, L0_attn_Wo,
           L0_ffn_U1, L0_ffn_V1, L0_ffn_b1, L0_ffn_U2, L0_ffn_V2, L0_ffn_b2):
    global _NC_CACHE
    x = np.ascontiguousarray(np.asarray(L0_attn_x, dtype=np.float32))
    wq = np.asarray(L0_attn_Wq, np.float32)
    wk = np.asarray(L0_attn_Wk, np.float32)
    wv = np.asarray(L0_attn_Wv, np.float32)
    wo = np.asarray(L0_attn_Wo, np.float32)
    u1 = np.ascontiguousarray(np.asarray(L0_ffn_U1, np.float32))
    v1 = np.ascontiguousarray(np.asarray(L0_ffn_V1, np.float32))
    u2 = np.ascontiguousarray(np.asarray(L0_ffn_U2, np.float32))
    v2 = np.ascontiguousarray(np.asarray(L0_ffn_V2, np.float32))
    b1 = np.asarray(L0_ffn_b1, np.float32)
    b2 = np.asarray(L0_ffn_b2, np.float32)

    wqT = np.ascontiguousarray(wq.T)
    wkT = np.ascontiguousarray(wk.T)
    wvT = np.ascontiguousarray(wv.T)
    woT = np.ascontiguousarray(wo.T)
    b1m = np.ascontiguousarray(b1.reshape(DFF // 128, 128).T)
    b2m = np.ascontiguousarray(b2.reshape(1, D))

    in_maps = []
    for c in range(NCORE):
        b, seg = divmod(c, SEGS)
        xs_f = np.ascontiguousarray(x[b, seg * TC:(seg + 1) * TC, :].T)
        xs_t = np.ascontiguousarray(x[b, seg * TC:(seg + 1) * TC, :])
        wm = np.zeros((128, NCORE), np.float32)
        for r in range(NCORE):
            if r // SEGS == b and r % SEGS < seg:
                wm[:, r] = 1.0
        in_maps.append({
            "x_f": xs_f, "x_t": xs_t,
            "wqT": wqT, "wkT": wkT, "wvT": wvT, "woT": woT,
            "u1": u1, "v1": v1, "u2": u2, "v2": v2,
            "b1i": b1m, "b2r": b2m, "wmi": wm,
        })

    if _NC_CACHE is None:
        _NC_CACHE = build_nc()
    res = run_bass_kernel_spmd(_NC_CACHE, in_maps, core_ids=list(range(NCORE)))
    out = np.empty((B, T, D), np.float32)
    for c in range(NCORE):
        b, seg = divmod(c, SEGS)
        out[b, seg * TC:(seg + 1) * TC, :] = \
            np.asarray(res.results[c]["out_t"], np.float32)
    return out


if __name__ == "__main__":
    nc = build_nc()
    print("built ok")


# version 29
# speedup vs baseline: 1.0485x; 1.0216x over previous
import numpy as np
from contextlib import ExitStack

import concourse.bass as bass
import concourse.mybir as mybir
from concourse.bass_utils import run_bass_kernel_spmd

F32 = mybir.dt.float32
BF16 = mybir.dt.bfloat16
AF = mybir.ActivationFunctionType
OP = mybir.AluOpType

B, T, D = 2, 8192, 1024
DKV = 512
DFF = 4096
RANK = 256
NCORE = 8
SEGS = 4
TC = T // SEGS
NSC = 4
SC = TC // NSC
EPS = 1e-5
AOFF = (0, 512, 896, 1152)
ANW = (512, 384, 256, 128)

ENGINES = ("tensor", "vector", "scalar", "gpsimd", "sync")


def build_nc():
    nc = bass.Bass("TRN2", target_bir_lowering=False, debug=False)

    x_f = nc.dram_tensor("x_f", [D, TC], F32, kind="ExternalInput").ap()
    x_t = nc.dram_tensor("x_t", [TC, D], F32, kind="ExternalInput").ap()
    wqT = nc.dram_tensor("wqT", [D, DKV], F32, kind="ExternalInput").ap()
    wkT = nc.dram_tensor("wkT", [D, DKV], F32, kind="ExternalInput").ap()
    wvT = nc.dram_tensor("wvT", [D, DKV], F32, kind="ExternalInput").ap()
    woT = nc.dram_tensor("woT", [DKV, D], F32, kind="ExternalInput").ap()
    u1 = nc.dram_tensor("u1", [D, RANK], F32, kind="ExternalInput").ap()
    v1 = nc.dram_tensor("v1", [RANK, DFF], F32, kind="ExternalInput").ap()
    u2 = nc.dram_tensor("u2", [DFF, RANK], F32, kind="ExternalInput").ap()
    v2 = nc.dram_tensor("v2", [RANK, D], F32, kind="ExternalInput").ap()
    b1i = nc.dram_tensor("b1i", [128, DFF // 128], F32, kind="ExternalInput").ap()
    b2r = nc.dram_tensor("b2r", [1, D], F32, kind="ExternalInput").ap()
    wmi = nc.dram_tensor("wmi", [128, NCORE], F32, kind="ExternalInput").ap()
    out_t = nc.dram_tensor("out_t", [TC, D], F32, kind="ExternalOutput").ap()
    cin_a = nc.dram_tensor("cin_a", [DKV, DKV], BF16)
    cin_b = nc.dram_tensor("cin_b", [DKV, DKV], BF16)
    cout_a = nc.dram_tensor("cout_a", [NCORE * DKV, DKV], BF16,
                            addr_space="Shared")
    cout_b = nc.dram_tensor("cout_b", [NCORE * DKV, DKV], BF16,
                            addr_space="Shared")

    ctx = ExitStack()
    with ctx:
        sbt = lambda n, s, d: ctx.enter_context(nc.sbuf_tensor(n, s, d))
        pstt = lambda n, s, d: ctx.enter_context(nc.psum_tensor(n, s, d))
        mksem = lambda n: ctx.enter_context(nc.semaphore(n))

        WKVU2 = sbt("WKVU2", [128, 8192], BF16)
        WO = sbt("WO", [128, 4, 1024], BF16)
        U1B = sbt("U1B", [128, 8, 256], BF16)
        V1B = sbt("V1B", [128, 2, 4096], BF16)
        V2B = sbt("V2B", [128, 2, 1024], BF16)
        VT = sbt("VT", [128, 16, 512], BF16)
        QK = sbt("QK", [128, 16, 512], BF16)
        AMP = sbt("AMP", [128, 4, 1280], BF16)
        MTB = sbt("MTB", [128, 16, 512], BF16)
        B1S = sbt("B1S", [128, DFF // 128], F32)
        B2R = sbt("B2R", [1, 1024], F32)
        B2B = sbt("B2B", [1, 1024], BF16)
        WMS = sbt("WMS", [128, NCORE], F32)
        MSK = sbt("MSK", [128, 128], BF16)
        IDN = sbt("IDN", [128, 128], BF16)
        IDN32 = sbt("IDN32", [128, 128], F32)
        ONER = sbt("ONER", [1, 128], BF16)
        ST = sbt("ST", [128, 128], F32)
        OUTS = sbt("OUTS", [128, 2, 1024], F32)

        ARN = ctx.enter_context(nc.sbuf_tensor("ARN", [128, 38912], BF16))
        arnb = ARN
        arnf = ARN.bitcast(F32)
        fap = arnf.ap()
        bap = arnb.ap()
        XT = fap[:, 0:4096].rearrange("p (c n) -> p c n", n=512)
        STG = fap[:, 4096:8192].rearrange("p (c n) -> p c n", n=2048)
        MR = fap[:, 8192:10240].rearrange("p (c n) -> p c n", n=512)
        MRf = fap[:, 8192:10240]
        WQ = bap[:, 20480:24576].rearrange("p (c n) -> p c n", n=512)
        XB = bap[:, 24576:28672].rearrange("p (c n) -> p c n", n=512)
        KT = bap[:, 28672:30720].rearrange("p (c n) -> p c n", n=512)
        KF = bap[:, 30720:38912].rearrange("p (c n) -> p c n", n=512)
        GST = bap[:, 8192:16384].rearrange("p (c n) -> p c n", n=512)
        XTOK = fap[:, 0:4096].rearrange("p (c n) -> p c n", n=1024)
        ZB_A = fap[:, 4096:8192].rearrange("p (c n) -> p c n", n=1024)
        ZB_B = fap[:, 15360:19456].rearrange("p (c n) -> p c n", n=1024)
        SQB = bap[:, 16384:17408].rearrange("p (c n) -> p c n", n=1024)
        PB = bap[:, 18432:19456].rearrange("p (c n) -> p c n", n=512)
        Q2B = bap[:, 19456:20480].rearrange("p (c n) -> p c n", n=512)
        XF = bap[:, 20480:24576].rearrange("p (c n) -> p c n", n=512)
        HB = bap[:, 24576:28672].rearrange("p (c n) -> p c n", n=512)
        YF = bap[:, 28672:30720].rearrange("p (c n) -> p c n", n=512)

        AMPf = AMP.bitcast(F32)
        wk_v = WKVU2[:, 0:4096].rearrange("p (c n) -> p c n", n=512)
        wv_v = WKVU2[:, 4096:8192].rearrange("p (c n) -> p c n", n=512)
        u2_v = WKVU2[:, :].rearrange("p (c n) -> p c n", n=256)

        psA = pstt("psA", [128, 512], F32)
        psB = pstt("psB", [128, 512], F32)
        psC = pstt("psC", [128, 512], F32)
        psD = pstt("psD", [128, 512], F32)
        psY0 = pstt("psY0", [128, 512], F32)
        psY1 = pstt("psY1", [128, 512], F32)
        psT0 = pstt("psT0", [128, 512], F32)
        psT1 = pstt("psT1", [128, 512], F32)
        psT0b = psT0.bitcast(BF16)
        psT1b = psT1.bitcast(BF16)

        sems = {n: mksem(n) for n in
                ["sx", "swa", "swb", "swc", "sba", "sbm", "sg0", "sg1", "sg2", "sg3", "so0", "so1", "scc", "scd", "so",
                 "sp", "sv", "sa", "sq"]}

        streams = {e: [] for e in ENGINES}
        cnt = {n: 0 for n in sems}
        ENG_SEM = {"tensor": "sp", "vector": "sv", "scalar": "sa",
                   "gpsimd": "sq"}

        def emit(eng, fn):
            streams[eng].append(fn)

        def wait(eng, sem_name, val):
            if val is None or val <= 0:
                return
            s = sems[sem_name]
            emit(eng, lambda e, s=s, v=val: e.wait_ge(s, v))

        def tick(eng, builder, sem_name=None, inc=1):
            sn = sem_name or ENG_SEM[eng]
            s = sems[sn]
            cnt[sn] += inc
            v = cnt[sn]
            emit(eng, lambda e, b=builder, s=s, k=inc: b(e).then_inc(s, k))
            return v

        def quiet(eng, builder):
            emit(eng, lambda e, b=builder: b(e))

        def dma(eng, out_ap, in_ap, sem_name):
            s = sems[sem_name]
            cnt[sem_name] += 16
            v = cnt[sem_name]
            emit(eng, lambda e, o=out_ap, i=in_ap, s=s:
                 e.dma_start(out=o, in_=i).then_inc(s, 16))
            return v

        bankfree = {id(p): ("sv", 0) for p in
                    (psA, psB, psC, psD, psY0, psY1, psT0, psT1)}

        def mm_group(bank, mms, skip_check=False):
            sn, sval = bankfree[id(bank)]
            wait("tensor", sn, sval)
            v = None
            for i, (lh, rh, osl) in enumerate(mms):
                st_ = (i == 0)
                sp_ = (i == len(mms) - 1)
                fn = (lambda e, lh=lh, rh=rh, o=osl, st=st_, sp=sp_, sk=skip_check:
                      e.matmul(o, lh, rh, start=st, stop=sp, skip_group_check=sk))
                if sp_:
                    v = tick("tensor", fn)
                else:
                    quiet("tensor", fn)
            return v

        _c1 = tick("gpsimd", lambda g: g.memset(IDN[:, :], 0.0))
        wait("gpsimd", "sq", _c1)
        tick("gpsimd", lambda g: g.affine_select(
            out=IDN[:, :], in_=IDN[:, :], compare_op=OP.not_equal, fill=1.0,
            base=0, pattern=[[-1, 128]], channel_multiplier=1))
        _c1b = tick("gpsimd", lambda g: g.memset(IDN32[:, :], 0.0))
        wait("gpsimd", "sq", _c1b)
        tick("gpsimd", lambda g: g.affine_select(
            out=IDN32[:, :], in_=IDN32[:, :], compare_op=OP.not_equal, fill=1.0,
            base=0, pattern=[[-1, 128]], channel_multiplier=1))
        _c2 = tick("gpsimd", lambda g: g.memset(MSK[:, :], 1.0))
        wait("gpsimd", "sq", _c2)
        tick("gpsimd", lambda g: g.affine_select(
            out=MSK[:, :], in_=MSK[:, :], compare_op=OP.is_gt,
            fill=0.0, base=0, pattern=[[1, 128]], channel_multiplier=-1))
        consts_done = tick("gpsimd", lambda g: g.memset(ONER[0:1, :], 1.0))
        wait("scalar", "sq", consts_done)
        wait("vector", "sq", consts_done)
        wait("tensor", "sq", consts_done)

        _slots = [STG[:, 0, :], STG[:, 1, :],
                  AMPf.ap().rearrange("p c n -> p (c n)")[:, 0:2048]]
        _sema = ["swa", "swb", "swc"]
        wstate = {"i": 0, "cast": {}, "dmav": {}, "pend": [], "slotc_cast": None}

        def stage_issue(src2d, blocker=None):
            i = wstate["i"]
            h = i % 3
            prev = wstate["cast"].get(i - 3)
            if prev is not None:
                wait("gpsimd", prev[0], prev[1])
            if blocker is not None:
                wait("gpsimd", blocker[0], blocker[1])
            n = int(src2d.shape[0]) * int(src2d.shape[1]) // 128
            nin = int(src2d.shape[1])
            sv = _slots[h][:, 0:n].rearrange("p (c n) -> p c n", n=nin)
            wstate["dmav"][i] = (dma("gpsimd", sv,
                                     src2d.rearrange("(c p) n -> p c n", p=128),
                                     _sema[h]), _sema[h])
            wstate["i"] = i + 1
            return i

        def stage_cast(i, dst3d, ceng, blocker=None):
            h = i % 3
            dv, sm = wstate["dmav"][i]
            wait(ceng, sm, dv)
            if blocker is not None:
                wait(ceng, blocker[0], blocker[1])
            nch, nin = int(dst3d.shape[1]), int(dst3d.shape[2])
            sview = _slots[h][:, 0:nch * nin].rearrange(
                "p (c n) -> p c n", n=nin)
            if ceng == "scalar":
                v = tick(ceng, lambda e, d=dst3d, s=sview: e.copy(out=d, in_=s))
            else:
                v = tick(ceng, lambda e, d=dst3d, s=sview:
                         e.tensor_copy(out=d, in_=s))
            wstate["cast"][i] = (ENG_SEM[ceng], v)
            if i % 3 == 2:
                wstate["slotc_cast"] = (ENG_SEM[ceng], v)
            return v

        def run_weight_jobs(jlist, blocker=None, two_slot=False):
            for (src2d, dst3d, ceng) in jlist:
                if two_slot and wstate["i"] % 3 == 2:
                    wstate["i"] += 1
                i = stage_issue(src2d)
                stage_cast(i, dst3d, ceng, blocker=blocker)
                blocker = None

        jobs_kv = []
        for m, dstv in ((wkT, wk_v), (wvT, wv_v)):
            for c in range(2):
                jobs_kv.append((m[c * 512:(c + 1) * 512, :],
                                dstv[:, 4 * c:4 * c + 4, :], "scalar"))
        jobs_q = [(wqT[c * 512:(c + 1) * 512, :], WQ[:, 4 * c:4 * c + 4, :],
                   "scalar") for c in range(2)]
        jobs_rest = []
        for c in range(2):
            jobs_rest.append((woT[c * 256:(c + 1) * 256, :],
                              WO[:, 2 * c:2 * c + 2, :], "scalar"))
        jobs_rest.append((u1, U1B[:, :, :], "scalar"))
        for rc in range(2):
            for k in range(2):
                jobs_rest.append((v1[rc * 128:(rc + 1) * 128,
                                     k * 2048:(k + 1) * 2048],
                                  V1B[:, rc, k * 2048:(k + 1) * 2048]
                                  .rearrange("p (c n) -> p c n", c=1),
                                  "scalar"))
        jobs_u2 = [(u2[c * 1024:(c + 1) * 1024, :],
                    u2_v[:, 8 * c:8 * c + 8, :], "scalar") for c in range(4)]
        jobs_u2 += [(v2[c * 128:(c + 1) * 128, :],
                     V2B[:, c, :].rearrange("p (c n) -> p c n", c=1),
                     "scalar") for c in range(2)]

        dma("sync", B1S[:, :], b1i, "sbm")
        dma("sync", B2R[0:1, :], b2r, "sbm")
        v_wm = dma("sync", WMS[:, :], wmi, "sbm")
        SBA_MISC = v_wm
        wait("scalar", "sbm", SBA_MISC)
        b2b_v = tick("scalar", lambda e: e.copy(out=B2B[0:1, :], in_=B2R[0:1, :]))

        x_src = x_f.rearrange("d (s t) -> d s t", t=SC)
        st1 = {
            "xt_free": ("sv", 0),
            "xb_free_p": 0,
            "mr_v": 0,
            "kt_evac": {},
            "vt_evac": {},
            "qf_evac": {},
            "kf_evac": {},
            "mtb_snap": {},
        }
        kv_last_p = 0

        pre_vx = dma("sync", XT[:, :, :],
                     x_src[:, 0, :].rearrange("(c p) t -> p c t", p=128), "sx")
        run_weight_jobs(jobs_kv[:2])
        kvcast_done = cnt["sa"]
        run_weight_jobs(jobs_kv[2:])
        vvcast_done = cnt["sa"]

        for sc in range(NSC):
            if sc == 0:
                vx = pre_vx
                wait("vector", "sx", vx)
                xcast = tick("vector", lambda e: e.tensor_copy(
                    out=XB[:, :, :].rearrange("p c n -> p (c n)"),
                    in_=XT[:, :, :].rearrange("p c n -> p (c n)")))
                st1["xt_free"] = ("sv", xcast)
            else:
                xcast = st1["pre_cast"]
            if sc < 3:
                sn, v = st1["xt_free"]
                wait("sync", sn, v)
                st1["nvx"] = dma("sync", XT[:, :, :],
                                 x_src[:, sc + 1, :]
                                 .rearrange("(c p) t -> p c t", p=128), "sx")
            if sc == 0:
                run_weight_jobs(jobs_q)
            elif sc == 1:
                run_weight_jobs(jobs_rest[:6])
            elif sc == 2:
                run_weight_jobs(jobs_rest[6:])

            wait("tensor", "sv", xcast)
            if sc == 0:
                wait("tensor", "sa", kvcast_done)
            for tc in range(4):
                bank = psA if tc % 2 == 0 else psB
                gv = mm_group(bank, [
                    (XB[:, c, tc * 128:(tc + 1) * 128], wk_v[:, c, :],
                     bank[:, :]) for c in range(8)])
                wait("vector", "sp", gv)
                if sc > 0:
                    wait("vector", "sp", kv_last_p)
                ev = tick("vector", lambda e, b=bank, t=tc:
                          e.tensor_copy(out=KT[:, t, :], in_=b[:, :]))
                bankfree[id(bank)] = ("sv", ev)
                st1["kt_evac"][tc] = ev
            if sc == 0:
                wait("tensor", "sa", vvcast_done)
            for tc in range(4):
                bank = psA if tc % 2 == 0 else psB
                gv = mm_group(bank, [
                    (XB[:, c, tc * 128:(tc + 1) * 128], wv_v[:, c, :],
                     bank[:, :]) for c in range(8)])
                wait("vector", "sp", gv)
                ev = tick("vector", lambda e, b=bank, s=sc, t=tc:
                          e.tensor_copy(out=VT[:, 4 * s + t, :], in_=b[:, :]))
                bankfree[id(bank)] = ("sv", ev)
                st1["vt_evac"][(sc, tc)] = ev

            if sc == 0:
                wait("tensor", "sa", cnt["sa"])
            for jc in range(4):
                bank = psY0 if jc % 2 == 0 else psY1
                gv = mm_group(bank, [
                    (WQ[:, c, jc * 128:(jc + 1) * 128], XB[:, c, :],
                     bank[:, :]) for c in range(8)])
                wait("vector", "sp", gv)
                ev = tick("vector", lambda e, b=bank, s=sc, j=jc:
                          e.tensor_copy(out=QK[:, 4 * s + j, :], in_=b[:, :]))
                bankfree[id(bank)] = ("sv", ev)
                st1["qf_evac"][(sc, jc)] = ev
            st1["xb_free_p"] = cnt["sp"]

            if sc < 3:
                wait("vector", "sx", st1["nvx"])
                wait("vector", "sp", st1["xb_free_p"])
                ncast = tick("vector", lambda e: e.tensor_copy(
                    out=XB[:, :, :].rearrange("p c n -> p (c n)"),
                    in_=XT[:, :, :].rearrange("p c n -> p (c n)")))
                st1["xt_free"] = ("sv", ncast)
                st1["pre_cast"] = ncast

            for tc in range(4):
                wait("tensor", "sv", st1["kt_evac"][tc])
            for lc in range(4):
                bank = psT0 if lc % 2 == 0 else psT1
                bankb = psT0b if lc % 2 == 0 else psT1b
                sn, v = bankfree[id(bank)]
                wait("tensor", sn, v)
                gv = None
                for tcc in range(4):
                    fn = (lambda e, b=bankb, l=lc, t=tcc:
                          e.transpose(b[:, t * 128:(t + 1) * 128],
                                      KT[:, t, l * 128:(l + 1) * 128],
                                      IDN[:, :]))
                    if tcc == 3:
                        gv = tick("tensor", fn)
                    else:
                        quiet("tensor", fn)
                wait("scalar", "sp", gv)
                ev = tick("scalar", lambda e, b=bankb, s=sc, l=lc:
                          e.copy(out=KF[:, 4 * s + l, :], in_=b[:, 0:512]))
                bankfree[id(bank)] = ("sa", ev)
                st1["kf_evac"][(sc, lc)] = ev
            for tc in range(4):
                wait("tensor", "sv", st1["kt_evac"][tc])
                wait("tensor", "sv", st1["vt_evac"][(sc, tc)])
            for kc in range(4):
                bank = psC if kc % 2 == 0 else psD
                gv = mm_group(bank, [
                    (KT[:, tc, kc * 128:(kc + 1) * 128], VT[:, 4 * sc + tc, :],
                     bank[:, :]) for tc in range(4)])
                wait("vector", "sp", gv)
                wait("vector", "sv", st1["mr_v"])
                if sc == 0:
                    mv = tick("vector", lambda e, b=bank, k=kc:
                              e.tensor_copy(out=MR[:, k, :], in_=b[:, :]))
                elif sc < 3:
                    mv = tick("vector", lambda e, b=bank, k=kc:
                              e.tensor_tensor(out=MR[:, k, :], in0=MR[:, k, :],
                                              in1=b[:, :], op=OP.add))
                else:
                    mv0 = tick("vector", lambda e, b=bank, k=kc:
                               e.tensor_tensor(out=MR[:, k, :], in0=MR[:, k, :],
                                               in1=b[:, :], op=OP.add))
                    wait("vector", "sv", mv0)
                    mv = tick("vector", lambda e, k=kc:
                              e.tensor_copy(out=MTB[:, k, :], in_=MR[:, k, :]))
                    wait("scalar", "sv", mv)
                    st1["cinb_dma"] = dma(
                        "scalar",
                        cin_b[kc * 128:(kc + 1) * 128, :]
                        .rearrange("(c p) i -> p c i", p=128),
                        MTB[:, kc:kc + 1, :], "sba")
                bankfree[id(bank)] = ("sv", mv)
                st1["mr_v"] = mv
            if sc == 3:
                wait("gpsimd", "sba", st1["cinb_dma"])
                tick("gpsimd", lambda e: e.collective_compute(
                    "AllGather", OP.bypass,
                    replica_groups=[list(range(NCORE))],
                    ins=[cin_b.ap()], outs=[cout_b.ap()]), sem_name="scd")

            if sc < 3:
                wait("scalar", "sv", st1["mr_v"])
                snap = tick("scalar", lambda e, s=sc: e.copy(
                    out=MTB[:, 4 * (s + 1):4 * (s + 2), :]
                    .rearrange("p c n -> p (c n)"), in_=MRf))
                st1["mtb_snap"][sc + 1] = snap
            kv_last_p = cnt["sp"]

        sn, v = st1["xt_free"]
        wait("sync", sn, v)
        xtok_v = dma("sync", XTOK[:, :, :],
                     x_t[0:SC, :].rearrange("(c p) n -> p c n", p=128), "sx")

        run_weight_jobs(jobs_u2, blocker=("sp", kv_last_p),
                        two_slot=True)
        w_all_sq = ("sa", cnt["sa"])

        if wstate["slotc_cast"] is not None:
            wait("vector", wstate["slotc_cast"][0], wstate["slotc_cast"][1])
        amask = {}
        for sc in range(NSC):
            for lc in range(4):
                wait("tensor", "sa", st1["kf_evac"][(sc, lc)])
                wait("tensor", "sv", st1["qf_evac"][(sc, lc)])
        for sc in range(NSC):
            for sj in range(4):
                nw = ANW[sj]
                bank = psA if sj % 2 == 0 else psB
                gv = mm_group(bank, [
                    (KF[:, 4 * sc + lc, sj * 128:(sj + 1) * 128],
                     QK[:, 4 * sc + lc, sj * 128:512], bank[:, 0:nw])
                    for lc in range(4)])
                off = AOFF[sj]
                wait("vector", "sp", gv)
                quiet("vector", lambda e, b=bank, s=sc, o=off:
                      e.tensor_tensor(out=AMP[:, s, o:o + 128],
                                      in0=b[:, 0:128], in1=MSK[:, :],
                                      op=OP.mult))
                if nw > 128:
                    mv = tick("vector", lambda e, b=bank, s=sc, o=off, n=nw:
                              e.tensor_copy(out=AMP[:, s, o + 128:o + n],
                                            in_=b[:, 128:n]))
                else:
                    mv = tick("vector", lambda e: e.engine_nop())
                bankfree[id(bank)] = ("sv", mv)
                amask[(sc, sj)] = mv

        for i, cv in list(wstate["cast"].items())[-3:]:
            wait("sync", cv[0], cv[1])
        wait("vector", "sbm", SBA_MISC)
        wait("vector", "sv", st1["mr_v"])
        wait("vector", "sa", st1["mtb_snap"][3])
        vstg = [None, None, None, None]
        mg = 0
        for r in range(NCORE):
            h = r % 4
            cc_sem = "scd"
            cc_src = cout_b
            rr = r % NCORE
            if vstg[h] is not None:
                wait("sync", "sv", vstg[h])
            wait("sync", cc_sem, 1)
            vg = dma("sync", GST[:, 4 * h:4 * h + 4, :],
                     cc_src[rr * DKV:(rr + 1) * DKV, :]
                     .rearrange("(c p) i -> p c i", p=128), f"sg{h}")
            wait("vector", f"sg{h}", vg)
            gfl = GST[:, 4 * h:4 * h + 4, :].rearrange("p c n -> p (c n)")
            if r == 0:
                mg = tick("vector", lambda e, g=gfl: e.tensor_scalar(
                    out=MRf, in0=g, scalar1=WMS[:, 0:1], scalar2=None,
                    op0=OP.mult))
            else:
                wait("vector", "sv", mg)
                mg = tick("vector", lambda e, g=gfl, rr=rr:
                          e.scalar_tensor_tensor(
                              out=MRf, in0=g, scalar=WMS[:, rr:rr + 1],
                              in1=MRf, op0=OP.mult, op1=OP.add))
            vstg[h] = mg
        mtb_fin = {}
        wait("vector", "sv", mg)
        wait("vector", "sba", st1["cinb_dma"])
        f0 = tick("vector", lambda e: e.tensor_copy(
            out=MTB[:, 0:4, :].rearrange("p c n -> p (c n)"), in_=MRf))
        mtb_fin[0] = f0
        for s in range(1, 4):
            wait("vector", "sa", st1["mtb_snap"][s])
            wait("vector", "sv", mtb_fin[s - 1])
            fs = tick("vector", lambda e, s=s: e.tensor_tensor(
                out=MTB[:, 4 * s:4 * (s + 1), :].rearrange("p c n -> p (c n)"),
                in0=MRf,
                in1=MTB[:, 4 * s:4 * (s + 1), :].rearrange("p c n -> p (c n)"),
                op=OP.add))
            mtb_fin[s] = fs
        mtb_fin_last = mtb_fin[3]

        st3 = {
            "yf_free": ("sa", 0),
            "wo_last": {},
            "zadd": {},
            "zadd_last": {},
            "apply": {},
            "apply_last": {},
            "rstd1": {}, "rstd2": {},
            "xf_evac": {},
            "gelu": {},
            "u2_tick": {},
            "q2_evac": {},
            "pb_evac": {},
            "z2add": {},
            "z2_last": {},
            "z2_last": {},
            "apply2": {},
            "outs_dma": [0, 0],
            "apply2_last": {},
            "xtok_v": {0: xtok_v},
            "y_last": {},
            "ffn_last": {},
        }

        def ZB(sc):
            return ZB_A if sc % 2 == 0 else ZB_B

        def stc(sc, which, i):
            base = (sc % 2) * 64 + (0 if which == 1 else 32)
            return ST[:, base + i:base + i + 1]

        def stc4(sc, which, i0):
            base = (sc % 2) * 64 + (0 if which == 1 else 32)
            return ST[:, base + i0:base + i0 + 4]

        def emit_Y(sc):
            wait("tensor", "sv", mtb_fin[sc])
            for sj in range(4):
                wait("tensor", "sv", amask[(sc, sj)])
            sn, v = st3["yf_free"]
            for vfc in range(4):
                bank = psY0 if vfc % 2 == 0 else psY1
                bsn, bv = bankfree[id(bank)]
                wait("tensor", bsn, bv)
                quiet("tensor", lambda e, b=bank, s=sc, i=vfc:
                      e.matmul(b[:, 0:512], VT[:, 4 * s, i * 128:(i + 1) * 128],
                               AMP[:, s, 0:512], start=True, stop=False))
                for tk in range(1, 4):
                    quiet("tensor", lambda e, b=bank, s=sc, i=vfc, t=tk:
                          e.matmul(b[:, 128 * t:512],
                                   VT[:, 4 * s + t, i * 128:(i + 1) * 128],
                                   AMP[:, s, AOFF[t]:AOFF[t] + ANW[t]],
                                   start=False, stop=False))
                gv = None
                for lc in range(4):
                    fn = (lambda e, b=bank, s=sc, i=vfc, l=lc, sp_=(lc == 3):
                          e.matmul(b[:, 0:512],
                                   MTB[:, 4 * s + l, i * 128:(i + 1) * 128],
                                   QK[:, 4 * s + l, :], start=False, stop=sp_))
                    if lc == 3:
                        gv = tick("tensor", fn)
                    else:
                        quiet("tensor", fn)
                wait("scalar", "sp", gv)
                sn, v = st3["yf_free"]
                wait("scalar", sn, v)
                ev = tick("scalar", lambda e, b=bank, i=vfc:
                          e.copy(out=YF[:, i, :], in_=b[:, :]))
                bankfree[id(bank)] = ("sa", ev)
            st3["y_last"][sc] = cnt["sa"]

        def emit_Wo_z(sc):
            yev = st3["y_last"][sc]
            wait("tensor", "sa", yev)
            banks = (psA, psB, psC, psT0, psT1)
            gi = 0
            for tc in range(4):
                zv_h = []
                for h in range(2):
                    bank = banks[gi % 5]
                    gi += 1
                    gv = mm_group(bank, [
                        (YF[:, ic, tc * 128:(tc + 1) * 128],
                         WO[:, ic, h * 512:(h + 1) * 512], bank[:, :])
                        for ic in range(4)])
                    wait("vector", "sp", gv)
                    wait("vector", "sx", st3["xtok_v"][sc])
                    if sc >= 2 and tc == 0 and h == 0:
                        wait("vector", "sv", st3["apply2_last"][sc - 2])
                    zv = tick("vector", lambda e, b=bank, s=sc, t=tc, hh=h:
                              e.scalar_tensor_tensor(
                                  out=ZB(s)[:, t, hh * 512:(hh + 1) * 512],
                                  in0=b[:, :], scalar=1.0,
                                  in1=XTOK[:, t, hh * 512:(hh + 1) * 512],
                                  op0=OP.mult, op1=OP.add,
                                  accum_out=stc(s, 1, 4 * hh + t)))
                    bankfree[id(bank)] = ("sv", zv)
                    zv_h.append(zv)
                st3["zadd"][(sc, tc)] = zv_h[-1]
            st3["zadd_last"][sc] = cnt["sv"]
            st3["wo_last"][sc] = cnt["sp"]
            if sc < NSC - 1:
                wait("sync", "sv", st3["zadd_last"][sc])
                st3["xtok_v"][sc + 1] = dma(
                    "sync", XTOK[:, :, :],
                    x_t[(sc + 1) * SC:(sc + 2) * SC, :]
                    .rearrange("(c p) n -> p c n", p=128), "sx")

        def emit_LN(sc, which):
            zb = ZB(sc)
            zlast = (st3["zadd_last"][sc] if which == 1
                     else st3["z2_last"][sc])
            sqv = {}
            for tc in range(4):
                if which == 1 and sc == 0 and tc == 0:
                    wait("scalar", "sv", mtb_fin_last)
                wait("scalar", "sv",
                     st3["zadd"][(sc, tc)] if which == 1
                     else st3["z2add"][(sc, tc, 1)])
                sqv[tc] = tick("scalar", lambda e, t=tc, z=zb, s=sc, w=which:
                               e.activation(out=SQB[:, 0, :], in_=z[:, t, :],
                                            func=AF.Square,
                                            accum_out=stc(s, w, 8 + t)))
            wait("vector", "sv", zlast)
            zs = tick("vector", lambda e, s=sc, w=which: e.tensor_tensor(
                out=stc4(s, w, 0), in0=stc4(s, w, 0), in1=stc4(s, w, 4),
                op=OP.add))
            wait("vector", "sv", zs)
            mean = tick("vector", lambda e, s=sc, w=which: e.tensor_scalar(
                out=stc4(s, w, 12), in0=stc4(s, w, 0), scalar1=1.0 / 1024.0,
                scalar2=None, op0=OP.mult))
            wait("vector", "sv", mean)
            msq = tick("vector", lambda e, s=sc, w=which: e.tensor_tensor(
                out=stc4(s, w, 16), in0=stc4(s, w, 12), in1=stc4(s, w, 12),
                op=OP.mult))
            for tc in range(4):
                wait("vector", "sa", sqv[tc])
            wait("vector", "sv", msq)
            varv = tick("vector", lambda e, s=sc, w=which: e.scalar_tensor_tensor(
                out=stc4(s, w, 16), in0=stc4(s, w, 8), scalar=1.0 / 1024.0,
                in1=stc4(s, w, 16), op0=OP.mult, op1=OP.subtract))
            wait("vector", "sv", varv)
            vee = tick("vector", lambda e, s=sc, w=which: e.tensor_scalar(
                out=stc4(s, w, 16), in0=stc4(s, w, 16), scalar1=EPS,
                scalar2=None, op0=OP.add))
            wait("vector", "sv", vee)
            inv = tick("vector", lambda e, s=sc, w=which: e.reciprocal(
                out=stc4(s, w, 20), in_=stc4(s, w, 16)))
            wait("scalar", "sv", inv)
            rst = tick("scalar", lambda e, s=sc, w=which: e.activation(
                out=stc4(s, w, 16), in_=stc4(s, w, 20), func=AF.Sqrt))
            if which == 1:
                st3["rstd1"][sc] = rst
            else:
                st3["rstd2"][sc] = rst
            return rst

        def emit_apply1(sc):
            rst = st3["rstd1"][sc]
            zb = ZB(sc)
            for tc in range(4):
                wait("vector", "sa", rst)
                av = tick("vector", lambda e, t=tc, z=zb, s=sc: e.tensor_scalar(
                    out=z[:, t, :], in0=z[:, t, :],
                    scalar1=stc(s, 1, 12 + t), scalar2=stc(s, 1, 16 + t),
                    op0=OP.subtract, op1=OP.mult))
                st3["apply"][(sc, tc)] = av
            st3["apply_last"][sc] = cnt["sv"]

        def emit_xT(sc):
            zb = ZB(sc)
            for dc in range(8):
                bank = psY0 if dc % 2 == 0 else psY1
                bsn, bv = bankfree[id(bank)]
                wait("tensor", bsn, bv)
                for tj in range(4):
                    if dc == 0:
                        wait("tensor", "sv", st3["apply"][(sc, tj)])
                gv = None
                for tj in range(4):
                    fn = (lambda e, b=bank, z=zb, t=tj, d=dc:
                          e.transpose(b[:, t * 128:(t + 1) * 128]
                                      .bitcast(F32) if False else
                                      b[:, t * 128:(t + 1) * 128],
                                      z[:, t, d * 128:(d + 1) * 128],
                                      IDN32[:, :]))
                    if tj == 3:
                        gv = tick("tensor", fn)
                    else:
                        quiet("tensor", fn)
                wait("scalar", "sp", gv)
                if sc == 0 and dc == 0:
                    wait("scalar", "sp", st1["xb_free_p"])
                ev = tick("scalar", lambda e, b=bank, d=dc:
                          e.copy(out=XF[:, d, :], in_=b[:, :]))
                bankfree[id(bank)] = ("sa", ev)
                st3["xf_evac"][(sc, dc)] = ev

        def emit_FFN(sc):
            for dc in range(8):
                wait("tensor", "sa", st3["xf_evac"][(sc, dc)])
            if sc == 0:
                wait("tensor", w_all_sq[0], w_all_sq[1])
                wait("tensor", "sa", b2b_v)
            for rc in range(2):
                bank = psC if rc == 0 else psD
                gv = mm_group(bank, [
                    (U1B[:, c, rc * 128:(rc + 1) * 128], XF[:, c, :],
                     bank[:, :]) for c in range(8)])
                wait("scalar", "sp", gv)
                if sc == 0 and rc == 0:
                    wait("scalar", "sv", mtb_fin_last)
                ev = tick("scalar", lambda e, b=bank, r=rc:
                          e.copy(out=PB[:, r, :], in_=b[:, :]))
                bankfree[id(bank)] = ("sa", ev)
                st3["pb_evac"][(sc, rc)] = ev
            for rc in range(2):
                wait("tensor", "sa", st3["pb_evac"][(sc, rc)])
            v1banks = (psY0, psY1, psA)

            def emit_V1(fc):
                vbank = v1banks[fc % 3]
                bsn, bv = bankfree[id(vbank)]
                wait("tensor", bsn, bv)
                gv = mm_group(vbank, [
                    (V1B[:, rc, fc * 128:(fc + 1) * 128], PB[:, rc, :],
                     vbank[:, :]) for rc in range(2)])
                wait("scalar", "sp", gv)
                if fc >= 8:
                    wait("scalar", "sp", st3["u2_tick"][(sc, fc - 8)])
                hslot = fc % 8
                ge = tick("scalar", lambda e, b=vbank, h=hslot, f=fc:
                          e.activation(out=HB[:, h, :], in_=b[:, :],
                                       func=AF.Gelu_apprx_tanh,
                                       bias=B1S[:, f:f + 1]))
                st3["gelu"][(sc, fc)] = ge
                bankfree[id(vbank)] = ("sa", ge)

            def emit_U2(fc):
                hslot = fc % 8
                wait("tensor", "sa", st3["gelu"][(sc, fc)])
                quiet("tensor", lambda e, f=fc, h=hslot, st_=(fc == 0):
                      e.matmul(psC[:, :], u2_v[:, f, 0:128], HB[:, h, :],
                               start=st_, stop=(f == 31),
                               skip_group_check=True))
                fn = (lambda e, f=fc, h=hslot, st_=(fc == 0):
                      e.matmul(psD[:, :], u2_v[:, f, 128:256], HB[:, h, :],
                               start=st_, stop=(f == 31),
                               skip_group_check=True))
                st3["u2_tick"][(sc, fc)] = tick("tensor", fn)

            emit_V1(0)
            emit_V1(1)
            for fc in range(32):
                if fc + 2 < 32:
                    emit_V1(fc + 2)
                emit_U2(fc)
            u2l = st3["u2_tick"][(sc, 31)]
            wait("scalar", "sp", u2l)
            q2a = tick("scalar", lambda e: e.copy(out=Q2B[:, 0, :], in_=psC[:, :]))
            q2b = tick("scalar", lambda e: e.copy(out=Q2B[:, 1, :], in_=psD[:, :]))
            bankfree[id(psC)] = ("sa", q2b)
            bankfree[id(psD)] = ("sa", q2b)
            st3["q2_evac"][sc] = q2b
            zb = ZB(sc)
            banks = (psA, psB, psT0, psT1)
            for tc in range(4):
                for h in range(2):
                    bank = banks[(2 * tc + h) % 4]
                    bsn, bv = bankfree[id(bank)]
                    wait("tensor", bsn, bv)
                    quiet("tensor", lambda e, b=bank, hh=h:
                          e.matmul(b[:, :], ONER[0:1, :],
                                   B2B[0:1, hh * 512:(hh + 1) * 512],
                                   start=True, stop=False))
                    if tc == 0 and h == 0:
                        wait("tensor", "sa", q2b)
                    gv = None
                    for rc in range(2):
                        fn = (lambda e, b=bank, t=tc, hh=h, r=rc, sp_=(rc == 1):
                              e.matmul(b[:, :], Q2B[:, r, t * 128:(t + 1) * 128],
                                       V2B[:, r, hh * 512:(hh + 1) * 512],
                                       start=False, stop=sp_))
                        if rc == 1:
                            gv = tick("tensor", fn)
                        else:
                            quiet("tensor", fn)
                    wait("vector", "sp", gv)
                    wait("vector", "sv", st3["apply"][(sc, tc)])
                    zv = tick("vector", lambda e, b=bank, z=zb, t=tc, hh=h, s=sc:
                              e.scalar_tensor_tensor(
                                  out=z[:, t, hh * 512:(hh + 1) * 512],
                                  in0=b[:, :], scalar=1.0,
                                  in1=z[:, t, hh * 512:(hh + 1) * 512],
                                  op0=OP.mult, op1=OP.add,
                                  accum_out=stc(s, 2, 4 * hh + t)))
                    bankfree[id(bank)] = ("sv", zv)
                    st3["z2add"][(sc, tc, h)] = zv
            st3["z2_last"][sc] = cnt["sv"]
            st3["ffn_last"][sc] = cnt["sp"]

        def emit_apply2_out(sc):
            rst = st3["rstd2"][sc]
            zb = ZB(sc)
            for tc in range(4):
                wait("vector", "sa", rst)
                slot = tc % 2
                wait("vector", f"so{slot}", st3["outs_dma"][slot])
                av = tick("vector", lambda e, t=tc, z=zb, s=sc, sl=slot:
                          e.tensor_scalar(
                              out=OUTS[:, sl, :], in0=z[:, t, :],
                              scalar1=stc(s, 2, 12 + t),
                              scalar2=stc(s, 2, 16 + t),
                              op0=OP.subtract, op1=OP.mult))
                st3["apply2"][(sc, tc)] = av
                wait("scalar", "sv", av)
                st3["outs_dma"][slot] = dma(
                    "scalar",
                    out_t[sc * SC + tc * 128: sc * SC + (tc + 1) * 128, :],
                    OUTS[:, slot, :], f"so{slot}")
            st3["apply2_last"][sc] = cnt["sv"]

        emit_Y(0)
        emit_Wo_z(0)
        emit_Y(1)
        emit_LN(0, 1); emit_apply1(0)
        emit_Wo_z(1)
        emit_xT(0)
        emit_FFN(0)
        emit_LN(1, 1); emit_apply1(1)
        emit_Y(2)
        emit_LN(0, 2); emit_apply2_out(0)
        emit_Wo_z(2)
        emit_xT(1)
        emit_FFN(1)
        emit_LN(2, 1); emit_apply1(2)
        emit_Y(3)
        emit_LN(1, 2); emit_apply2_out(1)
        emit_Wo_z(3)
        emit_xT(2)
        emit_FFN(2)
        emit_LN(3, 1); emit_apply1(3)
        emit_LN(2, 2); emit_apply2_out(2)
        emit_xT(3)
        emit_FFN(3)
        emit_LN(3, 2)
        emit_apply2_out(3)

        wait("sync", "so0", cnt["so0"])
        wait("sync", "so1", cnt["so1"])

        block = ctx.enter_context(nc.Block())

        @block.sync
        def _(e):
            for fn in streams["sync"]:
                fn(e)

        @block.tensor
        def _(e):
            for fn in streams["tensor"]:
                fn(e)

        @block.vector
        def _(e):
            for fn in streams["vector"]:
                fn(e)

        @block.scalar
        def _(e):
            for fn in streams["scalar"]:
                fn(e)

        @block.gpsimd
        def _(e):
            for fn in streams["gpsimd"]:
                fn(e)

    return nc


_NC_CACHE = None


def kernel(L0_attn_x, L0_attn_Wq, L0_attn_Wk, L0_attn_Wv, L0_attn_Wo,
           L0_ffn_U1, L0_ffn_V1, L0_ffn_b1, L0_ffn_U2, L0_ffn_V2, L0_ffn_b2):
    global _NC_CACHE
    x = np.ascontiguousarray(np.asarray(L0_attn_x, dtype=np.float32))
    wq = np.asarray(L0_attn_Wq, np.float32)
    wk = np.asarray(L0_attn_Wk, np.float32)
    wv = np.asarray(L0_attn_Wv, np.float32)
    wo = np.asarray(L0_attn_Wo, np.float32)
    u1 = np.ascontiguousarray(np.asarray(L0_ffn_U1, np.float32))
    v1 = np.ascontiguousarray(np.asarray(L0_ffn_V1, np.float32))
    u2 = np.ascontiguousarray(np.asarray(L0_ffn_U2, np.float32))
    v2 = np.ascontiguousarray(np.asarray(L0_ffn_V2, np.float32))
    b1 = np.asarray(L0_ffn_b1, np.float32)
    b2 = np.asarray(L0_ffn_b2, np.float32)

    wqT = np.ascontiguousarray(wq.T)
    wkT = np.ascontiguousarray(wk.T)
    wvT = np.ascontiguousarray(wv.T)
    woT = np.ascontiguousarray(wo.T)
    b1m = np.ascontiguousarray(b1.reshape(DFF // 128, 128).T)
    b2m = np.ascontiguousarray(b2.reshape(1, D))

    in_maps = []
    for c in range(NCORE):
        b, seg = divmod(c, SEGS)
        xs_f = np.ascontiguousarray(x[b, seg * TC:(seg + 1) * TC, :].T)
        xs_t = np.ascontiguousarray(x[b, seg * TC:(seg + 1) * TC, :])
        wm = np.zeros((128, NCORE), np.float32)
        for r in range(NCORE):
            if r // SEGS == b and r % SEGS < seg:
                wm[:, r] = 1.0
        in_maps.append({
            "x_f": xs_f, "x_t": xs_t,
            "wqT": wqT, "wkT": wkT, "wvT": wvT, "woT": woT,
            "u1": u1, "v1": v1, "u2": u2, "v2": v2,
            "b1i": b1m, "b2r": b2m, "wmi": wm,
        })

    if _NC_CACHE is None:
        _NC_CACHE = build_nc()
    res = run_bass_kernel_spmd(_NC_CACHE, in_maps, core_ids=list(range(NCORE)))
    out = np.empty((B, T, D), np.float32)
    for c in range(NCORE):
        b, seg = divmod(c, SEGS)
        out[b, seg * TC:(seg + 1) * TC, :] = \
            np.asarray(res.results[c]["out_t"], np.float32)
    return out


if __name__ == "__main__":
    nc = build_nc()
    print("built ok")


# version 30
# speedup vs baseline: 1.0564x; 1.0075x over previous
import numpy as np
from contextlib import ExitStack

import concourse.bass as bass
import concourse.mybir as mybir
from concourse.bass_utils import run_bass_kernel_spmd

F32 = mybir.dt.float32
BF16 = mybir.dt.bfloat16
AF = mybir.ActivationFunctionType
OP = mybir.AluOpType

B, T, D = 2, 8192, 1024
DKV = 512
DFF = 4096
RANK = 256
NCORE = 8
SEGS = 4
TC = T // SEGS
NSC = 4
SC = TC // NSC
EPS = 1e-5
AOFF = (0, 512, 896, 1152)
ANW = (512, 384, 256, 128)

ENGINES = ("tensor", "vector", "scalar", "gpsimd", "sync")


def build_nc():
    nc = bass.Bass("TRN2", target_bir_lowering=False, debug=False)

    x_f = nc.dram_tensor("x_f", [D, TC], F32, kind="ExternalInput").ap()
    x_t = nc.dram_tensor("x_t", [TC, D], F32, kind="ExternalInput").ap()
    wqT = nc.dram_tensor("wqT", [D, DKV], F32, kind="ExternalInput").ap()
    wkT = nc.dram_tensor("wkT", [D, DKV], F32, kind="ExternalInput").ap()
    wvT = nc.dram_tensor("wvT", [D, DKV], F32, kind="ExternalInput").ap()
    woT = nc.dram_tensor("woT", [DKV, D], F32, kind="ExternalInput").ap()
    u1 = nc.dram_tensor("u1", [D, RANK], F32, kind="ExternalInput").ap()
    v1 = nc.dram_tensor("v1", [RANK, DFF], F32, kind="ExternalInput").ap()
    u2 = nc.dram_tensor("u2", [DFF, RANK], F32, kind="ExternalInput").ap()
    v2 = nc.dram_tensor("v2", [RANK, D], F32, kind="ExternalInput").ap()
    b1i = nc.dram_tensor("b1i", [128, DFF // 128], F32, kind="ExternalInput").ap()
    b2r = nc.dram_tensor("b2r", [1, D], F32, kind="ExternalInput").ap()
    wmi = nc.dram_tensor("wmi", [128, NCORE], F32, kind="ExternalInput").ap()
    out_t = nc.dram_tensor("out_t", [TC, D], F32, kind="ExternalOutput").ap()
    cin_a = nc.dram_tensor("cin_a", [DKV, DKV], BF16)
    cin_b = nc.dram_tensor("cin_b", [DKV, DKV], BF16)
    cout_a = nc.dram_tensor("cout_a", [NCORE * DKV, DKV], BF16,
                            addr_space="Shared")
    cout_b = nc.dram_tensor("cout_b", [NCORE * DKV, DKV], BF16,
                            addr_space="Shared")

    ctx = ExitStack()
    with ctx:
        sbt = lambda n, s, d: ctx.enter_context(nc.sbuf_tensor(n, s, d))
        pstt = lambda n, s, d: ctx.enter_context(nc.psum_tensor(n, s, d))
        mksem = lambda n: ctx.enter_context(nc.semaphore(n))

        WKVU2 = sbt("WKVU2", [128, 8192], BF16)
        WO = sbt("WO", [128, 4, 1024], BF16)
        U1B = sbt("U1B", [128, 8, 256], BF16)
        V1B = sbt("V1B", [128, 2, 4096], BF16)
        V2B = sbt("V2B", [128, 2, 1024], BF16)
        VT = sbt("VT", [128, 16, 512], BF16)
        QK = sbt("QK", [128, 16, 512], BF16)
        AMP = sbt("AMP", [128, 4, 1280], BF16)
        MTB = sbt("MTB", [128, 16, 512], BF16)
        B1S = sbt("B1S", [128, DFF // 128], F32)
        B2R = sbt("B2R", [1, 1024], F32)
        B2B = sbt("B2B", [1, 1024], BF16)
        WMS = sbt("WMS", [128, NCORE], F32)
        MSK = sbt("MSK", [128, 128], BF16)
        IDN = sbt("IDN", [128, 128], BF16)
        IDN32 = sbt("IDN32", [128, 128], F32)
        ONER = sbt("ONER", [1, 128], BF16)
        ST = sbt("ST", [128, 128], F32)
        OUTS = sbt("OUTS", [128, 2, 1024], F32)

        ARN = ctx.enter_context(nc.sbuf_tensor("ARN", [128, 38912], BF16))
        arnb = ARN
        arnf = ARN.bitcast(F32)
        fap = arnf.ap()
        bap = arnb.ap()
        XT = fap[:, 0:4096].rearrange("p (c n) -> p c n", n=512)
        STG = fap[:, 4096:8192].rearrange("p (c n) -> p c n", n=2048)
        MR = fap[:, 8192:10240].rearrange("p (c n) -> p c n", n=512)
        MRf = fap[:, 8192:10240]
        WQ = bap[:, 20480:24576].rearrange("p (c n) -> p c n", n=512)
        XB = bap[:, 24576:28672].rearrange("p (c n) -> p c n", n=512)
        KT = bap[:, 28672:30720].rearrange("p (c n) -> p c n", n=512)
        KF = bap[:, 30720:38912].rearrange("p (c n) -> p c n", n=512)
        GST = bap[:, 8192:16384].rearrange("p (c n) -> p c n", n=512)
        XTOK = fap[:, 0:4096].rearrange("p (c n) -> p c n", n=1024)
        ZB_A = fap[:, 4096:8192].rearrange("p (c n) -> p c n", n=1024)
        ZB_B = fap[:, 15360:19456].rearrange("p (c n) -> p c n", n=1024)
        SQB = bap[:, 16384:17408].rearrange("p (c n) -> p c n", n=1024)
        PB = bap[:, 18432:19456].rearrange("p (c n) -> p c n", n=512)
        Q2B = bap[:, 19456:20480].rearrange("p (c n) -> p c n", n=512)
        XF = bap[:, 20480:24576].rearrange("p (c n) -> p c n", n=512)
        HB = bap[:, 24576:28672].rearrange("p (c n) -> p c n", n=512)
        YF = bap[:, 28672:30720].rearrange("p (c n) -> p c n", n=512)

        AMPf = AMP.bitcast(F32)
        wk_v = WKVU2[:, 0:4096].rearrange("p (c n) -> p c n", n=512)
        wv_v = WKVU2[:, 4096:8192].rearrange("p (c n) -> p c n", n=512)
        u2_v = WKVU2[:, :].rearrange("p (c n) -> p c n", n=256)

        psA = pstt("psA", [128, 512], F32)
        psB = pstt("psB", [128, 512], F32)
        psC = pstt("psC", [128, 512], F32)
        psD = pstt("psD", [128, 512], F32)
        psY0 = pstt("psY0", [128, 512], F32)
        psY1 = pstt("psY1", [128, 512], F32)
        psT0 = pstt("psT0", [128, 512], F32)
        psT1 = pstt("psT1", [128, 512], F32)
        psT0b = psT0.bitcast(BF16)
        psT1b = psT1.bitcast(BF16)

        sems = {n: mksem(n) for n in
                ["sx", "swa", "swb", "swc", "sba", "sbm", "sg0", "sg1", "sg2", "sg3", "so0", "so1", "scc", "scd", "so",
                 "sp", "sv", "sa", "sq"]}

        streams = {e: [] for e in ENGINES}
        cnt = {n: 0 for n in sems}
        ENG_SEM = {"tensor": "sp", "vector": "sv", "scalar": "sa",
                   "gpsimd": "sq"}

        def emit(eng, fn):
            streams[eng].append(fn)

        def wait(eng, sem_name, val):
            if val is None or val <= 0:
                return
            s = sems[sem_name]
            emit(eng, lambda e, s=s, v=val: e.wait_ge(s, v))

        def tick(eng, builder, sem_name=None, inc=1):
            sn = sem_name or ENG_SEM[eng]
            s = sems[sn]
            cnt[sn] += inc
            v = cnt[sn]
            emit(eng, lambda e, b=builder, s=s, k=inc: b(e).then_inc(s, k))
            return v

        def quiet(eng, builder):
            emit(eng, lambda e, b=builder: b(e))

        def dma(eng, out_ap, in_ap, sem_name):
            s = sems[sem_name]
            cnt[sem_name] += 16
            v = cnt[sem_name]
            emit(eng, lambda e, o=out_ap, i=in_ap, s=s:
                 e.dma_start(out=o, in_=i).then_inc(s, 16))
            return v

        bankfree = {id(p): ("sv", 0) for p in
                    (psA, psB, psC, psD, psY0, psY1, psT0, psT1)}

        def mm_group(bank, mms, skip_check=False):
            sn, sval = bankfree[id(bank)]
            wait("tensor", sn, sval)
            v = None
            for i, (lh, rh, osl) in enumerate(mms):
                st_ = (i == 0)
                sp_ = (i == len(mms) - 1)
                fn = (lambda e, lh=lh, rh=rh, o=osl, st=st_, sp=sp_, sk=skip_check:
                      e.matmul(o, lh, rh, start=st, stop=sp, skip_group_check=sk))
                if sp_:
                    v = tick("tensor", fn)
                else:
                    quiet("tensor", fn)
            return v

        _c1 = tick("gpsimd", lambda g: g.memset(IDN[:, :], 0.0))
        wait("gpsimd", "sq", _c1)
        tick("gpsimd", lambda g: g.affine_select(
            out=IDN[:, :], in_=IDN[:, :], compare_op=OP.not_equal, fill=1.0,
            base=0, pattern=[[-1, 128]], channel_multiplier=1))
        _c1b = tick("gpsimd", lambda g: g.memset(IDN32[:, :], 0.0))
        wait("gpsimd", "sq", _c1b)
        tick("gpsimd", lambda g: g.affine_select(
            out=IDN32[:, :], in_=IDN32[:, :], compare_op=OP.not_equal, fill=1.0,
            base=0, pattern=[[-1, 128]], channel_multiplier=1))
        _c2 = tick("gpsimd", lambda g: g.memset(MSK[:, :], 1.0))
        wait("gpsimd", "sq", _c2)
        tick("gpsimd", lambda g: g.affine_select(
            out=MSK[:, :], in_=MSK[:, :], compare_op=OP.is_gt,
            fill=0.0, base=0, pattern=[[1, 128]], channel_multiplier=-1))
        consts_done = tick("gpsimd", lambda g: g.memset(ONER[0:1, :], 1.0))
        wait("scalar", "sq", consts_done)
        wait("vector", "sq", consts_done)
        wait("tensor", "sq", consts_done)

        _slots = [STG[:, 0, :], STG[:, 1, :],
                  AMPf.ap().rearrange("p c n -> p (c n)")[:, 0:2048]]
        _sema = ["swa", "swb", "swc"]
        wstate = {"i": 0, "cast": {}, "dmav": {}, "pend": [], "slotc_cast": None}

        def stage_issue(src2d, blocker=None):
            i = wstate["i"]
            h = i % 3
            prev = wstate["cast"].get(i - 3)
            if prev is not None:
                wait("gpsimd", prev[0], prev[1])
            if blocker is not None:
                wait("gpsimd", blocker[0], blocker[1])
            n = int(src2d.shape[0]) * int(src2d.shape[1]) // 128
            nin = int(src2d.shape[1])
            sv = _slots[h][:, 0:n].rearrange("p (c n) -> p c n", n=nin)
            wstate["dmav"][i] = (dma("gpsimd", sv,
                                     src2d.rearrange("(c p) n -> p c n", p=128),
                                     _sema[h]), _sema[h])
            wstate["i"] = i + 1
            return i

        def stage_cast(i, dst3d, ceng, blocker=None):
            h = i % 3
            dv, sm = wstate["dmav"][i]
            wait(ceng, sm, dv)
            if blocker is not None:
                wait(ceng, blocker[0], blocker[1])
            nch, nin = int(dst3d.shape[1]), int(dst3d.shape[2])
            sview = _slots[h][:, 0:nch * nin].rearrange(
                "p (c n) -> p c n", n=nin)
            if ceng == "scalar":
                v = tick(ceng, lambda e, d=dst3d, s=sview: e.copy(out=d, in_=s))
            else:
                v = tick(ceng, lambda e, d=dst3d, s=sview:
                         e.tensor_copy(out=d, in_=s))
            wstate["cast"][i] = (ENG_SEM[ceng], v)
            if i % 3 == 2:
                wstate["slotc_cast"] = (ENG_SEM[ceng], v)
            return v

        def run_weight_jobs(jlist, blocker=None, two_slot=False):
            for (src2d, dst3d, ceng) in jlist:
                if two_slot and wstate["i"] % 3 == 2:
                    wstate["i"] += 1
                i = stage_issue(src2d)
                stage_cast(i, dst3d, ceng, blocker=blocker)
                blocker = None

        jobs_kv = []
        for m, dstv in ((wkT, wk_v), (wvT, wv_v)):
            for c in range(2):
                jobs_kv.append((m[c * 512:(c + 1) * 512, :],
                                dstv[:, 4 * c:4 * c + 4, :], "scalar"))
        jobs_q = [(wqT[c * 512:(c + 1) * 512, :], WQ[:, 4 * c:4 * c + 4, :],
                   "scalar") for c in range(2)]
        jobs_rest = []
        for c in range(2):
            jobs_rest.append((woT[c * 256:(c + 1) * 256, :],
                              WO[:, 2 * c:2 * c + 2, :], "scalar"))
        jobs_rest.append((u1, U1B[:, :, :], "scalar"))
        for rc in range(2):
            for k in range(2):
                jobs_rest.append((v1[rc * 128:(rc + 1) * 128,
                                     k * 2048:(k + 1) * 2048],
                                  V1B[:, rc, k * 2048:(k + 1) * 2048]
                                  .rearrange("p (c n) -> p c n", c=1),
                                  "scalar"))
        jobs_u2 = [(u2[c * 1024:(c + 1) * 1024, :],
                    u2_v[:, 8 * c:8 * c + 8, :], "scalar") for c in range(4)]
        jobs_u2 += [(v2[c * 128:(c + 1) * 128, :],
                     V2B[:, c, :].rearrange("p (c n) -> p c n", c=1),
                     "scalar") for c in range(2)]

        dma("sync", B1S[:, :], b1i, "sbm")
        dma("sync", B2R[0:1, :], b2r, "sbm")
        v_wm = dma("sync", WMS[:, :], wmi, "sbm")
        SBA_MISC = v_wm
        wait("scalar", "sbm", SBA_MISC)
        b2b_v = tick("scalar", lambda e: e.copy(out=B2B[0:1, :], in_=B2R[0:1, :]))

        x_src = x_f.rearrange("d (s t) -> d s t", t=SC)
        st1 = {
            "xt_free": ("sv", 0),
            "xb_free_p": 0,
            "mr_v": 0,
            "kt_evac": {},
            "vt_evac": {},
            "qf_evac": {},
            "kf_evac": {},
            "mtb_snap": {},
        }
        kv_last_p = 0

        pre_vx = dma("sync", XT[:, :, :],
                     x_src[:, 0, :].rearrange("(c p) t -> p c t", p=128), "sx")
        run_weight_jobs(jobs_kv[:2])
        kvcast_done = cnt["sa"]
        run_weight_jobs(jobs_kv[2:])
        vvcast_done = cnt["sa"]

        for sc in range(NSC):
            if sc == 0:
                vx = pre_vx
                wait("vector", "sx", vx)
                xcast = tick("vector", lambda e: e.tensor_copy(
                    out=XB[:, :, :].rearrange("p c n -> p (c n)"),
                    in_=XT[:, :, :].rearrange("p c n -> p (c n)")))
                st1["xt_free"] = ("sv", xcast)
            else:
                xcast = st1["pre_cast"]
            if sc < 3:
                sn, v = st1["xt_free"]
                wait("sync", sn, v)
                st1["nvx"] = dma("sync", XT[:, :, :],
                                 x_src[:, sc + 1, :]
                                 .rearrange("(c p) t -> p c t", p=128), "sx")
            if sc == 0:
                run_weight_jobs(jobs_q)
            elif sc == 1:
                run_weight_jobs(jobs_rest[:6])
            elif sc == 2:
                run_weight_jobs(jobs_rest[6:])

            wait("tensor", "sv", xcast)
            if sc == 0:
                wait("tensor", "sa", kvcast_done)
            for tc in range(4):
                bank = psA if tc % 2 == 0 else psB
                gv = mm_group(bank, [
                    (XB[:, c, tc * 128:(tc + 1) * 128], wk_v[:, c, :],
                     bank[:, :]) for c in range(8)])
                wait("vector", "sp", gv)
                if sc > 0:
                    wait("vector", "sp", kv_last_p)
                ev = tick("vector", lambda e, b=bank, t=tc:
                          e.tensor_copy(out=KT[:, t, :], in_=b[:, :]))
                bankfree[id(bank)] = ("sv", ev)
                st1["kt_evac"][tc] = ev
            if sc == 0:
                wait("tensor", "sa", vvcast_done)
            for tc in range(4):
                bank = psA if tc % 2 == 0 else psB
                gv = mm_group(bank, [
                    (XB[:, c, tc * 128:(tc + 1) * 128], wv_v[:, c, :],
                     bank[:, :]) for c in range(8)])
                wait("vector", "sp", gv)
                ev = tick("vector", lambda e, b=bank, s=sc, t=tc:
                          e.tensor_copy(out=VT[:, 4 * s + t, :], in_=b[:, :]))
                bankfree[id(bank)] = ("sv", ev)
                st1["vt_evac"][(sc, tc)] = ev

            if sc == 0:
                wait("tensor", "sa", cnt["sa"])
            for jc in range(4):
                bank = psY0 if jc % 2 == 0 else psY1
                gv = mm_group(bank, [
                    (WQ[:, c, jc * 128:(jc + 1) * 128], XB[:, c, :],
                     bank[:, :]) for c in range(8)])
                wait("vector", "sp", gv)
                ev = tick("vector", lambda e, b=bank, s=sc, j=jc:
                          e.tensor_copy(out=QK[:, 4 * s + j, :], in_=b[:, :]))
                bankfree[id(bank)] = ("sv", ev)
                st1["qf_evac"][(sc, jc)] = ev
            st1["xb_free_p"] = cnt["sp"]

            if sc < 3:
                wait("vector", "sx", st1["nvx"])
                wait("vector", "sp", st1["xb_free_p"])
                ncast = tick("vector", lambda e: e.tensor_copy(
                    out=XB[:, :, :].rearrange("p c n -> p (c n)"),
                    in_=XT[:, :, :].rearrange("p c n -> p (c n)")))
                st1["xt_free"] = ("sv", ncast)
                st1["pre_cast"] = ncast

            for tc in range(4):
                wait("tensor", "sv", st1["kt_evac"][tc])
                wait("tensor", "sv", st1["vt_evac"][(sc, tc)])
            for kc in range(4):
                bank = psC if kc % 2 == 0 else psD
                gv = mm_group(bank, [
                    (KT[:, tc, kc * 128:(kc + 1) * 128], VT[:, 4 * sc + tc, :],
                     bank[:, :]) for tc in range(4)])
                wait("vector", "sp", gv)
                wait("vector", "sv", st1["mr_v"])
                if sc == 0:
                    mv = tick("vector", lambda e, b=bank, k=kc:
                              e.tensor_copy(out=MR[:, k, :], in_=b[:, :]))
                elif sc < 3:
                    mv = tick("vector", lambda e, b=bank, k=kc:
                              e.tensor_tensor(out=MR[:, k, :], in0=MR[:, k, :],
                                              in1=b[:, :], op=OP.add))
                else:
                    mv0 = tick("vector", lambda e, b=bank, k=kc:
                               e.tensor_tensor(out=MR[:, k, :], in0=MR[:, k, :],
                                               in1=b[:, :], op=OP.add))
                    wait("vector", "sv", mv0)
                    mv = tick("vector", lambda e, k=kc:
                              e.tensor_copy(out=MTB[:, k, :], in_=MR[:, k, :]))
                    wait("scalar", "sv", mv)
                    st1["cinb_dma"] = dma(
                        "scalar",
                        cin_b[kc * 128:(kc + 1) * 128, :]
                        .rearrange("(c p) i -> p c i", p=128),
                        MTB[:, kc:kc + 1, :], "sba")
                bankfree[id(bank)] = ("sv", mv)
                st1["mr_v"] = mv
            if sc == 3:
                wait("gpsimd", "sba", st1["cinb_dma"])
                tick("gpsimd", lambda e: e.collective_compute(
                    "AllGather", OP.bypass,
                    replica_groups=[list(range(NCORE))],
                    ins=[cin_b.ap()], outs=[cout_b.ap()]), sem_name="scd")

            for tc in range(4):
                wait("tensor", "sv", st1["kt_evac"][tc])
            for lc in range(4):
                bank = psT0 if lc % 2 == 0 else psT1
                bankb = psT0b if lc % 2 == 0 else psT1b
                sn, v = bankfree[id(bank)]
                wait("tensor", sn, v)
                gv = None
                for tcc in range(4):
                    fn = (lambda e, b=bankb, l=lc, t=tcc:
                          e.transpose(b[:, t * 128:(t + 1) * 128],
                                      KT[:, t, l * 128:(l + 1) * 128],
                                      IDN[:, :]))
                    if tcc == 3:
                        gv = tick("tensor", fn)
                    else:
                        quiet("tensor", fn)
                wait("scalar", "sp", gv)
                ev = tick("scalar", lambda e, b=bankb, s=sc, l=lc:
                          e.copy(out=KF[:, 4 * s + l, :], in_=b[:, 0:512]))
                bankfree[id(bank)] = ("sa", ev)
                st1["kf_evac"][(sc, lc)] = ev
            if sc < 3:
                wait("scalar", "sv", st1["mr_v"])
                snap = tick("scalar", lambda e, s=sc: e.copy(
                    out=MTB[:, 4 * (s + 1):4 * (s + 2), :]
                    .rearrange("p c n -> p (c n)"), in_=MRf))
                st1["mtb_snap"][sc + 1] = snap
            kv_last_p = cnt["sp"]

        sn, v = st1["xt_free"]
        wait("sync", sn, v)
        xtok_v = dma("sync", XTOK[:, :, :],
                     x_t[0:SC, :].rearrange("(c p) n -> p c n", p=128), "sx")

        run_weight_jobs(jobs_u2, blocker=("sp", kv_last_p),
                        two_slot=True)
        w_all_sq = ("sa", cnt["sa"])

        if wstate["slotc_cast"] is not None:
            wait("vector", wstate["slotc_cast"][0], wstate["slotc_cast"][1])
        amask = {}
        for sc in range(NSC):
            for lc in range(4):
                wait("tensor", "sa", st1["kf_evac"][(sc, lc)])
                wait("tensor", "sv", st1["qf_evac"][(sc, lc)])
        for sc in range(NSC):
            for sj in range(4):
                nw = ANW[sj]
                bank = psA if sj % 2 == 0 else psB
                gv = mm_group(bank, [
                    (KF[:, 4 * sc + lc, sj * 128:(sj + 1) * 128],
                     QK[:, 4 * sc + lc, sj * 128:512], bank[:, 0:nw])
                    for lc in range(4)])
                off = AOFF[sj]
                wait("vector", "sp", gv)
                quiet("vector", lambda e, b=bank, s=sc, o=off:
                      e.tensor_tensor(out=AMP[:, s, o:o + 128],
                                      in0=b[:, 0:128], in1=MSK[:, :],
                                      op=OP.mult))
                if nw > 128:
                    mv = tick("vector", lambda e, b=bank, s=sc, o=off, n=nw:
                              e.tensor_copy(out=AMP[:, s, o + 128:o + n],
                                            in_=b[:, 128:n]))
                else:
                    mv = tick("vector", lambda e: e.engine_nop())
                bankfree[id(bank)] = ("sv", mv)
                amask[(sc, sj)] = mv

        for i, cv in list(wstate["cast"].items())[-3:]:
            wait("sync", cv[0], cv[1])
        wait("vector", "sbm", SBA_MISC)
        wait("vector", "sv", st1["mr_v"])
        wait("vector", "sa", st1["mtb_snap"][3])
        vstg = [None, None, None, None]
        mg = 0
        for r in range(NCORE):
            h = r % 4
            cc_sem = "scd"
            cc_src = cout_b
            rr = r % NCORE
            if vstg[h] is not None:
                wait("sync", "sv", vstg[h])
            wait("sync", cc_sem, 1)
            vg = dma("sync", GST[:, 4 * h:4 * h + 4, :],
                     cc_src[rr * DKV:(rr + 1) * DKV, :]
                     .rearrange("(c p) i -> p c i", p=128), f"sg{h}")
            wait("vector", f"sg{h}", vg)
            gfl = GST[:, 4 * h:4 * h + 4, :].rearrange("p c n -> p (c n)")
            if r == 0:
                mg = tick("vector", lambda e, g=gfl: e.tensor_scalar(
                    out=MRf, in0=g, scalar1=WMS[:, 0:1], scalar2=None,
                    op0=OP.mult))
            else:
                wait("vector", "sv", mg)
                mg = tick("vector", lambda e, g=gfl, rr=rr:
                          e.scalar_tensor_tensor(
                              out=MRf, in0=g, scalar=WMS[:, rr:rr + 1],
                              in1=MRf, op0=OP.mult, op1=OP.add))
            vstg[h] = mg
        mtb_fin = {}
        wait("vector", "sv", mg)
        wait("vector", "sba", st1["cinb_dma"])
        f0 = tick("vector", lambda e: e.tensor_copy(
            out=MTB[:, 0:4, :].rearrange("p c n -> p (c n)"), in_=MRf))
        mtb_fin[0] = f0
        for s in range(1, 4):
            wait("vector", "sa", st1["mtb_snap"][s])
            wait("vector", "sv", mtb_fin[s - 1])
            fs = tick("vector", lambda e, s=s: e.tensor_tensor(
                out=MTB[:, 4 * s:4 * (s + 1), :].rearrange("p c n -> p (c n)"),
                in0=MRf,
                in1=MTB[:, 4 * s:4 * (s + 1), :].rearrange("p c n -> p (c n)"),
                op=OP.add))
            mtb_fin[s] = fs
        mtb_fin_last = mtb_fin[3]

        st3 = {
            "yf_free": ("sa", 0),
            "wo_last": {},
            "zadd": {},
            "zadd_last": {},
            "apply": {},
            "apply_last": {},
            "rstd1": {}, "rstd2": {},
            "xf_evac": {},
            "gelu": {},
            "u2_tick": {},
            "q2_evac": {},
            "pb_evac": {},
            "z2add": {},
            "z2_last": {},
            "z2_last": {},
            "apply2": {},
            "outs_dma": [0, 0],
            "apply2_last": {},
            "xtok_v": {0: xtok_v},
            "y_last": {},
            "ffn_last": {},
        }

        def ZB(sc):
            return ZB_A if sc % 2 == 0 else ZB_B

        def stc(sc, which, i):
            base = (sc % 2) * 64 + (0 if which == 1 else 32)
            return ST[:, base + i:base + i + 1]

        def stc4(sc, which, i0):
            base = (sc % 2) * 64 + (0 if which == 1 else 32)
            return ST[:, base + i0:base + i0 + 4]

        def emit_Y(sc):
            wait("tensor", "sv", mtb_fin[sc])
            for sj in range(4):
                wait("tensor", "sv", amask[(sc, sj)])
            sn, v = st3["yf_free"]
            for vfc in range(4):
                bank = psY0 if vfc % 2 == 0 else psY1
                bsn, bv = bankfree[id(bank)]
                wait("tensor", bsn, bv)
                quiet("tensor", lambda e, b=bank, s=sc, i=vfc:
                      e.matmul(b[:, 0:512], VT[:, 4 * s, i * 128:(i + 1) * 128],
                               AMP[:, s, 0:512], start=True, stop=False))
                for tk in range(1, 4):
                    quiet("tensor", lambda e, b=bank, s=sc, i=vfc, t=tk:
                          e.matmul(b[:, 128 * t:512],
                                   VT[:, 4 * s + t, i * 128:(i + 1) * 128],
                                   AMP[:, s, AOFF[t]:AOFF[t] + ANW[t]],
                                   start=False, stop=False))
                gv = None
                for lc in range(4):
                    fn = (lambda e, b=bank, s=sc, i=vfc, l=lc, sp_=(lc == 3):
                          e.matmul(b[:, 0:512],
                                   MTB[:, 4 * s + l, i * 128:(i + 1) * 128],
                                   QK[:, 4 * s + l, :], start=False, stop=sp_))
                    if lc == 3:
                        gv = tick("tensor", fn)
                    else:
                        quiet("tensor", fn)
                wait("scalar", "sp", gv)
                sn, v = st3["yf_free"]
                wait("scalar", sn, v)
                ev = tick("scalar", lambda e, b=bank, i=vfc:
                          e.copy(out=YF[:, i, :], in_=b[:, :]))
                bankfree[id(bank)] = ("sa", ev)
            st3["y_last"][sc] = cnt["sa"]

        def emit_Wo_z(sc):
            yev = st3["y_last"][sc]
            wait("tensor", "sa", yev)
            banks = (psA, psB, psC, psT0, psT1)
            gi = 0
            for tc in range(4):
                zv_h = []
                for h in range(2):
                    bank = banks[gi % 5]
                    gi += 1
                    gv = mm_group(bank, [
                        (YF[:, ic, tc * 128:(tc + 1) * 128],
                         WO[:, ic, h * 512:(h + 1) * 512], bank[:, :])
                        for ic in range(4)])
                    wait("vector", "sp", gv)
                    wait("vector", "sx", st3["xtok_v"][sc])
                    if sc >= 2 and tc == 0 and h == 0:
                        wait("vector", "sv", st3["apply2_last"][sc - 2])
                    zv = tick("vector", lambda e, b=bank, s=sc, t=tc, hh=h:
                              e.scalar_tensor_tensor(
                                  out=ZB(s)[:, t, hh * 512:(hh + 1) * 512],
                                  in0=b[:, :], scalar=1.0,
                                  in1=XTOK[:, t, hh * 512:(hh + 1) * 512],
                                  op0=OP.mult, op1=OP.add,
                                  accum_out=stc(s, 1, 4 * hh + t)))
                    bankfree[id(bank)] = ("sv", zv)
                    zv_h.append(zv)
                st3["zadd"][(sc, tc)] = zv_h[-1]
            st3["zadd_last"][sc] = cnt["sv"]
            st3["wo_last"][sc] = cnt["sp"]
            if sc < NSC - 1:
                wait("sync", "sv", st3["zadd_last"][sc])
                st3["xtok_v"][sc + 1] = dma(
                    "sync", XTOK[:, :, :],
                    x_t[(sc + 1) * SC:(sc + 2) * SC, :]
                    .rearrange("(c p) n -> p c n", p=128), "sx")

        def emit_LN(sc, which):
            zb = ZB(sc)
            zlast = (st3["zadd_last"][sc] if which == 1
                     else st3["z2_last"][sc])
            sqv = {}
            for tc in range(4):
                if which == 1 and sc == 0 and tc == 0:
                    wait("scalar", "sv", mtb_fin_last)
                wait("scalar", "sv",
                     st3["zadd"][(sc, tc)] if which == 1
                     else st3["z2add"][(sc, tc, 1)])
                sqv[tc] = tick("scalar", lambda e, t=tc, z=zb, s=sc, w=which:
                               e.activation(out=SQB[:, 0, :], in_=z[:, t, :],
                                            func=AF.Square,
                                            accum_out=stc(s, w, 8 + t)))
            wait("vector", "sv", zlast)
            zs = tick("vector", lambda e, s=sc, w=which: e.tensor_tensor(
                out=stc4(s, w, 0), in0=stc4(s, w, 0), in1=stc4(s, w, 4),
                op=OP.add))
            wait("vector", "sv", zs)
            mean = tick("vector", lambda e, s=sc, w=which: e.tensor_scalar(
                out=stc4(s, w, 12), in0=stc4(s, w, 0), scalar1=1.0 / 1024.0,
                scalar2=None, op0=OP.mult))
            wait("vector", "sv", mean)
            msq = tick("vector", lambda e, s=sc, w=which: e.tensor_tensor(
                out=stc4(s, w, 16), in0=stc4(s, w, 12), in1=stc4(s, w, 12),
                op=OP.mult))
            for tc in range(4):
                wait("vector", "sa", sqv[tc])
            wait("vector", "sv", msq)
            varv = tick("vector", lambda e, s=sc, w=which: e.scalar_tensor_tensor(
                out=stc4(s, w, 16), in0=stc4(s, w, 8), scalar=1.0 / 1024.0,
                in1=stc4(s, w, 16), op0=OP.mult, op1=OP.subtract))
            wait("vector", "sv", varv)
            vee = tick("vector", lambda e, s=sc, w=which: e.tensor_scalar(
                out=stc4(s, w, 16), in0=stc4(s, w, 16), scalar1=EPS,
                scalar2=None, op0=OP.add))
            wait("vector", "sv", vee)
            inv = tick("vector", lambda e, s=sc, w=which: e.reciprocal(
                out=stc4(s, w, 20), in_=stc4(s, w, 16)))
            wait("scalar", "sv", inv)
            rst = tick("scalar", lambda e, s=sc, w=which: e.activation(
                out=stc4(s, w, 16), in_=stc4(s, w, 20), func=AF.Sqrt))
            if which == 1:
                st3["rstd1"][sc] = rst
            else:
                st3["rstd2"][sc] = rst
            return rst

        def emit_apply1(sc):
            rst = st3["rstd1"][sc]
            zb = ZB(sc)
            for tc in range(4):
                wait("vector", "sa", rst)
                av = tick("vector", lambda e, t=tc, z=zb, s=sc: e.tensor_scalar(
                    out=z[:, t, :], in0=z[:, t, :],
                    scalar1=stc(s, 1, 12 + t), scalar2=stc(s, 1, 16 + t),
                    op0=OP.subtract, op1=OP.mult))
                st3["apply"][(sc, tc)] = av
            st3["apply_last"][sc] = cnt["sv"]

        def emit_xT(sc):
            zb = ZB(sc)
            for dc in range(8):
                bank = psY0 if dc % 2 == 0 else psY1
                bsn, bv = bankfree[id(bank)]
                wait("tensor", bsn, bv)
                for tj in range(4):
                    if dc == 0:
                        wait("tensor", "sv", st3["apply"][(sc, tj)])
                gv = None
                for tj in range(4):
                    fn = (lambda e, b=bank, z=zb, t=tj, d=dc:
                          e.transpose(b[:, t * 128:(t + 1) * 128]
                                      .bitcast(F32) if False else
                                      b[:, t * 128:(t + 1) * 128],
                                      z[:, t, d * 128:(d + 1) * 128],
                                      IDN32[:, :]))
                    if tj == 3:
                        gv = tick("tensor", fn)
                    else:
                        quiet("tensor", fn)
                wait("scalar", "sp", gv)
                if sc == 0 and dc == 0:
                    wait("scalar", "sp", st1["xb_free_p"])
                ev = tick("scalar", lambda e, b=bank, d=dc:
                          e.copy(out=XF[:, d, :], in_=b[:, :]))
                bankfree[id(bank)] = ("sa", ev)
                st3["xf_evac"][(sc, dc)] = ev

        def emit_FFN(sc):
            for dc in range(8):
                wait("tensor", "sa", st3["xf_evac"][(sc, dc)])
            if sc == 0:
                wait("tensor", w_all_sq[0], w_all_sq[1])
                wait("tensor", "sa", b2b_v)
            for rc in range(2):
                bank = psC if rc == 0 else psD
                gv = mm_group(bank, [
                    (U1B[:, c, rc * 128:(rc + 1) * 128], XF[:, c, :],
                     bank[:, :]) for c in range(8)])
                wait("scalar", "sp", gv)
                if sc == 0 and rc == 0:
                    wait("scalar", "sv", mtb_fin_last)
                ev = tick("scalar", lambda e, b=bank, r=rc:
                          e.copy(out=PB[:, r, :], in_=b[:, :]))
                bankfree[id(bank)] = ("sa", ev)
                st3["pb_evac"][(sc, rc)] = ev
            for rc in range(2):
                wait("tensor", "sa", st3["pb_evac"][(sc, rc)])
            v1banks = (psY0, psY1, psA)

            def emit_V1(fc):
                vbank = v1banks[fc % 3]
                bsn, bv = bankfree[id(vbank)]
                wait("tensor", bsn, bv)
                gv = mm_group(vbank, [
                    (V1B[:, rc, fc * 128:(fc + 1) * 128], PB[:, rc, :],
                     vbank[:, :]) for rc in range(2)])
                wait("scalar", "sp", gv)
                if fc >= 8:
                    wait("scalar", "sp", st3["u2_tick"][(sc, fc - 8)])
                hslot = fc % 8
                ge = tick("scalar", lambda e, b=vbank, h=hslot, f=fc:
                          e.activation(out=HB[:, h, :], in_=b[:, :],
                                       func=AF.Gelu_apprx_tanh,
                                       bias=B1S[:, f:f + 1]))
                st3["gelu"][(sc, fc)] = ge
                bankfree[id(vbank)] = ("sa", ge)

            def emit_U2(fc):
                hslot = fc % 8
                wait("tensor", "sa", st3["gelu"][(sc, fc)])
                quiet("tensor", lambda e, f=fc, h=hslot, st_=(fc == 0):
                      e.matmul(psC[:, :], u2_v[:, f, 0:128], HB[:, h, :],
                               start=st_, stop=(f == 31),
                               skip_group_check=True))
                fn = (lambda e, f=fc, h=hslot, st_=(fc == 0):
                      e.matmul(psD[:, :], u2_v[:, f, 128:256], HB[:, h, :],
                               start=st_, stop=(f == 31),
                               skip_group_check=True))
                st3["u2_tick"][(sc, fc)] = tick("tensor", fn)

            emit_V1(0)
            emit_V1(1)
            for fc in range(32):
                if fc + 2 < 32:
                    emit_V1(fc + 2)
                emit_U2(fc)
            u2l = st3["u2_tick"][(sc, 31)]
            wait("scalar", "sp", u2l)
            q2a = tick("scalar", lambda e: e.copy(out=Q2B[:, 0, :], in_=psC[:, :]))
            q2b = tick("scalar", lambda e: e.copy(out=Q2B[:, 1, :], in_=psD[:, :]))
            bankfree[id(psC)] = ("sa", q2b)
            bankfree[id(psD)] = ("sa", q2b)
            st3["q2_evac"][sc] = q2b
            zb = ZB(sc)
            banks = (psA, psB, psT0, psT1)
            for tc in range(4):
                for h in range(2):
                    bank = banks[(2 * tc + h) % 4]
                    bsn, bv = bankfree[id(bank)]
                    wait("tensor", bsn, bv)
                    quiet("tensor", lambda e, b=bank, hh=h:
                          e.matmul(b[:, :], ONER[0:1, :],
                                   B2B[0:1, hh * 512:(hh + 1) * 512],
                                   start=True, stop=False))
                    if tc == 0 and h == 0:
                        wait("tensor", "sa", q2b)
                    gv = None
                    for rc in range(2):
                        fn = (lambda e, b=bank, t=tc, hh=h, r=rc, sp_=(rc == 1):
                              e.matmul(b[:, :], Q2B[:, r, t * 128:(t + 1) * 128],
                                       V2B[:, r, hh * 512:(hh + 1) * 512],
                                       start=False, stop=sp_))
                        if rc == 1:
                            gv = tick("tensor", fn)
                        else:
                            quiet("tensor", fn)
                    wait("vector", "sp", gv)
                    wait("vector", "sv", st3["apply"][(sc, tc)])
                    zv = tick("vector", lambda e, b=bank, z=zb, t=tc, hh=h, s=sc:
                              e.scalar_tensor_tensor(
                                  out=z[:, t, hh * 512:(hh + 1) * 512],
                                  in0=b[:, :], scalar=1.0,
                                  in1=z[:, t, hh * 512:(hh + 1) * 512],
                                  op0=OP.mult, op1=OP.add,
                                  accum_out=stc(s, 2, 4 * hh + t)))
                    bankfree[id(bank)] = ("sv", zv)
                    st3["z2add"][(sc, tc, h)] = zv
            st3["z2_last"][sc] = cnt["sv"]
            st3["ffn_last"][sc] = cnt["sp"]

        def emit_apply2_out(sc):
            rst = st3["rstd2"][sc]
            zb = ZB(sc)
            for tc in range(4):
                wait("vector", "sa", rst)
                slot = tc % 2
                wait("vector", f"so{slot}", st3["outs_dma"][slot])
                av = tick("vector", lambda e, t=tc, z=zb, s=sc, sl=slot:
                          e.tensor_scalar(
                              out=OUTS[:, sl, :], in0=z[:, t, :],
                              scalar1=stc(s, 2, 12 + t),
                              scalar2=stc(s, 2, 16 + t),
                              op0=OP.subtract, op1=OP.mult))
                st3["apply2"][(sc, tc)] = av
                wait("scalar", "sv", av)
                st3["outs_dma"][slot] = dma(
                    "scalar",
                    out_t[sc * SC + tc * 128: sc * SC + (tc + 1) * 128, :],
                    OUTS[:, slot, :], f"so{slot}")
            st3["apply2_last"][sc] = cnt["sv"]

        emit_Y(0)
        emit_Wo_z(0)
        emit_Y(1)
        emit_LN(0, 1); emit_apply1(0)
        emit_Wo_z(1)
        emit_xT(0)
        emit_FFN(0)
        emit_LN(1, 1); emit_apply1(1)
        emit_Y(2)
        emit_LN(0, 2); emit_apply2_out(0)
        emit_Wo_z(2)
        emit_xT(1)
        emit_FFN(1)
        emit_LN(2, 1); emit_apply1(2)
        emit_Y(3)
        emit_LN(1, 2); emit_apply2_out(1)
        emit_Wo_z(3)
        emit_xT(2)
        emit_FFN(2)
        emit_LN(3, 1); emit_apply1(3)
        emit_LN(2, 2); emit_apply2_out(2)
        emit_xT(3)
        emit_FFN(3)
        emit_LN(3, 2)
        emit_apply2_out(3)

        wait("sync", "so0", cnt["so0"])
        wait("sync", "so1", cnt["so1"])

        block = ctx.enter_context(nc.Block())

        @block.sync
        def _(e):
            for fn in streams["sync"]:
                fn(e)

        @block.tensor
        def _(e):
            for fn in streams["tensor"]:
                fn(e)

        @block.vector
        def _(e):
            for fn in streams["vector"]:
                fn(e)

        @block.scalar
        def _(e):
            for fn in streams["scalar"]:
                fn(e)

        @block.gpsimd
        def _(e):
            for fn in streams["gpsimd"]:
                fn(e)

    return nc


_NC_CACHE = None


def kernel(L0_attn_x, L0_attn_Wq, L0_attn_Wk, L0_attn_Wv, L0_attn_Wo,
           L0_ffn_U1, L0_ffn_V1, L0_ffn_b1, L0_ffn_U2, L0_ffn_V2, L0_ffn_b2):
    global _NC_CACHE
    x = np.ascontiguousarray(np.asarray(L0_attn_x, dtype=np.float32))
    wq = np.asarray(L0_attn_Wq, np.float32)
    wk = np.asarray(L0_attn_Wk, np.float32)
    wv = np.asarray(L0_attn_Wv, np.float32)
    wo = np.asarray(L0_attn_Wo, np.float32)
    u1 = np.ascontiguousarray(np.asarray(L0_ffn_U1, np.float32))
    v1 = np.ascontiguousarray(np.asarray(L0_ffn_V1, np.float32))
    u2 = np.ascontiguousarray(np.asarray(L0_ffn_U2, np.float32))
    v2 = np.ascontiguousarray(np.asarray(L0_ffn_V2, np.float32))
    b1 = np.asarray(L0_ffn_b1, np.float32)
    b2 = np.asarray(L0_ffn_b2, np.float32)

    wqT = np.ascontiguousarray(wq.T)
    wkT = np.ascontiguousarray(wk.T)
    wvT = np.ascontiguousarray(wv.T)
    woT = np.ascontiguousarray(wo.T)
    b1m = np.ascontiguousarray(b1.reshape(DFF // 128, 128).T)
    b2m = np.ascontiguousarray(b2.reshape(1, D))

    in_maps = []
    for c in range(NCORE):
        b, seg = divmod(c, SEGS)
        xs_f = np.ascontiguousarray(x[b, seg * TC:(seg + 1) * TC, :].T)
        xs_t = np.ascontiguousarray(x[b, seg * TC:(seg + 1) * TC, :])
        wm = np.zeros((128, NCORE), np.float32)
        for r in range(NCORE):
            if r // SEGS == b and r % SEGS < seg:
                wm[:, r] = 1.0
        in_maps.append({
            "x_f": xs_f, "x_t": xs_t,
            "wqT": wqT, "wkT": wkT, "wvT": wvT, "woT": woT,
            "u1": u1, "v1": v1, "u2": u2, "v2": v2,
            "b1i": b1m, "b2r": b2m, "wmi": wm,
        })

    if _NC_CACHE is None:
        _NC_CACHE = build_nc()
    res = run_bass_kernel_spmd(_NC_CACHE, in_maps, core_ids=list(range(NCORE)))
    out = np.empty((B, T, D), np.float32)
    for c in range(NCORE):
        b, seg = divmod(c, SEGS)
        out[b, seg * TC:(seg + 1) * TC, :] = \
            np.asarray(res.results[c]["out_t"], np.float32)
    return out


if __name__ == "__main__":
    nc = build_nc()
    print("built ok")
